# version 05
# baseline (speedup 1.0000x reference)
import math
import numpy as np
import ml_dtypes
import concourse.bass as bass
import concourse.mybir as mybir
from concourse.bass_utils import run_bass_kernel_spmd
from contextlib import ExitStack

F32 = mybir.dt.float32
BF16 = mybir.dt.bfloat16
I32 = mybir.dt.int32
ALU = mybir.AluOpType
ACTF = mybir.ActivationFunctionType
AX = mybir.AxisListType.X
BF = ml_dtypes.bfloat16

Bb, S, D, NH, DH = 4, 2048, 4096, 32, 128
NIC = D // 128
TPC = 1024
HALF = 512
LS = [5, 6, 7, 8, 13, 14, 15, 16]
CHUNKS = ([0, 1, 2, 3, 12, 13, 14, 15], [4, 5, 6, 7, 8, 9, 10, 11])
SCALE = float(DH) ** -0.5
NEG = -1.0e30
MAXPH = 99


def _slopes():
    start = 2.0 ** (-2.0 ** (-(math.log2(NH) - 3)))
    return np.array([start ** (i + 1) for i in range(NH)], dtype=np.float32)


class Sem:
    def __init__(self, nc, es, name):
        self.h = es.enter_context(nc.semaphore(name))
        self.v = 0

    def inc(self, instr, k=1):
        instr.then_inc(self.h, k)
        self.v += k
        return self.v

    def dma(self, instr):
        return self.inc(instr, 16)


def _w(eng, sem, val):
    if val > 0:
        eng.wait_ge(sem.h, val)


def build_nc():
    nc = bass.Bass("TRN2", target_bir_lowering=False)
    x_own = nc.dram_tensor("x_own", [TPC, D], F32, kind="ExternalInput").ap()
    x_kv = nc.dram_tensor("x_kv", [S, D], F32, kind="ExternalInput").ap()
    wq = nc.dram_tensor("wq", [D, D], F32, kind="ExternalInput").ap()
    wo = nc.dram_tensor("wo", [D, D], F32, kind="ExternalInput").ap()
    wf = nc.dram_tensor("wf", [D, D], F32, kind="ExternalInput").ap()
    wkv = nc.dram_tensor("wkv", [D, 256], F32, kind="ExternalInput").ap()
    biasr = nc.dram_tensor("biasr", [1, D], F32, kind="ExternalInput").ap()
    cent = nc.dram_tensor("cent", [64, D], F32, kind="ExternalInput").ap()
    assign_row = nc.dram_tensor("assign_row", [1, D], I32, kind="ExternalInput").ap()
    alibi = nc.dram_tensor("alibi", [NH, 128, 512], F32, kind="ExternalInput").ap()
    masks = nc.dram_tensor("masks", [8, 128, 640], BF16, kind="ExternalInput").ap()
    identf = nc.dram_tensor("identf", [128, 128], F32, kind="ExternalInput").ap()
    onesf = nc.dram_tensor("onesf", [128, 128], F32, kind="ExternalInput").ap()
    iota64 = nc.dram_tensor("iota64", [64, 1], F32, kind="ExternalInput").ap()
    out = nc.dram_tensor("out", [TPC, D], F32, kind="ExternalOutput").ap()
    o_scr = [nc.dram_tensor(f"o_scr{p}", [NIC, 128, TPC], BF16).ap() for p in range(2)]
    xt_scr = [nc.dram_tensor(f"xt_scr{p}", [NIC, 128, TPC], BF16).ap() for p in range(2)]
    z1_scr = [nc.dram_tensor(f"z1_scr{p}", [NIC, 128, TPC], BF16).ap() for p in range(2)]
    z2_scr = [nc.dram_tensor(f"z2_scr{p}", [NIC, 128, TPC], BF16).ap() for p in range(2)]
    cc_in = nc.dram_tensor("cc_in", [64, 1], F32)
    cc_out = nc.dram_tensor("cc_out", [64, 1], F32)

    es = ExitStack()
    qS = Sem(nc, es, "qS")
    qG = Sem(nc, es, "qG")
    tE = Sem(nc, es, "tE")
    vE = Sem(nc, es, "vE")
    aE = Sem(nc, es, "aE")
    cS = Sem(nc, es, "cS")
    bar = Sem(nc, es, "bar")
    s_cst = Sem(nc, es, "s_cst")
    s_wkv = Sem(nc, es, "s_wkv")
    s_xnb = [Sem(nc, es, f"s_xnb{p}") for p in range(2)]
    s_mask = Sem(nc, es, "s_mask")
    s_xnat = Sem(nc, es, "s_xnat")
    s_wq = Sem(nc, es, "s_wq")
    s_al = [Sem(nc, es, f"s_al{p}") for p in range(2)]
    s_osp = Sem(nc, es, "s_osp")
    s_old = Sem(nc, es, "s_old")
    s_xoc = [Sem(nc, es, f"s_xoc{p}") for p in range(2)]
    s_wo = Sem(nc, es, "s_wo")
    s_wf = Sem(nc, es, "s_wf")
    s_z1sp = [Sem(nc, es, f"s_z1sp{p}") for p in range(2)]
    s_z2sp = [Sem(nc, es, f"s_z2sp{p}") for p in range(2)]
    s_zr = [Sem(nc, es, f"s_zr{p}") for p in range(2)]
    s_zo = [Sem(nc, es, f"s_zo{p}") for p in range(2)]
    s_ost = [Sem(nc, es, f"s_ost{p}") for p in range(2)]
    s_cc = Sem(nc, es, "s_cc")
    SY_SEMS = [qS, s_cst, s_wkv, s_xnb[0], s_xnb[1], s_mask, s_xnat, s_wq,
               s_al[0], s_al[1], s_wo, s_wf, s_ost[0], s_ost[1]]
    GP_SEMS = [qG, s_osp, s_old, s_xoc[0], s_xoc[1], s_z1sp[0], s_z1sp[1],
               s_z2sp[0], s_z2sp[1], s_zr[0], s_zr[1], s_zo[0], s_zo[1], s_cc]

    def wde(e, key):
        sm, v = ev[key]
        _w(e, sm, v)

    def sw(e, sem):
        e.wait_ge(sem.h, sem.v)

    idn = es.enter_context(nc.sbuf_tensor("idn", [128, 128], F32))
    onesb_f = es.enter_context(nc.sbuf_tensor("onesb_f", [128, 128], F32))
    onesb = es.enter_context(nc.sbuf_tensor("onesb", [128, 128], BF16))
    iota_s = es.enter_context(nc.sbuf_tensor("iota_s", [64, 1], F32))
    rs_s = es.enter_context(nc.sbuf_tensor("rs_s", [128, 32], F32))
    csb = es.enter_context(nc.sbuf_tensor("csb", [64, 1], BF16))

    B = nc.Block()
    B.__enter__()

    def barrier():
        tgt = bar.v + 5
        bar.v = tgt
        sy_vals = [(s.h, s.v) for s in SY_SEMS if s.v > 0]
        gp_vals = [(s.h, s.v) for s in GP_SEMS if s.v > 0]
        for attr, drains, dr in ((B.tensor, None, True), (B.scalar, None, True),
                                 (B.vector, None, True), (B.sync, sy_vals, False),
                                 (B.gpsimd, gp_vals, True)):
            def f(e, tgt=tgt, drains=drains, dr=dr):
                if drains:
                    for h, v in drains:
                        e.wait_ge(h, v)
                if dr:
                    e.drain()
                e.sem_inc(bar.h, 1)
                e.wait_ge(bar.h, tgt)
            attr(f)

    ev = {}

    def tsp_rounds(tag, key, ntiles, src_of, hi_of, lo_of, ps, slots, tile_w,
                   pre_wait=None, scale=1.0, src_p=128, ident_ap=None):
        idap = ident_ap if ident_ap is not None else idn[:]
        assert slots * tile_w <= 512
        nr = (ntiles + slots - 1) // slots

        def boff(r):
            return (r % 2) * 512

        for r in range(nr):
            i0, i1 = r * slots, min((r + 1) * slots, ntiles)

            def f_te(e, r=r, i0=i0, i1=i1):
                if r == 0 and pre_wait is not None:
                    pre_wait(e)
                if r >= 2 and (tag + "_r", key, r - 2) in ev:
                    _w(e, vE, ev[(tag + "_r", key, r - 2)])
                for i in range(i0, i1):
                    g = i - i0
                    o = boff(r) + g * tile_w
                    tE.inc(e.transpose(ps[:, o:o + tile_w], src_of(i), idap))
                ev[(tag + "_tr", key, r)] = tE.v

            def f_act(e, r=r, i0=i0, i1=i1):
                _w(e, tE, ev[(tag + "_tr", key, r)])
                for i in range(i0, i1):
                    g = i - i0
                    o = boff(r) + g * tile_w
                    aE.inc(e.activation(out=hi_of(i), in_=ps[:, o:o + tile_w],
                                        func=ACTF.Copy, bias=0.0, scale=scale))
                ev[(tag + "_hr", key, r)] = aE.v

            def f_dve(e, r=r, i0=i0, i1=i1):
                _w(e, aE, ev[(tag + "_hr", key, r)])
                for i in range(i0, i1):
                    g = i - i0
                    o = boff(r) + g * tile_w
                    if scale == 1.0:
                        vE.inc(e.tensor_tensor(out=lo_of(i), in0=ps[:, o:o + tile_w],
                                               in1=hi_of(i), op=ALU.subtract))
                    else:
                        vE.inc(e.scalar_tensor_tensor(
                            out=lo_of(i), in0=ps[:, o:o + tile_w],
                            scalar=scale, in1=hi_of(i), op0=ALU.mult, op1=ALU.subtract))
                ev[(tag + "_r", key, r)] = vE.v
            B.tensor(f_te)
            B.scalar(f_act)
            B.vector(f_dve)
        ev[(tag + "_hi", key)] = ev[(tag + "_hr", key, nr - 1)]
        ev[(tag + "_lo", key)] = ev[(tag + "_r", key, nr - 1)]
        ev[(tag + "_trlast", key)] = ev[(tag + "_tr", key, nr - 1)]

    es1kv = ExitStack()
    kth = es1kv.enter_context(nc.sbuf_tensor("kth", [128, S], BF16))
    ktl = es1kv.enter_context(nc.sbuf_tensor("ktl", [128, S], BF16))
    vth = es1kv.enter_context(nc.sbuf_tensor("vth", [128, S], BF16))
    vtl = es1kv.enter_context(nc.sbuf_tensor("vtl", [128, S], BF16))
    es0 = ExitStack()
    wkvf = es0.enter_context(nc.sbuf_tensor("wkvf", [128, NIC * 256], F32))
    wkvh = es0.enter_context(nc.sbuf_tensor("wkvh", [128, NIC * 256], BF16))
    wkvl = es0.enter_context(nc.sbuf_tensor("wkvl", [128, NIC * 256], BF16))
    xnb0 = es0.enter_context(nc.sbuf_tensor("xnb0", [128, 2 * D], F32))
    xtkh = es0.enter_context(nc.sbuf_tensor("xtkh", [128, NIC * 256], BF16))
    xtkl = es0.enter_context(nc.sbuf_tensor("xtkl", [128, NIC * 256], BF16))
    vtf = es0.enter_context(nc.sbuf_tensor("vtf", [128, 256], F32))
    ps0 = ExitStack()
    psK = ps0.enter_context(nc.psum_tensor("psK", [128, 512], F32))
    psV = ps0.enter_context(nc.psum_tensor("psV", [128, 512], F32))
    psX0 = ps0.enter_context(nc.psum_tensor("psX0", [128, 1024], F32))
    psVt = ps0.enter_context(nc.psum_tensor("psVt", [128, 512], F32))

    def f_sync0(e):
        s_cst.dma(e.dma_start(out=idn[:], in_=identf))
        s_cst.dma(e.dma_start(out=onesb_f[:], in_=onesf))
        s_cst.dma(e.dma_start(out=iota_s[:], in_=iota64))
        ev["p0_loads"] = (s_cst, s_cst.v)
        s_wkv.dma(e.dma_start(
            out=wkvf[:].rearrange("p (ic c) -> p ic c", c=256),
            in_=wkv.rearrange("(ic p) c -> p ic c", p=128)))
        ev["p0_wkv"] = (s_wkv, s_wkv.v)
    B.sync(f_sync0)

    def f_act0(e):
        wde(e, "p0_loads")
        aE.inc(e.activation(out=onesb[:], in_=onesb_f[:], func=ACTF.Copy, bias=0.0, scale=1.0))
        ev["p0_hi1"] = aE.v
        wde(e, "p0_wkv")
        aE.inc(e.activation(out=wkvh[:], in_=wkvf[:], func=ACTF.Copy, bias=0.0, scale=1.0))
        ev["p0_wkvh"] = aE.v
    B.scalar(f_act0)

    def f_dve0(e):
        _w(e, aE, ev["p0_wkvh"])
        vE.inc(e.tensor_tensor(out=wkvl[:], in0=wkvf[:], in1=wkvh[:], op=ALU.subtract))
        ev["p0_wkvl"] = vE.v
    B.vector(f_dve0)

    for ks in range(8):
        def f_sync(e, ks=ks):
            for tt in range(2):
                if ks > 0:
                    _w(e, tE, ev[("kvx_trlast", (ks - 1, tt))])
                s_xnb[tt].dma(e.dma_start(out=xnb0[:, tt * D:(tt + 1) * D],
                                          in_=x_kv[(ks * 2 + tt) * 128:(ks * 2 + tt + 1) * 128, :]))
                ev[("kv_x", ks, tt)] = (s_xnb[tt], s_xnb[tt].v)
        B.sync(f_sync)

        for tt in range(2):
            def prew(e, ks=ks, tt=tt):
                wde(e, "p0_loads")
                wde(e, ("kv_x", ks, tt))
                if (ks, tt) != (0, 0):
                    pks, ptt = (ks, tt - 1) if tt > 0 else (ks - 1, 1)
                    _w(e, vE, ev[("kvx_lo", (pks, ptt))])
            tsp_rounds("kvx", (ks, tt), NIC,
                       (lambda i, ks=ks, tt=tt:
                        xnb0[:, tt * D + i * 128:tt * D + (i + 1) * 128]),
                       (lambda i, tt=tt: xtkh[:, i * 256 + tt * 128:i * 256 + (tt + 1) * 128]),
                       (lambda i, tt=tt: xtkl[:, i * 256 + tt * 128:i * 256 + (tt + 1) * 128]),
                       psX0, 4, 128, pre_wait=prew)

        def f_te(e, ks=ks):
            _w(e, aE, ev[("kvx_hi", (ks, 1))])
            _w(e, vE, ev[("kvx_lo", (ks, 1))])
            if ks == 0:
                _w(e, aE, ev["p0_wkvh"])
                _w(e, vE, ev["p0_wkvl"])
            else:
                _w(e, aE, ev[("kv_kh", ks - 1)])
                _w(e, vE, ev[("kv_kl", ks - 1)])
                _w(e, vE, ev[("kv_vtf", ks - 1)])
                _w(e, vE, ev[("kv_vl", ks - 1, 1)])
            for ic in range(NIC):
                wk_h = wkvh[:, ic * 256:ic * 256 + 128]
                wk_l = wkvl[:, ic * 256:ic * 256 + 128]
                xh = xtkh[:, ic * 256:(ic + 1) * 256]
                xl = xtkl[:, ic * 256:(ic + 1) * 256]
                e.matmul(psK[:, 0:256], wk_h, xh, start=(ic == 0), stop=False)
                e.matmul(psK[:, 0:256], wk_h, xl, start=False, stop=False)
                mmK = e.matmul(psK[:, 0:256], wk_l, xh, start=False, stop=(ic == NIC - 1))
            tE.inc(mmK)
            ev[("kv_kmm", ks)] = tE.v
            for ic in range(NIC):
                wv_h = wkvh[:, ic * 256 + 128:(ic + 1) * 256]
                wv_l = wkvl[:, ic * 256 + 128:(ic + 1) * 256]
                xh = xtkh[:, ic * 256:(ic + 1) * 256]
                xl = xtkl[:, ic * 256:(ic + 1) * 256]
                e.matmul(psV[:, 0:256], wv_h, xh, start=(ic == 0), stop=False)
                e.matmul(psV[:, 0:256], wv_h, xl, start=False, stop=False)
                mmV = e.matmul(psV[:, 0:256], wv_l, xh, start=False, stop=(ic == NIC - 1))
            tE.inc(mmV)
            ev[("kv_vmm", ks)] = tE.v
        B.tensor(f_te)

        def f_dve_v(e, ks=ks):
            _w(e, tE, ev[("kv_vmm", ks)])
            if ks > 0:
                _w(e, aE, ev[("kv_vh", ks - 1, 1)])
                _w(e, vE, ev[("kv_vl", ks - 1, 1)])
            vE.inc(e.tensor_copy(out=vtf[:], in_=psV[:, 0:256]))
            ev[("kv_vtf", ks)] = vE.v
        B.vector(f_dve_v)

        def f_te_v(e, ks=ks):
            _w(e, vE, ev[("kv_vtf", ks)])
            for sub in range(2):
                tE.inc(e.transpose(psVt[:, sub * 128:(sub + 1) * 128],
                                   vtf[:, sub * 128:(sub + 1) * 128], idn[:]))
                ev[("kv_vtr", ks, sub)] = tE.v
        B.tensor(f_te_v)

        def f_act(e, ks=ks):
            _w(e, tE, ev[("kv_kmm", ks)])
            aE.inc(e.activation(out=kth[:, ks * 256:(ks + 1) * 256], in_=psK[:, 0:256],
                                func=ACTF.Copy, bias=0.0, scale=1.0))
            ev[("kv_kh", ks)] = aE.v
            for sub in range(2):
                _w(e, tE, ev[("kv_vtr", ks, sub)])
                u = ks * 2 + sub
                aE.inc(e.activation(out=vth[:, u * 128:(u + 1) * 128],
                                    in_=psVt[:, sub * 128:(sub + 1) * 128],
                                    func=ACTF.Copy, bias=0.0, scale=1.0))
                ev[("kv_vh", ks, sub)] = aE.v
        B.scalar(f_act)

        def f_dve(e, ks=ks):
            _w(e, aE, ev[("kv_kh", ks)])
            vE.inc(e.tensor_tensor(out=ktl[:, ks * 256:(ks + 1) * 256], in0=psK[:, 0:256],
                                   in1=kth[:, ks * 256:(ks + 1) * 256], op=ALU.subtract))
            ev[("kv_kl", ks)] = vE.v
            for sub in range(2):
                _w(e, aE, ev[("kv_vh", ks, sub)])
                u = ks * 2 + sub
                vE.inc(e.tensor_tensor(out=vtl[:, u * 128:(u + 1) * 128],
                                       in0=psVt[:, sub * 128:(sub + 1) * 128],
                                       in1=vth[:, u * 128:(u + 1) * 128], op=ALU.subtract))
                ev[("kv_vl", ks, sub)] = vE.v
        B.vector(f_dve)
    barrier()
    ps0.close()
    es0.close()

    es1 = ExitStack()
    mask_s = es1.enter_context(nc.sbuf_tensor("mask_s", [128, 4 * 640], BF16))
    albf = es1.enter_context(nc.sbuf_tensor("albf", [128, 1024], F32))
    xth = es1.enter_context(nc.sbuf_tensor("xth", [128, NIC * HALF], BF16))
    xtl = es1.enter_context(nc.sbuf_tensor("xtl", [128, NIC * HALF], BF16))
    wqf = es1.enter_context(nc.sbuf_tensor("wqf", [128, NIC * 128], F32))
    wqh = es1.enter_context(nc.sbuf_tensor("wqh", [128, NIC * 128], BF16))
    wql = es1.enter_context(nc.sbuf_tensor("wql", [128, NIC * 128], BF16))
    qsh = es1.enter_context(nc.sbuf_tensor("qsh", [128, 2 * HALF], BF16))
    qsl = es1.enter_context(nc.sbuf_tensor("qsl", [128, 2 * HALF], BF16))
    logits = es1.enter_context(nc.sbuf_tensor("logits", [128, 3 * 2048], F32))
    pbuf = es1.enter_context(nc.sbuf_tensor("pbuf", [128, 2 * 2048], F32))
    pth = es1.enter_context(nc.sbuf_tensor("pth", [128, 2 * 2048], BF16))
    ptl = es1.enter_context(nc.sbuf_tensor("ptl", [128, 2 * 2048], BF16))
    obh = es1.enter_context(nc.sbuf_tensor("obh", [128, 512], BF16))
    obl = es1.enter_context(nc.sbuf_tensor("obl", [128, 512], BF16))
    negm = es1.enter_context(nc.sbuf_tensor("negm", [128, 2], F32))
    sums = es1.enter_context(nc.sbuf_tensor("sums", [128, 2], F32))
    rrec = es1.enter_context(nc.sbuf_tensor("rrec", [128, 2], F32))
    ps1 = ExitStack()
    psQ = ps1.enter_context(nc.psum_tensor("psQ", [128, 1024], F32))
    psS = ps1.enter_context(nc.psum_tensor("psS", [128, 1024], F32))
    psT = ps1.enter_context(nc.psum_tensor("psT", [128, 1024], F32))
    psO = ps1.enter_context(nc.psum_tensor("psO", [128, 1024], F32))

    def wq_dma(e, h):
        for dcg in range(4):
            src_ = wq[dcg * 1024:(dcg + 1) * 1024,
                      h * 128:(h + 1) * 128].rearrange("(ic p) o -> p ic o", p=128)
            dst = wqf[:, dcg * 1024:(dcg + 1) * 1024].rearrange("p (ic c) -> p ic c", c=128)
            s_wq.dma(e.dma_start(out=dst, in_=src_))
        ev[("wq", h)] = (s_wq, s_wq.v)

    def wq_split_hi(e, h):
        wde(e, ("wq", h))
        if ("qmm", h - 1) in ev:
            _w(e, tE, ev[("qmm", h - 1)])
        aE.inc(e.activation(out=wqh[:], in_=wqf[:], func=ACTF.Copy, bias=0.0, scale=1.0))
        ev[("wqh", h)] = aE.v

    def wq_split_lo(e, h):
        _w(e, aE, ev[("wqh", h)])
        vE.inc(e.tensor_tensor(out=wql[:], in0=wqf[:], in1=wqh[:], op=ALU.subtract))
        ev[("wql", h)] = vE.v

    def q_proj(e, h, half):
        _w(e, aE, ev[("wqh", h)])
        _w(e, vE, ev[("wql", h)])
        _w(e, aE, ev[("xb_a", half)])
        _w(e, vE, ev[("xb_v", half)])
        if ("qlo", h - 2) in ev:
            _w(e, vE, ev[("qlo", h - 2)])
        par = h % 2
        dst = psQ[:, par * 512:(par + 1) * 512]
        for ic in range(NIC):
            wh = wqh[:, ic * 128:(ic + 1) * 128]
            wl = wql[:, ic * 128:(ic + 1) * 128]
            xh = xth[:, ic * HALF:(ic + 1) * HALF]
            xl = xtl[:, ic * HALF:(ic + 1) * HALF]
            e.matmul(dst, wh, xh, start=(ic == 0), stop=False)
            e.matmul(dst, wh, xl, start=False, stop=False)
            mmq = e.matmul(dst, wl, xh, start=False, stop=(ic == NIC - 1))
        tE.inc(mmq)
        ev[("qmm", h)] = tE.v

    def q_evac_hi(e, h):
        _w(e, tE, ev[("qmm", h)])
        if ("amm_last", h - 2) in ev:
            _w(e, tE, ev[("amm_last", h - 2)])
        par = h % 2
        aE.inc(e.activation(out=qsh[:, par * 512:(par + 1) * 512],
                            in_=psQ[:, par * 512:(par + 1) * 512],
                            func=ACTF.Copy, bias=0.0, scale=SCALE))
        ev[("qhi", h)] = aE.v

    def q_evac_lo(e, h):
        _w(e, aE, ev[("qhi", h)])
        par = h % 2
        vE.inc(e.scalar_tensor_tensor(out=qsl[:, par * 512:(par + 1) * 512],
                                      in0=psQ[:, par * 512:(par + 1) * 512], scalar=SCALE,
                                      in1=qsh[:, par * 512:(par + 1) * 512],
                                      op0=ALU.mult, op1=ALU.subtract))
        ev[("qlo", h)] = vE.v

    def p1_half(half):
        slopes = _slopes()
        for tt in range(4):
            def f_sync(e, tt=tt):
                if tt > 0:
                    _w(e, tE, ev[("xb_trlast", (half, tt - 1))])
                s_xnat.dma(e.dma_start(out=pbuf[:, 0:4096],
                                       in_=x_own[(half * 4 + tt) * 128:(half * 4 + tt + 1) * 128, :]))
                ev[("x_nat", half, tt)] = (s_xnat, s_xnat.v)
                if tt == 0:
                    for s in range(4):
                        s_mask.dma(e.dma_start(out=mask_s[:, s * 640:(s + 1) * 640],
                                               in_=masks[half * 4 + s]))
                    ev[("mask", half)] = (s_mask, s_mask.v)
            B.sync(f_sync)

            def prew(e, tt=tt):
                wde(e, ("x_nat", half, tt))
                if tt > 0:
                    _w(e, vE, ev[("xb_lo", (half, tt - 1))])
            tsp_rounds("xb", (half, tt), NIC,
                       (lambda i: pbuf[:, i * 128:(i + 1) * 128]),
                       (lambda i, tt=tt: xth[:, i * HALF + tt * 128:i * HALF + (tt + 1) * 128]),
                       (lambda i, tt=tt: xtl[:, i * HALF + tt * 128:i * HALF + (tt + 1) * 128]),
                       psT, 4, 128, pre_wait=prew)
        ev[("xb_a", half)] = ev[("xb_hi", (half, 3))]
        ev[("xb_v", half)] = ev[("xb_lo", (half, 3))]

        def f_gp_sp(e):
            _w(e, aE, ev[("xb_a", half)])
            _w(e, vE, ev[("xb_v", half)])
            for ic in range(NIC):
                qG.dma(e.dma_start(out=xt_scr[0][ic, :, half * HALF:(half + 1) * HALF],
                                   in_=xth[:, ic * HALF:(ic + 1) * HALF]))
                qG.dma(e.dma_start(out=xt_scr[1][ic, :, half * HALF:(half + 1) * HALF],
                                   in_=xtl[:, ic * HALF:(ic + 1) * HALF]))
            ev[("xt_spill", half)] = qG.v
        B.gpsimd(f_gp_sp)

        def f_sync_pro(e):
            wq_dma(e, 0)
            s_al[0].dma(e.dma_start(out=albf[:, 0:512], in_=alibi[0]))
            ev[("al", 0)] = (s_al[0], s_al[0].v)
        B.sync(f_sync_pro)
        B.scalar(lambda e: wq_split_hi(e, 0))
        B.vector(lambda e: wq_split_lo(e, 0))

        def f_sync_pro2(e):
            _w(e, aE, ev[("wqh", 0)])
            _w(e, vE, ev[("wql", 0)])
            wq_dma(e, 1)
        B.sync(f_sync_pro2)
        B.tensor(lambda e: q_proj(e, 0, half))
        B.scalar(lambda e: q_evac_hi(e, 0))
        B.vector(lambda e: q_evac_lo(e, 0))

        iters = [(h, s) for h in range(NH) for s in range(4)]
        NIT = len(iters)
        gb = [0]
        gtg = [0]

        def blocks_of(sg):
            L = LS[sg] * 128
            res = []
            off = 0
            while off < L:
                res.append((off, min(512, L - off)))
                off += 512
            return res

        pss_read = {}

        def score_block_secs(n):
            if not (0 <= n < NIT):
                return
            h, s = iters[n]
            sg = half * 4 + s
            L = LS[sg] * 128
            lpar = n % 3
            slope_h = float(slopes[h])
            for bi, (off, nb) in enumerate(blocks_of(sg)):
                gidx = gb[0]
                gb[0] += 1
                spar = gidx % 2

                def f_te(e, n=n, h=h, s=s, off=off, nb=nb, spar=spar, bi=bi):
                    if bi == 0 and s == 0:
                        _w(e, aE, ev[("qhi", h)])
                        _w(e, vE, ev[("qlo", h)])
                        wde(e, ("al", h))
                        if h == 0:
                            wde(e, ("mask", half))
                    if spar in pss_read:
                        _w(e, vE, pss_read[spar])
                    qpar = h % 2
                    qh_t = qsh[:, qpar * 512 + s * 128:qpar * 512 + (s + 1) * 128]
                    ql_t = qsl[:, qpar * 512 + s * 128:qpar * 512 + (s + 1) * 128]
                    dst = psS[:, spar * 512:spar * 512 + nb]
                    e.matmul(dst, qh_t, kth[:, off:off + nb], start=True, stop=False)
                    e.matmul(dst, qh_t, ktl[:, off:off + nb], start=False, stop=False)
                    mms_ = e.matmul(dst, ql_t, kth[:, off:off + nb], start=False, stop=True)
                    tE.inc(mms_)
                    ev[("smm", n, off)] = tE.v
                B.tensor(f_te)

                def f_dve(e, n=n, off=off, nb=nb, spar=spar, L=L, s=s,
                          lpar=lpar, slope_h=slope_h):
                    _w(e, tE, ev[("smm", n, off)])
                    r0 = L - 640
                    lo = max(off, r0)
                    hi = off + nb
                    if lo < hi:
                        vE.inc(e.tensor_tensor(
                            out=psS[:, spar * 512 + (lo - off):spar * 512 + (hi - off)],
                            in0=psS[:, spar * 512 + (lo - off):spar * 512 + (hi - off)],
                            in1=mask_s[:, s * 640 + (lo - r0):s * 640 + (hi - r0)],
                            op=ALU.add))
                        sw(e, vE)
                    c_hb = float(slope_h * off)
                    vE.inc(e.scalar_tensor_tensor(
                        out=logits[:, lpar * 2048 + off:lpar * 2048 + off + nb],
                        in0=psS[:, spar * 512:spar * 512 + nb], scalar=c_hb,
                        in1=albf[:, (h % 2) * 512:(h % 2) * 512 + nb],
                        op0=ALU.add, op1=ALU.add))
                    pss_read[spar] = vE.v
                    ev[("evac", n)] = vE.v
                B.vector(f_dve)
            if s == 3:
                ev[("amm_last", h)] = tE.v

        def pt_group_secs(n):
            m = n - 3
            if not (0 <= m < NIT):
                return
            h, s = iters[m]
            sg = half * 4 + s
            nu = LS[sg]
            lpar = m % 3
            for g in range((nu + 3) // 4):
                gidx = gtg[0]
                gtg[0] += 1
                ev[("pt_gpar", m, g)] = gidx
                u0 = g * 4
                u1 = min(u0 + 4, nu)

                def f_te(e, m=m, g=g, gidx=gidx, u0=u0, u1=u1, lpar=lpar):
                    if g == 0:
                        _w(e, aE, ev[("pn", m)])
                    if ("ptev_l", gidx - 2) in ev:
                        _w(e, vE, ev[("ptev_l", gidx - 2)])
                    tpar = gidx % 2
                    for u in range(u0, u1):
                        tE.inc(e.transpose(
                            psT[:, tpar * 512 + (u - u0) * 128:tpar * 512 + (u - u0 + 1) * 128],
                            logits[:, lpar * 2048 + u * 128:lpar * 2048 + (u + 1) * 128],
                            idn[:]))
                    ev[("pt_tr", m, g)] = tE.v

                def f_act(e, m=m, g=g, gidx=gidx, u0=u0, u1=u1):
                    _w(e, tE, ev[("pt_tr", m, g)])
                    tpar = gidx % 2
                    ppar = m % 2
                    w_ = (u1 - u0) * 128
                    aE.inc(e.activation(
                        out=pth[:, ppar * 2048 + u0 * 128:ppar * 2048 + u0 * 128 + w_],
                        in_=psT[:, tpar * 512:tpar * 512 + w_],
                        func=ACTF.Copy, bias=0.0, scale=1.0))
                    ev[("ptev_h_g", m, g)] = aE.v

                def f_dve(e, m=m, g=g, gidx=gidx, u0=u0, u1=u1):
                    _w(e, aE, ev[("ptev_h_g", m, g)])
                    tpar = gidx % 2
                    ppar = m % 2
                    w_ = (u1 - u0) * 128
                    vE.inc(e.tensor_tensor(
                        out=ptl[:, ppar * 2048 + u0 * 128:ppar * 2048 + u0 * 128 + w_],
                        in0=psT[:, tpar * 512:tpar * 512 + w_],
                        in1=pth[:, ppar * 2048 + u0 * 128:ppar * 2048 + u0 * 128 + w_],
                        op=ALU.subtract))
                    ev[("ptev_l_g", m, g)] = vE.v
                    ev[("ptev_l", gidx)] = vE.v
                B.tensor(f_te)
                B.scalar(f_act)
                B.vector(f_dve)

        def te_sec(n):
            def f(e):
                m = n - 4
                if 0 <= m < NIT:
                    h, s = iters[m]
                    sg = half * 4 + s
                    nu = LS[sg]
                    opar = m % 2
                    if ("o_lo", m - 2) in ev:
                        _w(e, vE, ev[("o_lo", m - 2)])
                    ppar = m % 2
                    for u in range(nu):
                        g = u // 4
                        _w(e, aE, ev[("ptev_h_g", m, g)])
                        _w(e, vE, ev[("ptev_l_g", m, g)])
                        vh_t = vth[:, u * 128:(u + 1) * 128]
                        vl_t = vtl[:, u * 128:(u + 1) * 128]
                        ph_t = pth[:, ppar * 2048 + u * 128:ppar * 2048 + (u + 1) * 128]
                        pl_t = ptl[:, ppar * 2048 + u * 128:ppar * 2048 + (u + 1) * 128]
                        dst = psO[:, opar * 512:opar * 512 + 128]
                        e.matmul(dst, vh_t, ph_t, start=(u == 0), stop=False)
                        e.matmul(dst, vh_t, pl_t, start=False, stop=False)
                        mmav = e.matmul(dst, vl_t, ph_t, start=False, stop=(u == nu - 1))
                    tE.inc(mmav)
                    ev[("av", m)] = tE.v
                if n < NIT:
                    h, s = iters[n]
                    if s == 2 and h + 1 < NH:
                        q_proj(e, h + 1, half)
            return f

        def act_sec(n):
            def f(e):
                if n < NIT:
                    h, s = iters[n]
                    if s == 1 and h + 1 < NH:
                        wq_split_hi(e, h + 1)
                    if s == 3 and h + 1 < NH:
                        q_evac_hi(e, h + 1)
                m = n - 1
                if 0 <= m < NIT:
                    h, s = iters[m]
                    sg = half * 4 + s
                    L = LS[sg] * 128
                    _w(e, vE, ev[("max", m)])
                    lpar = m % 3
                    ppar = m % 2
                    aE.inc(e.activation(out=pbuf[:, ppar * 2048:ppar * 2048 + L],
                                        in_=logits[:, lpar * 2048:lpar * 2048 + L],
                                        func=ACTF.Exp, bias=negm[:, m % 2:m % 2 + 1],
                                        scale=1.0, accum_out=sums[:, m % 2:m % 2 + 1]))
                    ev[("exp", m)] = aE.v
                m = n - 2
                if 0 <= m < NIT:
                    h, s = iters[m]
                    sg = half * 4 + s
                    L = LS[sg] * 128
                    _w(e, vE, ev[("recip", m)])
                    lpar = m % 3
                    ppar = m % 2
                    aE.inc(e.activation(out=logits[:, lpar * 2048:lpar * 2048 + L],
                                        in_=pbuf[:, ppar * 2048:ppar * 2048 + L],
                                        func=ACTF.Copy, bias=0.0,
                                        scale=rrec[:, m % 2:m % 2 + 1]))
                    ev[("pn", m)] = aE.v
                m = n - 4
                if 0 <= m < NIT:
                    h, s = iters[m]
                    _w(e, tE, ev[("av", m)])
                    if ("ospill", h - 1) in ev:
                        wde(e, ("ospill", h - 1))
                    opar = m % 2
                    aE.inc(e.activation(out=obh[:, s * 128:(s + 1) * 128],
                                        in_=psO[:, opar * 512:opar * 512 + 128],
                                        func=ACTF.Copy, bias=0.0, scale=1.0))
                    ev[("o_hi", m)] = aE.v
            return f

        def dve_sec(n):
            def f(e):
                if n < NIT:
                    h, s = iters[n]
                    if s == 1 and h + 1 < NH:
                        wq_split_lo(e, h + 1)
                    if s == 3 and h + 1 < NH:
                        q_evac_lo(e, h + 1)
                    sg = half * 4 + s
                    L = LS[sg] * 128
                    lpar = n % 3
                    if ("exp", n - 2) in ev:
                        _w(e, aE, ev[("exp", n - 2)])
                    sw(e, vE)
                    vE.inc(e.reduce_max(out=negm[:, n % 2:n % 2 + 1],
                                        in_=logits[:, lpar * 2048:lpar * 2048 + L],
                                        axis=AX, negate=True))
                    ev[("max", n)] = vE.v
                m = n - 1
                if 0 <= m < NIT:
                    _w(e, aE, ev[("exp", m)])
                    vE.inc(e.reciprocal(out=rrec[:, m % 2:m % 2 + 1],
                                        in_=sums[:, m % 2:m % 2 + 1]))
                    ev[("recip", m)] = vE.v
                m = n - 4
                if 0 <= m < NIT:
                    h, s = iters[m]
                    _w(e, aE, ev[("o_hi", m)])
                    opar = m % 2
                    vE.inc(e.tensor_tensor(out=obl[:, s * 128:(s + 1) * 128],
                                           in0=psO[:, opar * 512:opar * 512 + 128],
                                           in1=obh[:, s * 128:(s + 1) * 128],
                                           op=ALU.subtract))
                    ev[("o_lo", m)] = vE.v
            return f

        def sync_sec(n):
            def f(e):
                if n < NIT:
                    h, s = iters[n]
                    if s == 2 and h + 2 < NH:
                        _w(e, aE, ev[("wqh", h + 1)])
                        _w(e, vE, ev[("wql", h + 1)])
                        wq_dma(e, h + 2)
                    if s == 0 and h + 1 < NH:
                        if h >= 1:
                            _w(e, vE, ev[("max", (h - 1) * 4 + 3)])
                        p_ = (h + 1) % 2
                        s_al[p_].dma(e.dma_start(
                            out=albf[:, p_ * 512:(p_ + 1) * 512], in_=alibi[h + 1]))
                        ev[("al", h + 1)] = (s_al[p_], s_al[p_].v)
            return f

        def gp_sec(n):
            def f(e):
                m = n - 4
                if 0 <= m < NIT:
                    h, s = iters[m]
                    if s == 3:
                        _w(e, aE, ev[("o_hi", m)])
                        _w(e, vE, ev[("o_lo", m)])
                        s_osp.dma(e.dma_start(out=o_scr[0][h, :, half * HALF:(half + 1) * HALF],
                                              in_=obh[:]))
                        s_osp.dma(e.dma_start(out=o_scr[1][h, :, half * HALF:(half + 1) * HALF],
                                              in_=obl[:]))
                        ev[("ospill", h)] = (s_osp, s_osp.v)
            return f

        for n in range(NIT + 5):
            pt_group_secs(n)
            score_block_secs(n)
            B.tensor(te_sec(n))
            B.scalar(act_sec(n))
            B.vector(dve_sec(n))
            B.sync(sync_sec(n))
            B.gpsimd(gp_sec(n))
        barrier()

    if MAXPH >= 1:
        for half in range(2):
            p1_half(half)
            ev = {k: v for k, v in ev.items() if not isinstance(k, tuple)}

    ps1.close()
    es1.close()
    es1kv.close()

    esA = ExitStack()
    acth = esA.enter_context(nc.sbuf_tensor("acth", [128, NIC * HALF], BF16))
    actl = esA.enter_context(nc.sbuf_tensor("actl", [128, NIC * HALF], BF16))
    q_b = esA.enter_context(nc.sbuf_tensor("q_b", [128, TPC], F32))
    mu_b = esA.enter_context(nc.sbuf_tensor("mu_b", [128, HALF], F32))
    r_b = esA.enter_context(nc.sbuf_tensor("r_b", [128, HALF], F32))
    mu1r = esA.enter_context(nc.sbuf_tensor("mu1r", [1, TPC], F32))
    rs1r = esA.enter_context(nc.sbuf_tensor("rs1r", [1, TPC], F32))
    mu2r = esA.enter_context(nc.sbuf_tensor("mu2r", [1, TPC], F32))
    rs2r = esA.enter_context(nc.sbuf_tensor("rs2r", [1, TPC], F32))
    srow = esA.enter_context(nc.sbuf_tensor("srow", [1, HALF], F32))
    srow2 = esA.enter_context(nc.sbuf_tensor("srow2", [1, HALF], F32))
    rowh = esA.enter_context(nc.sbuf_tensor("rowh", [1, HALF], BF16))
    rowl = esA.enter_context(nc.sbuf_tensor("rowl", [1, HALF], BF16))
    ones_row = esA.enter_context(nc.sbuf_tensor("ones_row", [1, HALF], BF16))

    def f_dve_or(e):
        vE.inc(e.memset(ones_row[:], 1.0))
        ev["q_ones_v"] = vE.v
    B.vector(f_dve_or)

    es2 = ExitStack()
    woF = es2.enter_context(nc.sbuf_tensor("woF", [128, NIC * 128], F32))
    woh = es2.enter_context(nc.sbuf_tensor("woh", [128, NIC * 128], BF16))
    wol = es2.enter_context(nc.sbuf_tensor("wol", [128, NIC * 128], BF16))
    xoch = es2.enter_context(nc.sbuf_tensor("xoch", [128, 2 * HALF], BF16))
    xocl = es2.enter_context(nc.sbuf_tensor("xocl", [128, 2 * HALF], BF16))
    z1f = es2.enter_context(nc.sbuf_tensor("z1f", [128, 2 * HALF], F32))
    sqf = es2.enter_context(nc.sbuf_tensor("sqf", [128, 2 * HALF], F32))
    z1hb = es2.enter_context(nc.sbuf_tensor("z1hb", [128, 2 * HALF], BF16))
    z1lb = es2.enter_context(nc.sbuf_tensor("z1lb", [128, 2 * HALF], BF16))
    sqhb = es2.enter_context(nc.sbuf_tensor("sqhb", [128, 2 * HALF], BF16))
    sqlb = es2.enter_context(nc.sbuf_tensor("sqlb", [128, 2 * HALF], BF16))
    ps2 = ExitStack()
    psY = ps2.enter_context(nc.psum_tensor("psY", [128, 1024], F32))
    psZs = ps2.enter_context(nc.psum_tensor("psZs", [1, 512], F32))
    psQs = ps2.enter_context(nc.psum_tensor("psQs", [1, 512], F32))

    def wo_dma(e, oc):
        for dcg in range(4):
            src_ = wo[dcg * 1024:(dcg + 1) * 1024,
                      oc * 128:(oc + 1) * 128].rearrange("(ic p) o -> p ic o", p=128)
            dst = woF[:, dcg * 1024:(dcg + 1) * 1024].rearrange("p (ic c) -> p ic c", c=128)
            s_wo.dma(e.dma_start(out=dst, in_=src_))
        ev[("woq", oc)] = (s_wo, s_wo.v)

    def p2_half(half):
        def f_gp0(e):
            for dc in range(NIC):
                s_old.dma(e.dma_start(out=acth[:, dc * HALF:(dc + 1) * HALF],
                                      in_=o_scr[0][dc, :, half * HALF:(half + 1) * HALF]))
                s_old.dma(e.dma_start(out=actl[:, dc * HALF:(dc + 1) * HALF],
                                      in_=o_scr[1][dc, :, half * HALF:(half + 1) * HALF]))
            ev[("oload", half)] = (s_old, s_old.v)
            for oc in range(2):
                s_xoc[oc % 2].dma(e.dma_start(out=xoch[:, (oc % 2) * HALF:(oc % 2 + 1) * HALF],
                                              in_=xt_scr[0][oc, :, half * HALF:(half + 1) * HALF]))
                s_xoc[oc % 2].dma(e.dma_start(out=xocl[:, (oc % 2) * HALF:(oc % 2 + 1) * HALF],
                                              in_=xt_scr[1][oc, :, half * HALF:(half + 1) * HALF]))
                ev[("xoc", half, oc)] = (s_xoc[oc % 2], s_xoc[oc % 2].v)
        B.gpsimd(f_gp0)

        def f_sync0(e):
            wo_dma(e, 0)
        B.sync(f_sync0)

        def f_act0(e):
            wde(e, ("woq", 0))
            aE.inc(e.activation(out=woh[:], in_=woF[:], func=ACTF.Copy, bias=0.0, scale=1.0))
            ev[("wosh", 0)] = aE.v
        B.scalar(f_act0)

        def f_dve0(e):
            _w(e, aE, ev[("wosh", 0)])
            vE.inc(e.tensor_tensor(out=wol[:], in0=woF[:], in1=woh[:], op=ALU.subtract))
            ev[("wosl", 0)] = vE.v
        B.vector(f_dve0)

        def f_sync1(e):
            _w(e, aE, ev[("wosh", 0)])
            _w(e, vE, ev[("wosl", 0)])
            wo_dma(e, 1)
        B.sync(f_sync1)

        for oc in range(NIC + 2):
            def f_te(e, oc=oc):
                if oc < NIC:
                    _w(e, aE, ev[("wosh", oc)])
                    _w(e, vE, ev[("wosl", oc)])
                    if oc == 0:
                        wde(e, ("oload", half))
                    if ("z1f", half, oc - 2) in ev:
                        _w(e, vE, ev[("z1f", half, oc - 2)])
                    par = oc % 2
                    dst = psY[:, par * 512:(par + 1) * 512]
                    for dc in range(NIC):
                        wh = woh[:, dc * 128:(dc + 1) * 128]
                        wl = wol[:, dc * 128:(dc + 1) * 128]
                        ohd = acth[:, dc * HALF:(dc + 1) * HALF]
                        old = actl[:, dc * HALF:(dc + 1) * HALF]
                        e.matmul(dst, wh, ohd, start=(dc == 0), stop=False)
                        e.matmul(dst, wh, old, start=False, stop=False)
                        mmy = e.matmul(dst, wl, ohd, start=False, stop=(dc == NIC - 1))
                    tE.inc(mmy)
                    ev[("ymm", half, oc)] = tE.v
                m = oc - 2
                if 0 <= m < NIC:
                    par = m % 2
                    _w(e, aE, ev[("z1sq_h", half, m)])
                    _w(e, vE, ev[("z1sq_l", half, m)])
                    st = m == 0
                    e.matmul(psZs[:], onesb[:, 0:1], z1hb[:, par * HALF:(par + 1) * HALF],
                             start=st, stop=False)
                    e.matmul(psZs[:], onesb[:, 0:1], z1lb[:, par * HALF:(par + 1) * HALF],
                             start=False, stop=(m == NIC - 1))
                    e.matmul(psQs[:], onesb[:, 0:1], sqhb[:, par * HALF:(par + 1) * HALF],
                             start=st, stop=False)
                    mmst = e.matmul(psQs[:], onesb[:, 0:1],
                                    sqlb[:, par * HALF:(par + 1) * HALF],
                                    start=False, stop=(m == NIC - 1))
                    tE.inc(mmst)
                    ev[("stats", half, m)] = tE.v

            def f_act(e, oc=oc):
                m = oc + 1
                if m < NIC:
                    wde(e, ("woq", m))
                    if ("ymm", half, m - 1) in ev:
                        _w(e, tE, ev[("ymm", half, m - 1)])
                    aE.inc(e.activation(out=woh[:], in_=woF[:], func=ACTF.Copy,
                                        bias=0.0, scale=1.0))
                    ev[("wosh", m)] = aE.v
                m = oc - 1
                if 0 <= m < NIC:
                    par = m % 2
                    _w(e, vE, ev[("z1f", half, m)])
                    if ("stats", half, m - 2) in ev:
                        _w(e, tE, ev[("stats", half, m - 2)])
                    if ("z1sp", half, m - 2) in ev:
                        wde(e, ("z1sp", half, m - 2))
                    aE.inc(e.activation(out=z1hb[:, par * HALF:(par + 1) * HALF],
                                        in_=z1f[:, par * HALF:(par + 1) * HALF],
                                        func=ACTF.Copy, bias=0.0, scale=1.0))
                    ev[("z1h", half, m)] = aE.v
                    aE.inc(e.activation(out=sqf[:, par * HALF:(par + 1) * HALF],
                                        in_=z1f[:, par * HALF:(par + 1) * HALF],
                                        func=ACTF.Square, bias=0.0, scale=1.0))
                    sw(e, aE)
                    aE.inc(e.activation(out=sqhb[:, par * HALF:(par + 1) * HALF],
                                        in_=sqf[:, par * HALF:(par + 1) * HALF],
                                        func=ACTF.Copy, bias=0.0, scale=1.0))
                    ev[("z1sq_h", half, m)] = aE.v

            def f_dve(e, oc=oc):
                m = oc + 1
                if m < NIC:
                    _w(e, aE, ev[("wosh", m)])
                    vE.inc(e.tensor_tensor(out=wol[:], in0=woF[:], in1=woh[:],
                                           op=ALU.subtract))
                    ev[("wosl", m)] = vE.v
                if oc < NIC:
                    _w(e, tE, ev[("ymm", half, oc)])
                    wde(e, ("xoc", half, oc))
                    if ("z1sq_h", half, oc - 2) in ev:
                        _w(e, aE, ev[("z1sq_h", half, oc - 2)])
                    sw(e, vE)
                    par = oc % 2
                    z1s = z1f[:, par * HALF:(par + 1) * HALF]
                    vE.inc(e.tensor_tensor(out=z1s, in0=psY[:, par * 512:(par + 1) * 512],
                                           in1=xoch[:, par * HALF:(par + 1) * HALF],
                                           op=ALU.add))
                    sw(e, vE)
                    vE.inc(e.tensor_tensor(out=z1s, in0=z1s,
                                           in1=xocl[:, par * HALF:(par + 1) * HALF],
                                           op=ALU.add))
                    ev[("z1f", half, oc)] = vE.v
                m = oc - 1
                if 0 <= m < NIC:
                    par = m % 2
                    _w(e, aE, ev[("z1h", half, m)])
                    vE.inc(e.tensor_tensor(out=z1lb[:, par * HALF:(par + 1) * HALF],
                                           in0=z1f[:, par * HALF:(par + 1) * HALF],
                                           in1=z1hb[:, par * HALF:(par + 1) * HALF],
                                           op=ALU.subtract))
                    ev[("z1l", half, m)] = vE.v
                    _w(e, aE, ev[("z1sq_h", half, m)])
                    vE.inc(e.tensor_tensor(out=sqlb[:, par * HALF:(par + 1) * HALF],
                                           in0=sqf[:, par * HALF:(par + 1) * HALF],
                                           in1=sqhb[:, par * HALF:(par + 1) * HALF],
                                           op=ALU.subtract))
                    ev[("z1sq_l", half, m)] = vE.v

            def f_sync(e, oc=oc):
                m = oc + 2
                if m < NIC:
                    _w(e, aE, ev[("wosh", m - 1)])
                    _w(e, vE, ev[("wosl", m - 1)])
                    wo_dma(e, m)

            def f_gp(e, oc=oc):
                m = oc + 2
                if m < NIC:
                    if ("z1f", half, m - 2) in ev:
                        _w(e, vE, ev[("z1f", half, m - 2)])
                    s_xoc[m % 2].dma(e.dma_start(out=xoch[:, (m % 2) * HALF:(m % 2 + 1) * HALF],
                                                 in_=xt_scr[0][m, :, half * HALF:(half + 1) * HALF]))
                    s_xoc[m % 2].dma(e.dma_start(out=xocl[:, (m % 2) * HALF:(m % 2 + 1) * HALF],
                                                 in_=xt_scr[1][m, :, half * HALF:(half + 1) * HALF]))
                    ev[("xoc", half, m)] = (s_xoc[m % 2], s_xoc[m % 2].v)
                m = oc - 1
                if 0 <= m < NIC:
                    _w(e, vE, ev[("z1l", half, m)])
                    par = m % 2
                    s_z1sp[par].dma(e.dma_start(out=z1_scr[0][m, :, half * HALF:(half + 1) * HALF],
                                                in_=z1hb[:, par * HALF:(par + 1) * HALF]))
                    s_z1sp[par].dma(e.dma_start(out=z1_scr[1][m, :, half * HALF:(half + 1) * HALF],
                                                in_=z1lb[:, par * HALF:(par + 1) * HALF]))
                    ev[("z1sp", half, m)] = (s_z1sp[par], s_z1sp[par].v)
            B.tensor(f_te)
            B.scalar(f_act)
            B.vector(f_dve)
            B.sync(f_sync)
            B.gpsimd(f_gp)

        def f_dve_fin(e):
            _w(e, tE, ev[("stats", half, NIC - 1)])
            vE.inc(e.tensor_scalar(out=mu1r[0:1, half * HALF:(half + 1) * HALF], in0=psZs[:],
                                   scalar1=1.0 / D, scalar2=None, op0=ALU.mult))
            vE.inc(e.tensor_scalar(out=srow[:], in0=psQs[:], scalar1=1.0 / D, scalar2=None,
                                   op0=ALU.mult))
            mu = mu1r[0:1, half * HALF:(half + 1) * HALF]
            sw(e, vE)
            vE.inc(e.tensor_tensor(out=srow2[:], in0=mu, in1=mu, op=ALU.mult))
            sw(e, vE)
            vE.inc(e.tensor_tensor(out=srow[:], in0=srow[:], in1=srow2[:], op=ALU.subtract))
            sw(e, vE)
            vE.inc(e.tensor_scalar(out=srow[:], in0=srow[:], scalar1=1e-5, scalar2=None,
                                   op0=ALU.add))
            ev[("ln1_var", half)] = vE.v
        B.vector(f_dve_fin)

        def f_act_fin(e):
            _w(e, vE, ev[("ln1_var", half)])
            aE.inc(e.activation(out=srow[:], in_=srow[:], func=ACTF.Sqrt, bias=0.0, scale=1.0))
            ev[("ln1_sqrt", half)] = aE.v
        B.scalar(f_act_fin)

        def f_dve_fin2(e):
            _w(e, aE, ev[("ln1_sqrt", half)])
            vE.inc(e.reciprocal(out=rs1r[0:1, half * HALF:(half + 1) * HALF], in_=srow[:]))
        B.vector(f_dve_fin2)
        barrier()

    if MAXPH >= 2:
        for half in range(2):
            p2_half(half)
            ev = {k: v for k, v in ev.items() if not isinstance(k, tuple)}

    ps2.close()
    es2.close()

    es3 = ExitStack()
    zrh = es3.enter_context(nc.sbuf_tensor("zrh", [128, 2 * HALF], BF16))
    zrl = es3.enter_context(nc.sbuf_tensor("zrl", [128, 2 * HALF], BF16))
    x1tmp = es3.enter_context(nc.sbuf_tensor("x1tmp", [128, 2 * HALF], F32))
    es3b = ExitStack()
    dotk = es3b.enter_context(nc.sbuf_tensor("dotk", [128, 256], F32))
    psD_sb = es3b.enter_context(nc.sbuf_tensor("psD_sb", [64, 512], F32))
    ebuf = es3b.enter_context(nc.sbuf_tensor("ebuf", [128, 64], F32))
    ebufb = es3b.enter_context(nc.sbuf_tensor("ebufb", [128, 64], BF16))
    ebuf_oh = es3b.enter_context(nc.sbuf_tensor("ebuf_oh", [64, 128], F32))
    ohb = es3b.enter_context(nc.sbuf_tensor("ohb", [64, 128], BF16))
    dnegm = es3b.enter_context(nc.sbuf_tensor("dnegm", [128, 1], F32))
    dsum = es3b.enter_context(nc.sbuf_tensor("dsum", [128, 1], F32))
    dthr = es3b.enter_context(nc.sbuf_tensor("dthr", [128, 1], F32))
    q01 = es3b.enter_context(nc.sbuf_tensor("q01", [128, 1], F32))
    centf = es3b.enter_context(nc.sbuf_tensor("centf", [64, D], F32))
    cth = es3b.enter_context(nc.sbuf_tensor("cth", [128, NIC * 64], BF16))
    ctl = es3b.enter_context(nc.sbuf_tensor("ctl", [128, NIC * 64], BF16))
    cnt_s = es3b.enter_context(nc.sbuf_tensor("cnt_s", [64, 1], F32))
    glob_s = es3b.enter_context(nc.sbuf_tensor("glob_s", [64, 1], F32))
    arowb = es3b.enter_context(nc.sbuf_tensor("arowb", [1, 128], BF16))
    ar_i = es3b.enter_context(nc.sbuf_tensor("ar_i", [1, D], I32))
    ar_f = es3b.enter_context(nc.sbuf_tensor("ar_f", [1, D], F32))
    qrow = es3b.enter_context(nc.sbuf_tensor("qrow", [1, TPC], F32))
    qrowb = es3b.enter_context(nc.sbuf_tensor("qrowb", [1, TPC], BF16))
    ps3 = ExitStack()
    psD = ps3.enter_context(nc.psum_tensor("psD", [64, 512], F32))
    psT3 = ps3.enter_context(nc.psum_tensor("psT3", [128, 1024], F32))
    psC = ps3.enter_context(nc.psum_tensor("psC", [64, 512], F32))
    psB3 = ps3.enter_context(nc.psum_tensor("psB3", [128, 512], F32))
    psR3 = ps3.enter_context(nc.psum_tensor("psR3", [128, 512], F32))

    def f_sync_c(e):
        if MAXPH < 3:
            return
        s_cst.dma(e.dma_start(out=centf[:], in_=cent))
        s_cst.dma(e.dma_start(out=ar_i[:], in_=assign_row))
        ev["p3_loads"] = (s_cst, s_cst.v)
    B.sync(f_sync_c)

    def f_dve_ar(e):
        if MAXPH < 3:
            return
        wde(e, "p3_loads")
        vE.inc(e.tensor_copy(out=ar_f[:], in_=ar_i[:]))
    B.vector(f_dve_ar)

    if MAXPH >= 3:
        tsp_rounds("ct", "c", NIC,
               (lambda i: centf[:, i * 128:(i + 1) * 128]),
               (lambda i: cth[:, i * 64:(i + 1) * 64]),
               (lambda i: ctl[:, i * 64:(i + 1) * 64]),
               psT3, 4, 64,
                   pre_wait=(lambda e: wde(e, "p3_loads")),
                   scale=10.0, ident_ap=idn[0:64, 0:64])

    def bcast_rows(tag, srcrow, dst, half, psb):
        def f_act(e):
            _w(e, vE, vE.v)
            _w(e, tE, tE.v)
            aE.inc(e.activation(out=rowh[:], in_=srcrow[0:1, half * HALF:(half + 1) * HALF],
                                func=ACTF.Copy, bias=0.0, scale=1.0))
            ev[(tag + "_rh", half)] = aE.v
        def f_dve(e):
            _w(e, aE, ev[(tag + "_rh", half)])
            vE.inc(e.scalar_tensor_tensor(out=rowl[:],
                                          in0=srcrow[0:1, half * HALF:(half + 1) * HALF],
                                          scalar=1.0, in1=rowh[:], op0=ALU.mult,
                                          op1=ALU.subtract))
            ev[(tag + "_rl", half)] = vE.v
        def f_te(e):
            _w(e, aE, ev[(tag + "_rh", half)])
            _w(e, vE, ev[(tag + "_rl", half)])
            e.matmul(psb[:], onesb[0:1, :], rowh[:], start=True, stop=False)
            mmb = e.matmul(psb[:], onesb[0:1, :], rowl[:], start=False, stop=True)
            tE.inc(mmb)
            ev[(tag + "_mm", half)] = tE.v
        def f_dve2(e):
            _w(e, tE, ev[(tag + "_mm", half)])
            vE.inc(e.tensor_copy(out=dst[:], in_=psb[:]))
            ev[(tag + "_bc", half)] = vE.v
        B.scalar(f_act)
        B.vector(f_dve)
        B.tensor(f_te)
        B.vector(f_dve2)

    def x1_build(half, with_dots):
        for ic in range(NIC + 2):
            def f_gp(e, ic=ic):
                if ic < NIC:
                    if ("x1f", half, ic - 2) in ev:
                        _w(e, vE, ev[("x1f", half, ic - 2)])
                    par = ic % 2
                    s_zr[par].dma(e.dma_start(out=zrh[:, par * HALF:(par + 1) * HALF],
                                              in_=z1_scr[0][ic, :, half * HALF:(half + 1) * HALF]))
                    s_zr[par].dma(e.dma_start(out=zrl[:, par * HALF:(par + 1) * HALF],
                                              in_=z1_scr[1][ic, :, half * HALF:(half + 1) * HALF]))
                    ev[("zr", half, ic)] = (s_zr[par], s_zr[par].v)
            def f_dve(e, ic=ic):
                m = ic - 1
                if 0 <= m < NIC:
                    if m == 0:
                        _w(e, vE, ev[("ln1mu_bc", half)])
                        _w(e, vE, ev[("ln1rs_bc", half)])
                    par = m % 2
                    wde(e, ("zr", half, m))
                    if ("x1h", half, m - 2) in ev:
                        _w(e, aE, ev[("x1h", half, m - 2)])
                    sw(e, vE)
                    x1s = x1tmp[:, par * HALF:(par + 1) * HALF]
                    vE.inc(e.tensor_tensor(out=x1s, in0=zrh[:, par * HALF:(par + 1) * HALF],
                                           in1=zrl[:, par * HALF:(par + 1) * HALF], op=ALU.add))
                    sw(e, vE)
                    vE.inc(e.tensor_tensor(out=x1s, in0=x1s, in1=mu_b[:], op=ALU.subtract))
                    sw(e, vE)
                    vE.inc(e.tensor_tensor(out=x1s, in0=x1s, in1=r_b[:], op=ALU.mult))
                    ev[("x1f", half, m)] = vE.v
            def f_act(e, ic=ic):
                m = ic - 1
                if 0 <= m < NIC:
                    par = m % 2
                    _w(e, vE, ev[("x1f", half, m)])
                    aE.inc(e.activation(out=acth[:, m * HALF:(m + 1) * HALF],
                                        in_=x1tmp[:, par * HALF:(par + 1) * HALF],
                                        func=ACTF.Copy, bias=0.0, scale=1.0))
                    ev[("x1h", half, m)] = aE.v
            def f_dve2(e, ic=ic):
                m = ic - 1
                if 0 <= m < NIC:
                    par = m % 2
                    _w(e, aE, ev[("x1h", half, m)])
                    vE.inc(e.tensor_tensor(out=actl[:, m * HALF:(m + 1) * HALF],
                                           in0=x1tmp[:, par * HALF:(par + 1) * HALF],
                                           in1=acth[:, m * HALF:(m + 1) * HALF],
                                           op=ALU.subtract))
                    ev[("x1l", half, m)] = vE.v
            def f_te(e, ic=ic):
                if not with_dots:
                    return
                m = ic - 1
                if 0 <= m < NIC:
                    if m == 0 and ("ct_hi", "c") in ev:
                        _w(e, aE, ev[("ct_hi", "c")])
                        _w(e, vE, ev[("ct_lo", "c")])
                    _w(e, aE, ev[("x1h", half, m)])
                    _w(e, vE, ev[("x1l", half, m)])
                    ch = cth[:, m * 64:(m + 1) * 64]
                    cl = ctl[:, m * 64:(m + 1) * 64]
                    xh = acth[:, m * HALF:(m + 1) * HALF]
                    xl = actl[:, m * HALF:(m + 1) * HALF]
                    e.matmul(psD[:], ch, xh, start=(m == 0), stop=False)
                    e.matmul(psD[:], ch, xl, start=False, stop=False)
                    mmd = e.matmul(psD[:], cl, xh, start=False, stop=(m == NIC - 1))
                    tE.inc(mmd)
                    ev[("dmm", half, m)] = tE.v
            B.gpsimd(f_gp)
            B.vector(f_dve)
            B.scalar(f_act)
            B.vector(f_dve2)
            B.tensor(f_te)

    def p3_half(half):
        bcast_rows("ln1mu", mu1r, mu_b, half, psB3)
        bcast_rows("ln1rs", rs1r, r_b, half, psB3)
        x1_build(half, with_dots=True)

        def f_dve_cp(e):
            _w(e, tE, ev[("dmm", half, NIC - 1)])
            vE.inc(e.tensor_copy(out=psD_sb[:], in_=psD[:]))
            ev[("dot_sb", half)] = vE.v
        B.vector(f_dve_cp)

        for tt in range(4):
            def f_te(e, tt=tt):
                _w(e, vE, ev[("dot_sb", half)])
                if tt > 0:
                    _w(e, vE, ev[("dot_ev", half, tt - 1)])
                    _w(e, vE, ev[("q_ev", half, tt - 1)])
                tE.inc(e.transpose(psT3[:, 0:64], psD_sb[:, tt * 128:(tt + 1) * 128],
                                   idn[0:64, 0:64]))
                ev[("dot_tr", half, tt)] = tE.v
            B.tensor(f_te)

            def f_dve(e, tt=tt):
                _w(e, tE, ev[("dot_tr", half, tt)])
                vE.inc(e.tensor_copy(out=dotk[:, tt * 64:(tt + 1) * 64], in_=psT3[:, 0:64]))
                ev[("dot_ev", half, tt)] = vE.v
                sw(e, vE)
                if ("dot_exp", half, tt - 1) in ev:
                    _w(e, aE, ev[("dot_exp", half, tt - 1)])
                vE.inc(e.reduce_max(out=dnegm[:], in_=dotk[:, tt * 64:(tt + 1) * 64],
                                    axis=AX, negate=True))
                ev[("dot_max", half, tt)] = vE.v
            B.vector(f_dve)

            def f_act(e, tt=tt):
                _w(e, vE, ev[("dot_max", half, tt)])
                aE.inc(e.activation(out=ebuf[:], in_=dotk[:, tt * 64:(tt + 1) * 64],
                                    func=ACTF.Exp, bias=dnegm[:], scale=1.0, accum_out=dsum[:]))
                ev[("dot_exp", half, tt)] = aE.v
            B.scalar(f_act)

            def f_dve2(e, tt=tt):
                _w(e, aE, ev[("dot_exp", half, tt)])
                vE.inc(e.tensor_scalar(out=dthr[:], in0=dsum[:], scalar1=0.01, scalar2=None,
                                       op0=ALU.mult))
                sw(e, vE)
                vE.inc(e.tensor_scalar(out=ebuf[:], in0=ebuf[:], scalar1=dthr[:], scalar2=None,
                                       op0=ALU.is_gt))
                sw(e, vE)
                vE.inc(e.reduce_max(out=q01[:], in_=ebuf[:], axis=AX))
                ev[("dot_hits", half, tt)] = vE.v
            B.vector(f_dve2)

            def f_act2(e, tt=tt):
                _w(e, vE, ev[("dot_hits", half, tt)])
                aE.inc(e.activation(out=ebufb[:], in_=ebuf[:], func=ACTF.Copy, bias=0.0,
                                    scale=1.0))
                ev[("dot_hb", half, tt)] = aE.v
            B.scalar(f_act2)

            def f_te2(e, tt=tt):
                _w(e, aE, ev[("dot_hb", half, tt)])
                st = (half == 0 and tt == 0)
                sp = (half == 1 and tt == 3)
                mmc = e.matmul(psC[:, 0:1], ebufb[:], onesb[:, 0:1], start=st, stop=sp)
                tE.inc(mmc)
                ev[("cnt", half, tt)] = tE.v
                _w(e, vE, ev[("dot_hits", half, tt)])
                _w(e, vE, ev[("dot_ev", half, tt)])
                tE.inc(e.transpose(psT3[0:1, 384:512], q01[:], idn[:]))
                ev[("q_tr", half, tt)] = tE.v
            B.tensor(f_te2)

            def f_dve3(e, tt=tt):
                _w(e, tE, ev[("q_tr", half, tt)])
                vE.inc(e.tensor_copy(
                    out=qrow[0:1, half * HALF + tt * 128:half * HALF + (tt + 1) * 128],
                    in_=psT3[0:1, 384:512]))
                ev[("q_ev", half, tt)] = vE.v
            B.vector(f_dve3)

    if MAXPH >= 3:
        p3_half(0)
        barrier()
        p3_half(1)
    barrier()

    def f_dve_cnt(e):
        if MAXPH < 3:
            return
        vE.inc(e.tensor_copy(out=cnt_s[:], in_=psC[:, 0:1]))
        ev["cnt_sb"] = vE.v
    B.vector(f_dve_cnt)

    def f_gp_cc(e):
        if MAXPH < 3:
            return
        _w(e, vE, ev["cnt_sb"])
        s_cc.dma(e.dma_start(out=cc_in.ap(), in_=cnt_s[:]))
        e.wait_ge(s_cc.h, s_cc.v)
        cS.inc(e.collective_compute(
            "AllReduce", ALU.add, replica_groups=[[0, 1, 2, 3, 4, 5, 6, 7]],
            ins=[cc_in.ap().opt()], outs=[cc_out.ap().opt()]))
        e.wait_ge(cS.h, cS.v)
        s_cc.dma(e.dma_start(out=glob_s[:], in_=cc_out.ap()))
        ev["cc_done"] = (s_cc, s_cc.v)
    B.gpsimd(f_gp_cc)

    def f_dve_cs(e):
        if MAXPH < 3:
            return
        wde(e, "cc_done")
        vE.inc(e.tensor_scalar(out=glob_s[:], in0=glob_s[:], scalar1=0.5, scalar2=None,
                               op0=ALU.is_ge))
        ev["cs01"] = vE.v
    B.vector(f_dve_cs)

    def f_act_cs(e):
        if MAXPH < 3:
            return
        _w(e, vE, ev["cs01"])
        aE.inc(e.activation(out=csb[:], in_=glob_s[:], func=ACTF.Copy, bias=0.0, scale=1.0))
        ev["csb"] = aE.v
    B.scalar(f_act_cs)

    for oc in range(NIC if MAXPH >= 3 else 0):
        def f_act(e, oc=oc):
            if oc == 0:
                _w(e, aE, ev["csb"])
            if ("abc", oc - 1) in ev:
                _w(e, tE, ev[("abc", oc - 1)])
            aE.inc(e.activation(out=arowb[:], in_=ar_f[0:1, oc * 128:(oc + 1) * 128],
                                func=ACTF.Copy, bias=0.0, scale=1.0))
            ev[("arowb", oc)] = aE.v
        def f_te(e, oc=oc):
            _w(e, aE, ev[("arowb", oc)])
            if ("abc01", oc - 1) in ev:
                _w(e, vE, ev[("abc01", oc - 1)])
            mma = e.matmul(psB3[0:64, 0:128], onesb[0:1, 0:64], arowb[:],
                           start=True, stop=True)
            tE.inc(mma)
            ev[("abc", oc)] = tE.v
        def f_dve(e, oc=oc):
            _w(e, tE, ev[("abc", oc)])
            vE.inc(e.tensor_scalar(out=ebuf_oh[:], in0=psB3[0:64, 0:128], scalar1=iota_s[:],
                                   scalar2=None, op0=ALU.is_equal))
            ev[("abc01", oc)] = vE.v
        def f_act2(e, oc=oc):
            _w(e, vE, ev[("abc01", oc)])
            if ("rsmm", oc - 1) in ev:
                _w(e, tE, ev[("rsmm", oc - 1)])
            aE.inc(e.activation(out=ohb[:], in_=ebuf_oh[:], func=ACTF.Copy, bias=0.0, scale=1.0))
            ev[("ohb", oc)] = aE.v
        def f_te2(e, oc=oc):
            _w(e, aE, ev[("ohb", oc)])
            if ("rs", oc - 1) in ev:
                _w(e, vE, ev[("rs", oc - 1)])
            mmr = e.matmul(psR3[:, 0:1], ohb[:], csb[:], start=True, stop=True)
            tE.inc(mmr)
            ev[("rsmm", oc)] = tE.v
        def f_dve2(e, oc=oc):
            _w(e, tE, ev[("rsmm", oc)])
            vE.inc(e.tensor_copy(out=rs_s[:, oc:oc + 1], in_=psR3[:, 0:1]))
            ev[("rs", oc)] = vE.v
        B.scalar(f_act)
        B.tensor(f_te)
        B.vector(f_dve)
        B.scalar(f_act2)
        B.tensor(f_te2)
        B.vector(f_dve2)

    for half in range(2 if MAXPH >= 3 else 0):
        def f_act(e, half=half):
            _w(e, vE, ev[("q_ev", half, 3)] if ("q_ev", half, 3) in ev else 0)
            aE.inc(e.activation(out=qrowb[0:1, half * HALF:(half + 1) * HALF],
                                in_=qrow[0:1, half * HALF:(half + 1) * HALF],
                                func=ACTF.Copy, bias=0.0, scale=1.0))
            ev[("qrb", half)] = aE.v
        def f_te(e, half=half):
            _w(e, aE, ev[("qrb", half)])
            _w(e, vE, vE.v)
            mmq3 = e.matmul(psB3[:], onesb[0:1, :],
                            qrowb[0:1, half * HALF:(half + 1) * HALF],
                            start=True, stop=True)
            tE.inc(mmq3)
            ev[("qmm3", half)] = tE.v
        def f_dve(e, half=half):
            _w(e, tE, ev[("qmm3", half)])
            vE.inc(e.tensor_copy(out=q_b[:, half * HALF:(half + 1) * HALF], in_=psB3[:]))
            ev[("qbc", half)] = vE.v
        B.scalar(f_act)
        B.tensor(f_te)
        B.vector(f_dve)
    barrier()
    ps3.close()
    es3b.close()

    es4 = ExitStack()
    wfF = es4.enter_context(nc.sbuf_tensor("wfF", [128, D], F32))
    biasf4 = es4.enter_context(nc.sbuf_tensor("biasf4", [1, D], F32))
    biash = es4.enter_context(nc.sbuf_tensor("biash", [1, D], BF16))
    biasl = es4.enter_context(nc.sbuf_tensor("biasl", [1, D], BF16))
    wth = es4.enter_context(nc.sbuf_tensor("wth", [128, NIC * 128], BF16))
    wtl = es4.enter_context(nc.sbuf_tensor("wtl", [128, NIC * 128], BF16))
    gsf = es4.enter_context(nc.sbuf_tensor("gsf", [128, 2 * HALF], F32))
    z2ff = es4.enter_context(nc.sbuf_tensor("z2ff", [128, 2 * HALF], F32))
    z2hb = es4.enter_context(nc.sbuf_tensor("z2hb", [128, 2 * HALF], BF16))
    z2lb = es4.enter_context(nc.sbuf_tensor("z2lb", [128, 2 * HALF], BF16))
    sq2f = es4.enter_context(nc.sbuf_tensor("sq2f", [128, 2 * HALF], F32))
    sq2hb = es4.enter_context(nc.sbuf_tensor("sq2hb", [128, 2 * HALF], BF16))
    sq2lb = es4.enter_context(nc.sbuf_tensor("sq2lb", [128, 2 * HALF], BF16))
    ps4 = ExitStack()
    psF = ps4.enter_context(nc.psum_tensor("psF", [128, 1024], F32))
    psW = ps4.enter_context(nc.psum_tensor("psW", [128, 1024], F32))
    psZs4 = ps4.enter_context(nc.psum_tensor("psZs4", [1, 512], F32))
    psQs4 = ps4.enter_context(nc.psum_tensor("psQs4", [1, 512], F32))
    psB4 = ps4.enter_context(nc.psum_tensor("psB4", [128, 512], F32))

    def f_sync_b4(e):
        if MAXPH < 4:
            return
        s_cst.dma(e.dma_start(out=biasf4[:], in_=biasr))
        ev["p4_bias"] = (s_cst, s_cst.v)
    B.sync(f_sync_b4)

    def f_act_b4(e):
        if MAXPH < 4:
            return
        wde(e, "p4_bias")
        aE.inc(e.activation(out=biash[:], in_=biasf4[:], func=ACTF.Copy, bias=0.0, scale=1.0))
        ev["p4_biash"] = aE.v
    B.scalar(f_act_b4)

    def f_dve_b4(e):
        if MAXPH < 4:
            return
        _w(e, aE, ev["p4_biash"])
        vE.inc(e.tensor_tensor(out=biasl[:], in0=biasf4[:], in1=biash[:], op=ALU.subtract))
        ev["q_ones"] = vE.v
    B.vector(f_dve_b4)

    def p4_half(half):
        bcast_rows("ln1mu", mu1r, mu_b, half, psB4)
        bcast_rows("ln1rs", rs1r, r_b, half, psB4)
        x1_build(half, with_dots=False)

        def f_sync0(e):
            s_wf.dma(e.dma_start(out=wfF[:], in_=wf[0:128, :]))
            ev[("wfq", half, 0)] = (s_wf, s_wf.v)
        B.sync(f_sync0)

        def wt_group_secs(oc):
            if oc >= NIC:
                return
            for g in range(8):
                def f_te(e, oc=oc, g=g):
                    if g == 0:
                        wde(e, ("wfq", half, oc))
                    if g == 0 and oc == 0:
                        _w(e, aE, ev[("x1h", half, NIC - 1)])
                        _w(e, vE, ev[("x1l", half, NIC - 1)])
                    if g >= 2:
                        _w(e, vE, ev[("wt_l", half, oc, g - 2)])
                    elif oc > 0:
                        _w(e, vE, ev[("wt_l", half, oc - 1, 6 + g)])
                    tpar = g % 2
                    for u in range(4):
                        ic = g * 4 + u
                        tE.inc(e.transpose(
                            psW[:, tpar * 512 + u * 128:tpar * 512 + (u + 1) * 128],
                            wfF[:, ic * 128:(ic + 1) * 128], idn[:]))
                    ev[("wt_tr", half, oc, g)] = tE.v

                def f_act(e, oc=oc, g=g):
                    _w(e, tE, ev[("wt_tr", half, oc, g)])
                    if ("fmm", half, oc - 1) in ev:
                        _w(e, tE, ev[("fmm", half, oc - 1)])
                    tpar = g % 2
                    aE.inc(e.activation(out=wth[:, g * 512:(g + 1) * 512],
                                        in_=psW[:, tpar * 512:(tpar + 1) * 512],
                                        func=ACTF.Copy, bias=0.0, scale=1.0))
                    ev[("wt_h", half, oc, g)] = aE.v

                def f_dve(e, oc=oc, g=g):
                    _w(e, aE, ev[("wt_h", half, oc, g)])
                    tpar = g % 2
                    vE.inc(e.tensor_tensor(out=wtl[:, g * 512:(g + 1) * 512],
                                           in0=psW[:, tpar * 512:(tpar + 1) * 512],
                                           in1=wth[:, g * 512:(g + 1) * 512],
                                           op=ALU.subtract))
                    ev[("wt_l", half, oc, g)] = vE.v
                B.tensor(f_te)
                B.scalar(f_act)
                B.vector(f_dve)

        for oc in range(NIC + 3):
            def f_te(e, oc=oc):
                m = oc - 1
                if 0 <= m < NIC:
                    _w(e, aE, ev[("wt_h", half, m, 7)])
                    _w(e, vE, ev[("wt_l", half, m, 7)])
                    _w(e, vE, ev["q_ones"])
                    _w(e, vE, ev["q_ones_v"])
                    if ("gs", half, m - 2) in ev:
                        _w(e, vE, ev[("gs", half, m - 2)])
                    par = m % 2
                    dst = psF[:, par * 512:(par + 1) * 512]
                    for ic in range(NIC):
                        wh = wth[:, ic * 128:(ic + 1) * 128]
                        wl = wtl[:, ic * 128:(ic + 1) * 128]
                        xh = acth[:, ic * HALF:(ic + 1) * HALF]
                        xl = actl[:, ic * HALF:(ic + 1) * HALF]
                        e.matmul(dst, wh, xh, start=(ic == 0), stop=False)
                        e.matmul(dst, wh, xl, start=False, stop=False)
                        e.matmul(dst, wl, xh, start=False, stop=False)
                    e.matmul(dst, biash[0:1, m * 128:(m + 1) * 128], ones_row[:],
                             start=False, stop=False)
                    mmf = e.matmul(dst, biasl[0:1, m * 128:(m + 1) * 128], ones_row[:],
                                   start=False, stop=True)
                    tE.inc(mmf)
                    ev[("fmm", half, m)] = tE.v
                m = oc - 3
                if 0 <= m < NIC:
                    par = m % 2
                    _w(e, aE, ev[("z2sq_h", half, m)])
                    _w(e, vE, ev[("z2sq_l", half, m)])
                    st = m == 0
                    e.matmul(psZs4[:], onesb[:, 0:1], z2hb[:, par * HALF:(par + 1) * HALF],
                             start=st, stop=False)
                    e.matmul(psZs4[:], onesb[:, 0:1], z2lb[:, par * HALF:(par + 1) * HALF],
                             start=False, stop=(m == NIC - 1))
                    e.matmul(psQs4[:], onesb[:, 0:1], sq2hb[:, par * HALF:(par + 1) * HALF],
                             start=st, stop=False)
                    mmst4 = e.matmul(psQs4[:], onesb[:, 0:1],
                                     sq2lb[:, par * HALF:(par + 1) * HALF],
                                     start=False, stop=(m == NIC - 1))
                    tE.inc(mmst4)
                    ev[("stats4", half, m)] = tE.v

            def f_act(e, oc=oc):
                m = oc - 2
                if 0 <= m < NIC:
                    par = m % 2
                    _w(e, vE, ev[("z2f", half, m)])
                    if ("stats4", half, m - 2) in ev:
                        _w(e, tE, ev[("stats4", half, m - 2)])
                    if ("z2sp", half, m - 2) in ev:
                        wde(e, ("z2sp", half, m - 2))
                    aE.inc(e.activation(out=z2hb[:, par * HALF:(par + 1) * HALF],
                                        in_=z2ff[:, par * HALF:(par + 1) * HALF],
                                        func=ACTF.Copy, bias=0.0, scale=1.0))
                    ev[("z2h", half, m)] = aE.v
                    aE.inc(e.activation(out=sq2f[:, par * HALF:(par + 1) * HALF],
                                        in_=z2ff[:, par * HALF:(par + 1) * HALF],
                                        func=ACTF.Square, bias=0.0, scale=1.0))
                    sw(e, aE)
                    aE.inc(e.activation(out=sq2hb[:, par * HALF:(par + 1) * HALF],
                                        in_=sq2f[:, par * HALF:(par + 1) * HALF],
                                        func=ACTF.Copy, bias=0.0, scale=1.0))
                    ev[("z2sq_h", half, m)] = aE.v

            def f_dve(e, oc=oc):
                m = oc - 1
                if 0 <= m < NIC:
                    _w(e, tE, ev[("fmm", half, m)])
                    _w(e, vE, ev[("rs", m)])
                    _w(e, vE, ev[("qbc", half)])
                    par = m % 2
                    gs = gsf[:, par * HALF:(par + 1) * HALF]
                    vE.inc(e.scalar_tensor_tensor(out=gs, in0=psF[:, par * 512:(par + 1) * 512],
                                                  scalar=rs_s[:, m:m + 1],
                                                  in1=q_b[:, half * HALF:(half + 1) * HALF],
                                                  op0=ALU.mult, op1=ALU.mult))
                    ev[("gs", half, m)] = vE.v
                    if ("z2sq_h", half, m - 2) in ev:
                        _w(e, aE, ev[("z2sq_h", half, m - 2)])
                    z2s = z2ff[:, par * HALF:(par + 1) * HALF]
                    sw(e, vE)
                    vE.inc(e.tensor_tensor(out=z2s, in0=gs,
                                           in1=acth[:, m * HALF:(m + 1) * HALF], op=ALU.add))
                    sw(e, vE)
                    vE.inc(e.tensor_tensor(out=z2s, in0=z2s,
                                           in1=actl[:, m * HALF:(m + 1) * HALF], op=ALU.add))
                    ev[("z2f", half, m)] = vE.v
                m = oc - 2
                if 0 <= m < NIC:
                    par = m % 2
                    _w(e, aE, ev[("z2h", half, m)])
                    vE.inc(e.tensor_tensor(out=z2lb[:, par * HALF:(par + 1) * HALF],
                                           in0=z2ff[:, par * HALF:(par + 1) * HALF],
                                           in1=z2hb[:, par * HALF:(par + 1) * HALF],
                                           op=ALU.subtract))
                    ev[("z2l", half, m)] = vE.v
                    _w(e, aE, ev[("z2sq_h", half, m)])
                    vE.inc(e.tensor_tensor(out=sq2lb[:, par * HALF:(par + 1) * HALF],
                                           in0=sq2f[:, par * HALF:(par + 1) * HALF],
                                           in1=sq2hb[:, par * HALF:(par + 1) * HALF],
                                           op=ALU.subtract))
                    ev[("z2sq_l", half, m)] = vE.v

            def f_sync(e, oc=oc):
                m = oc + 1
                if m < NIC:
                    _w(e, aE, ev[("wt_h", half, m - 1, 7)])
                    _w(e, vE, ev[("wt_l", half, m - 1, 7)])
                    s_wf.dma(e.dma_start(out=wfF[:], in_=wf[m * 128:(m + 1) * 128, :]))
                    ev[("wfq", half, m)] = (s_wf, s_wf.v)

            def f_gp(e, oc=oc):
                m = oc - 2
                if 0 <= m < NIC:
                    _w(e, vE, ev[("z2l", half, m)])
                    par = m % 2
                    s_z2sp[par].dma(e.dma_start(out=z2_scr[0][m, :, half * HALF:(half + 1) * HALF],
                                                in_=z2hb[:, par * HALF:(par + 1) * HALF]))
                    s_z2sp[par].dma(e.dma_start(out=z2_scr[1][m, :, half * HALF:(half + 1) * HALF],
                                                in_=z2lb[:, par * HALF:(par + 1) * HALF]))
                    ev[("z2sp", half, m)] = (s_z2sp[par], s_z2sp[par].v)
            B.tensor(f_te)
            wt_group_secs(oc)
            B.scalar(f_act)
            B.vector(f_dve)
            B.sync(f_sync)
            B.gpsimd(f_gp)

        def f_dve_fin(e):
            _w(e, tE, ev[("stats4", half, NIC - 1)])
            vE.inc(e.tensor_scalar(out=mu2r[0:1, half * HALF:(half + 1) * HALF], in0=psZs4[:],
                                   scalar1=1.0 / D, scalar2=None, op0=ALU.mult))
            vE.inc(e.tensor_scalar(out=srow[:], in0=psQs4[:], scalar1=1.0 / D, scalar2=None,
                                   op0=ALU.mult))
            mu = mu2r[0:1, half * HALF:(half + 1) * HALF]
            sw(e, vE)
            vE.inc(e.tensor_tensor(out=srow2[:], in0=mu, in1=mu, op=ALU.mult))
            sw(e, vE)
            vE.inc(e.tensor_tensor(out=srow[:], in0=srow[:], in1=srow2[:], op=ALU.subtract))
            sw(e, vE)
            vE.inc(e.tensor_scalar(out=srow[:], in0=srow[:], scalar1=1e-5, scalar2=None,
                                   op0=ALU.add))
            ev[("ln2_var", half)] = vE.v
        B.vector(f_dve_fin)

        def f_act_fin(e):
            _w(e, vE, ev[("ln2_var", half)])
            aE.inc(e.activation(out=srow[:], in_=srow[:], func=ACTF.Sqrt, bias=0.0, scale=1.0))
            ev[("ln2_sqrt", half)] = aE.v
        B.scalar(f_act_fin)

        def f_dve_fin2(e):
            _w(e, aE, ev[("ln2_sqrt", half)])
            vE.inc(e.reciprocal(out=rs2r[0:1, half * HALF:(half + 1) * HALF], in_=srow[:]))
        B.vector(f_dve_fin2)
        barrier()

    for half in range(2 if MAXPH >= 4 else 0):
        p4_half(half)
        keep = ("rs", "qbc", "z2sp", "q_ones")
        ev = {k: v for k, v in ev.items()
              if not isinstance(k, tuple) or k[0] in keep}

    ps4.close()
    es4.close()

    es5 = ExitStack()
    zoh = es5.enter_context(nc.sbuf_tensor("zoh", [128, 2 * HALF], BF16))
    zol = es5.enter_context(nc.sbuf_tensor("zol", [128, 2 * HALF], BF16))
    x2tmp = es5.enter_context(nc.sbuf_tensor("x2tmp", [128, 2 * HALF], F32))
    ostg = es5.enter_context(nc.sbuf_tensor("ostg", [128, 2 * HALF], F32))
    ps5 = ExitStack()
    psX5 = ps5.enter_context(nc.psum_tensor("psX5", [128, 1024], F32))
    psB5 = ps5.enter_context(nc.psum_tensor("psB5", [128, 512], F32))

    def p5_half(half):
        bcast_rows("ln2mu", mu2r, mu_b, half, psB5)
        bcast_rows("ln2rs", rs2r, r_b, half, psB5)
        for ic in range(NIC + 2):
            def f_gp(e, ic=ic):
                if ic < NIC:
                    if ("x2f", half, ic - 2) in ev:
                        _w(e, vE, ev[("x2f", half, ic - 2)])
                    par = ic % 2
                    s_zo[par].dma(e.dma_start(out=zoh[:, par * HALF:(par + 1) * HALF],
                                              in_=z2_scr[0][ic, :, half * HALF:(half + 1) * HALF]))
                    s_zo[par].dma(e.dma_start(out=zol[:, par * HALF:(par + 1) * HALF],
                                              in_=z2_scr[1][ic, :, half * HALF:(half + 1) * HALF]))
                    ev[("zo", half, ic)] = (s_zo[par], s_zo[par].v)
            def f_dve(e, ic=ic):
                m = ic - 1
                if 0 <= m < NIC:
                    if m == 0:
                        _w(e, vE, ev[("ln2mu_bc", half)])
                        _w(e, vE, ev[("ln2rs_bc", half)])
                    par = m % 2
                    wde(e, ("zo", half, m))
                    if ("x2tr", half, m - 2) in ev:
                        _w(e, tE, ev[("x2tr", half, m - 2)])
                    sw(e, vE)
                    x2 = x2tmp[:, par * HALF:(par + 1) * HALF]
                    vE.inc(e.tensor_tensor(out=x2, in0=zoh[:, par * HALF:(par + 1) * HALF],
                                           in1=zol[:, par * HALF:(par + 1) * HALF], op=ALU.add))
                    sw(e, vE)
                    vE.inc(e.tensor_tensor(out=x2, in0=x2, in1=mu_b[:], op=ALU.subtract))
                    sw(e, vE)
                    vE.inc(e.tensor_tensor(out=x2, in0=x2, in1=r_b[:], op=ALU.mult))
                    ev[("x2f", half, m)] = vE.v
            def f_te(e, ic=ic):
                m = ic - 1
                if 0 <= m < NIC:
                    par = m % 2
                    _w(e, vE, ev[("x2f", half, m)])
                    if ("x2ev", half, m - 2) in ev:
                        _w(e, vE, ev[("x2ev", half, m - 2)])
                    for tt in range(4):
                        tE.inc(e.transpose(
                            psX5[:, par * 512 + tt * 128:par * 512 + (tt + 1) * 128],
                            x2tmp[:, par * HALF + tt * 128:par * HALF + (tt + 1) * 128],
                            idn[:]))
                    ev[("x2tr", half, m)] = tE.v
            def f_dve2(e, ic=ic):
                m = ic - 1
                if 0 <= m < NIC:
                    par = m % 2
                    _w(e, tE, ev[("x2tr", half, m)])
                    if ("ost", half, m - 2) in ev:
                        wde(e, ("ost", half, m - 2))
                    vE.inc(e.tensor_copy(out=ostg[:, par * HALF:(par + 1) * HALF],
                                         in_=psX5[:, par * 512:(par + 1) * 512]))
                    ev[("x2ev", half, m)] = vE.v
            def f_sync(e, ic=ic):
                m = ic - 1
                if 0 <= m < NIC:
                    par = m % 2
                    _w(e, vE, ev[("x2ev", half, m)])
                    for tt in range(4):
                        s_ost[par].dma(e.dma_start(
                            out=out[(half * 4 + tt) * 128:(half * 4 + tt + 1) * 128,
                                    m * 128:(m + 1) * 128],
                            in_=ostg[:, par * HALF + tt * 128:par * HALF + (tt + 1) * 128]))
                    ev[("ost", half, m)] = (s_ost[par], s_ost[par].v)
            B.gpsimd(f_gp)
            B.vector(f_dve)
            B.tensor(f_te)
            B.vector(f_dve2)
            B.sync(f_sync)

    for half in range(2 if MAXPH >= 5 else 0):
        p5_half(half)
        barrier()

    def f_sync_fin(e):
        for sset in (SY_SEMS, GP_SEMS):
            for sm in sset:
                if sm.v > 0:
                    e.wait_ge(sm.h, sm.v)
    B.sync(f_sync_fin)

    ps5.close()
    es5.close()
    es3.close()
    esA.close()
    B.__exit__(None, None, None)
    es.close()
    return nc


_NC_CACHE = {}


def kernel(L0_attn_hidden_states, L0_attn_Wq, L0_attn_Wkv, L0_attn_Wo,
           L0_ffn_weight, L0_ffn_bias, L0_ffn_centroids, L0_ffn_assignments):
    x = np.ascontiguousarray(np.asarray(L0_attn_hidden_states, dtype=np.float32))
    wq = np.ascontiguousarray(np.asarray(L0_attn_Wq, dtype=np.float32))
    wkv = np.ascontiguousarray(np.asarray(L0_attn_Wkv, dtype=np.float32))
    wo = np.ascontiguousarray(np.asarray(L0_attn_Wo, dtype=np.float32))
    wf = np.ascontiguousarray(np.asarray(L0_ffn_weight, dtype=np.float32))
    bias = np.ascontiguousarray(np.asarray(L0_ffn_bias, dtype=np.float32)).reshape(1, D)
    cent = np.ascontiguousarray(np.asarray(L0_ffn_centroids, dtype=np.float32))
    assign = np.ascontiguousarray(np.asarray(L0_ffn_assignments, dtype=np.int32)).reshape(1, D)

    slopes = _slopes()
    jj = np.arange(512, dtype=np.float32)[None, None, :]
    ii = np.arange(128, dtype=np.float32)[None, :, None]
    alibi = (slopes[:, None, None] * (jj - ii)).astype(np.float32)

    ident = np.eye(128, dtype=np.float32)
    ones128 = np.ones((128, 128), dtype=np.float32)
    iota = np.arange(64, dtype=np.float32).reshape(64, 1)

    in_maps = []
    for c in range(8):
        b, j = divmod(c, 2)
        ch = CHUNKS[j]
        x_own = np.concatenate([x[b, k * 128:(k + 1) * 128] for k in ch], axis=0)
        mk = np.zeros((8, 128, 640), dtype=np.float32)
        for sg in range(8):
            L = LS[sg] * 128
            t0 = ch[sg] * 128
            r0 = L - 640
            keys = r0 + np.arange(640)[None, :]
            toks = t0 + np.arange(128)[:, None]
            mk[sg] = np.where(keys <= toks, 0.0, NEG).astype(np.float32)
        in_maps.append({
            "x_own": x_own, "x_kv": x[b], "wq": wq, "wo": wo, "wf": wf,
            "wkv": wkv, "biasr": bias, "cent": cent, "assign_row": assign,
            "alibi": alibi, "masks": mk.astype(BF),
            "identf": ident, "onesf": ones128, "iota64": iota,
        })

    if "nc" not in _NC_CACHE:
        _NC_CACHE["nc"] = build_nc()
    nc = _NC_CACHE["nc"]
    res = run_bass_kernel_spmd(nc, in_maps, core_ids=list(range(8)))

    full = np.empty((Bb, S, D), dtype=np.float32)
    for c in range(8):
        b, j = divmod(c, 2)
        o = np.asarray(res.results[c]["out"])
        for k, chk in enumerate(CHUNKS[j]):
            full[b, chk * 128:(chk + 1) * 128] = o[k * 128:(k + 1) * 128]
    return full


# version 08
# speedup vs baseline: 1.0789x; 1.0789x over previous
import math
import numpy as np
import ml_dtypes
import concourse.bass as bass
import concourse.mybir as mybir
from concourse.bass_utils import run_bass_kernel_spmd
from contextlib import ExitStack

F32 = mybir.dt.float32
BF16 = mybir.dt.bfloat16
I32 = mybir.dt.int32
ALU = mybir.AluOpType
ACTF = mybir.ActivationFunctionType
AX = mybir.AxisListType.X
BF = ml_dtypes.bfloat16

Bb, S, D, NH, DH = 4, 2048, 4096, 32, 128
NIC = D // 128
TPC = 1024
HALF = 512
LS = [5, 6, 7, 8, 13, 14, 15, 16]
CHUNKS = ([0, 1, 2, 3, 12, 13, 14, 15], [4, 5, 6, 7, 8, 9, 10, 11])
SCALE = float(DH) ** -0.5
NEG = -1.0e30
MAXPH = 99


def _slopes():
    start = 2.0 ** (-2.0 ** (-(math.log2(NH) - 3)))
    return np.array([start ** (i + 1) for i in range(NH)], dtype=np.float32)


class Sem:
    def __init__(self, nc, es, name):
        self.h = es.enter_context(nc.semaphore(name))
        self.v = 0

    def inc(self, instr, k=1):
        instr.then_inc(self.h, k)
        self.v += k
        return self.v

    def dma(self, instr):
        return self.inc(instr, 16)


def _w(eng, sem, val):
    if val > 0:
        eng.wait_ge(sem.h, val)


def build_nc():
    nc = bass.Bass("TRN2", target_bir_lowering=False)
    x_own = nc.dram_tensor("x_own", [TPC, D], F32, kind="ExternalInput").ap()
    x_kv = nc.dram_tensor("x_kv", [S, D], F32, kind="ExternalInput").ap()
    wq = nc.dram_tensor("wq", [D, D], F32, kind="ExternalInput").ap()
    wo = nc.dram_tensor("wo", [D, D], F32, kind="ExternalInput").ap()
    wf = nc.dram_tensor("wf", [D, D], F32, kind="ExternalInput").ap()
    wkv = nc.dram_tensor("wkv", [D, 256], F32, kind="ExternalInput").ap()
    biasr = nc.dram_tensor("biasr", [1, D], F32, kind="ExternalInput").ap()
    cent = nc.dram_tensor("cent", [64, D], F32, kind="ExternalInput").ap()
    assign_row = nc.dram_tensor("assign_row", [1, D], I32, kind="ExternalInput").ap()
    alibi = nc.dram_tensor("alibi", [NH, 128, 512], F32, kind="ExternalInput").ap()
    masks = nc.dram_tensor("masks", [8, 128, 640], BF16, kind="ExternalInput").ap()
    identf = nc.dram_tensor("identf", [128, 128], F32, kind="ExternalInput").ap()
    onesf = nc.dram_tensor("onesf", [128, 128], F32, kind="ExternalInput").ap()
    iota64 = nc.dram_tensor("iota64", [64, 1], F32, kind="ExternalInput").ap()
    out = nc.dram_tensor("out", [TPC, D], F32, kind="ExternalOutput").ap()
    o_scr = [nc.dram_tensor(f"o_scr{p}", [NIC, 128, TPC], BF16).ap() for p in range(2)]
    xt_scr = [nc.dram_tensor(f"xt_scr{p}", [NIC, 128, TPC], BF16).ap() for p in range(2)]
    z1_scr = [nc.dram_tensor(f"z1_scr{p}", [NIC, 128, TPC], BF16).ap() for p in range(2)]
    z2_scr = [nc.dram_tensor(f"z2_scr{p}", [NIC, 128, TPC], BF16).ap() for p in range(2)]
    cc_in = nc.dram_tensor("cc_in", [64, 1], F32)
    cc_out = nc.dram_tensor("cc_out", [64, 1], F32)

    es = ExitStack()
    qS = Sem(nc, es, "qS")
    qG = Sem(nc, es, "qG")
    tE = Sem(nc, es, "tE")
    vE = Sem(nc, es, "vE")
    aE = Sem(nc, es, "aE")
    cS = Sem(nc, es, "cS")
    bar = Sem(nc, es, "bar")
    s_cst = Sem(nc, es, "s_cst")
    s_wkv = Sem(nc, es, "s_wkv")
    s_xnb = [Sem(nc, es, f"s_xnb{p}") for p in range(2)]
    s_mask = Sem(nc, es, "s_mask")
    s_xnat = Sem(nc, es, "s_xnat")
    s_wq = Sem(nc, es, "s_wq")
    s_al = [Sem(nc, es, f"s_al{p}") for p in range(2)]
    s_osp = Sem(nc, es, "s_osp")
    s_old = Sem(nc, es, "s_old")
    s_xoc = [Sem(nc, es, f"s_xoc{p}") for p in range(2)]
    s_wo = Sem(nc, es, "s_wo")
    s_wf = Sem(nc, es, "s_wf")
    s_z1sp = [Sem(nc, es, f"s_z1sp{p}") for p in range(2)]
    s_z2sp = [Sem(nc, es, f"s_z2sp{p}") for p in range(2)]
    s_zr = [Sem(nc, es, f"s_zr{p}") for p in range(2)]
    s_zo = [Sem(nc, es, f"s_zo{p}") for p in range(2)]
    s_ost = [Sem(nc, es, f"s_ost{p}") for p in range(2)]
    s_cc = Sem(nc, es, "s_cc")
    SY_SEMS = [qS, s_cst, s_wkv, s_xnb[0], s_xnb[1], s_mask, s_xnat, s_wq,
               s_al[0], s_al[1], s_wo, s_wf, s_ost[0], s_ost[1]]
    GP_SEMS = [qG, s_osp, s_old, s_xoc[0], s_xoc[1], s_z1sp[0], s_z1sp[1],
               s_z2sp[0], s_z2sp[1], s_zr[0], s_zr[1], s_zo[0], s_zo[1], s_cc]

    def wde(e, key):
        sm, v = ev[key]
        _w(e, sm, v)

    def sw(e, sem):
        e.wait_ge(sem.h, sem.v)

    idn = es.enter_context(nc.sbuf_tensor("idn", [128, 128], F32))
    onesb_f = es.enter_context(nc.sbuf_tensor("onesb_f", [128, 128], F32))
    onesb = es.enter_context(nc.sbuf_tensor("onesb", [128, 128], BF16))
    iota_s = es.enter_context(nc.sbuf_tensor("iota_s", [64, 1], F32))
    rs_s = es.enter_context(nc.sbuf_tensor("rs_s", [128, 32], F32))
    csb = es.enter_context(nc.sbuf_tensor("csb", [64, 1], BF16))

    B = nc.Block()
    B.__enter__()

    def barrier():
        tgt = bar.v + 5
        bar.v = tgt
        sy_vals = [(s.h, s.v) for s in SY_SEMS if s.v > 0]
        gp_vals = [(s.h, s.v) for s in GP_SEMS if s.v > 0]
        for attr, drains, dr in ((B.tensor, None, True), (B.scalar, None, True),
                                 (B.vector, None, True), (B.sync, sy_vals, False),
                                 (B.gpsimd, gp_vals, True)):
            def f(e, tgt=tgt, drains=drains, dr=dr):
                if drains:
                    for h, v in drains:
                        e.wait_ge(h, v)
                if dr:
                    e.drain()
                e.sem_inc(bar.h, 1)
                e.wait_ge(bar.h, tgt)
            attr(f)

    ev = {}

    def tsp_rounds(tag, key, ntiles, src_of, hi_of, lo_of, ps, slots, tile_w,
                   pre_wait=None, scale=1.0, src_p=128, ident_ap=None):
        idap = ident_ap if ident_ap is not None else idn[:]
        assert slots * tile_w <= 512
        nr = (ntiles + slots - 1) // slots

        def boff(r):
            return (r % 2) * 512

        for r in range(nr):
            i0, i1 = r * slots, min((r + 1) * slots, ntiles)

            def f_te(e, r=r, i0=i0, i1=i1):
                if r == 0 and pre_wait is not None:
                    pre_wait(e)
                if r >= 2 and (tag + "_r", key, r - 2) in ev:
                    _w(e, vE, ev[(tag + "_r", key, r - 2)])
                for i in range(i0, i1):
                    g = i - i0
                    o = boff(r) + g * tile_w
                    tE.inc(e.transpose(ps[:, o:o + tile_w], src_of(i), idap))
                ev[(tag + "_tr", key, r)] = tE.v

            def f_dve(e, r=r, i0=i0, i1=i1):
                _w(e, tE, ev[(tag + "_tr", key, r)])
                for i in range(i0, i1):
                    g = i - i0
                    o = boff(r) + g * tile_w
                    if scale == 1.0:
                        vE.inc(e.tensor_copy(out=hi_of(i), in_=ps[:, o:o + tile_w]))
                    else:
                        vE.inc(e.tensor_scalar(out=hi_of(i), in0=ps[:, o:o + tile_w],
                                               scalar1=scale, scalar2=None, op0=ALU.mult))
                ev[(tag + "_hr", key, r)] = vE.v
                sw(e, vE)
                for i in range(i0, i1):
                    g = i - i0
                    o = boff(r) + g * tile_w
                    if scale == 1.0:
                        vE.inc(e.tensor_tensor(out=lo_of(i), in0=ps[:, o:o + tile_w],
                                               in1=hi_of(i), op=ALU.subtract))
                    else:
                        vE.inc(e.scalar_tensor_tensor(
                            out=lo_of(i), in0=ps[:, o:o + tile_w],
                            scalar=scale, in1=hi_of(i), op0=ALU.mult, op1=ALU.subtract))
                ev[(tag + "_r", key, r)] = vE.v
            B.tensor(f_te)
            B.vector(f_dve)
        ev[(tag + "_hi", key)] = ev[(tag + "_hr", key, nr - 1)]
        ev[(tag + "_lo", key)] = ev[(tag + "_r", key, nr - 1)]
        ev[(tag + "_trlast", key)] = ev[(tag + "_tr", key, nr - 1)]

    es1kv = ExitStack()
    kth = es1kv.enter_context(nc.sbuf_tensor("kth", [128, S], BF16))
    ktl = es1kv.enter_context(nc.sbuf_tensor("ktl", [128, S], BF16))
    vth = es1kv.enter_context(nc.sbuf_tensor("vth", [128, S], BF16))
    vtl = es1kv.enter_context(nc.sbuf_tensor("vtl", [128, S], BF16))
    es0 = ExitStack()
    wkvf = es0.enter_context(nc.sbuf_tensor("wkvf", [128, NIC * 256], F32))
    wkvh = es0.enter_context(nc.sbuf_tensor("wkvh", [128, NIC * 256], BF16))
    wkvl = es0.enter_context(nc.sbuf_tensor("wkvl", [128, NIC * 256], BF16))
    xnb0 = es0.enter_context(nc.sbuf_tensor("xnb0", [128, 2 * D], F32))
    xtkh = es0.enter_context(nc.sbuf_tensor("xtkh", [128, NIC * 256], BF16))
    xtkl = es0.enter_context(nc.sbuf_tensor("xtkl", [128, NIC * 256], BF16))
    vtf = es0.enter_context(nc.sbuf_tensor("vtf", [128, 256], F32))
    ps0 = ExitStack()
    psK = ps0.enter_context(nc.psum_tensor("psK", [128, 512], F32))
    psV = ps0.enter_context(nc.psum_tensor("psV", [128, 512], F32))
    psX0 = ps0.enter_context(nc.psum_tensor("psX0", [128, 1024], F32))
    psVt = ps0.enter_context(nc.psum_tensor("psVt", [128, 512], F32))

    def f_sync0(e):
        s_cst.dma(e.dma_start(out=idn[:], in_=identf))
        s_cst.dma(e.dma_start(out=onesb_f[:], in_=onesf))
        s_cst.dma(e.dma_start(out=iota_s[:], in_=iota64))
        ev["p0_loads"] = (s_cst, s_cst.v)
        s_wkv.dma(e.dma_start(
            out=wkvf[:].rearrange("p (ic c) -> p ic c", c=256),
            in_=wkv.rearrange("(ic p) c -> p ic c", p=128)))
        ev["p0_wkv"] = (s_wkv, s_wkv.v)
    B.sync(f_sync0)

    def f_dve0(e):
        wde(e, "p0_loads")
        vE.inc(e.tensor_copy(out=onesb[:], in_=onesb_f[:]))
        wde(e, "p0_wkv")
        vE.inc(e.tensor_copy(out=wkvh[:], in_=wkvf[:]))
        ev["p0_wkvh"] = vE.v
        sw(e, vE)
        vE.inc(e.tensor_tensor(out=wkvl[:], in0=wkvf[:], in1=wkvh[:], op=ALU.subtract))
        ev["p0_wkvl"] = vE.v
    B.vector(f_dve0)

    for ks in range(8):
        def f_sync(e, ks=ks):
            for tt in range(2):
                if ks > 0:
                    _w(e, tE, ev[("kvx_trlast", (ks - 1, tt))])
                s_xnb[tt].dma(e.dma_start(out=xnb0[:, tt * D:(tt + 1) * D],
                                          in_=x_kv[(ks * 2 + tt) * 128:(ks * 2 + tt + 1) * 128, :]))
                ev[("kv_x", ks, tt)] = (s_xnb[tt], s_xnb[tt].v)
        B.sync(f_sync)

        for tt in range(2):
            def prew(e, ks=ks, tt=tt):
                wde(e, "p0_loads")
                wde(e, ("kv_x", ks, tt))
                if (ks, tt) != (0, 0):
                    pks, ptt = (ks, tt - 1) if tt > 0 else (ks - 1, 1)
                    _w(e, vE, ev[("kvx_lo", (pks, ptt))])
            tsp_rounds("kvx", (ks, tt), NIC,
                       (lambda i, ks=ks, tt=tt:
                        xnb0[:, tt * D + i * 128:tt * D + (i + 1) * 128]),
                       (lambda i, tt=tt: xtkh[:, i * 256 + tt * 128:i * 256 + (tt + 1) * 128]),
                       (lambda i, tt=tt: xtkl[:, i * 256 + tt * 128:i * 256 + (tt + 1) * 128]),
                       psX0, 4, 128, pre_wait=prew)

        def f_te(e, ks=ks):
            _w(e, vE, ev[("kvx_hi", (ks, 1))])
            _w(e, vE, ev[("kvx_lo", (ks, 1))])
            if ks == 0:
                _w(e, vE, ev["p0_wkvl"])
            else:
                _w(e, aE, ev[("kv_kh", ks - 1)])
                _w(e, vE, ev[("kv_kl", ks - 1)])
                _w(e, vE, ev[("kv_vtf", ks - 1)])
                _w(e, vE, ev[("kv_vl", ks - 1, 1)])
            for ic in range(NIC):
                wk_h = wkvh[:, ic * 256:ic * 256 + 128]
                wk_l = wkvl[:, ic * 256:ic * 256 + 128]
                xh = xtkh[:, ic * 256:(ic + 1) * 256]
                xl = xtkl[:, ic * 256:(ic + 1) * 256]
                e.matmul(psK[:, 0:256], wk_h, xh, start=(ic == 0), stop=False)
                e.matmul(psK[:, 0:256], wk_h, xl, start=False, stop=False)
                mmK = e.matmul(psK[:, 0:256], wk_l, xh, start=False, stop=(ic == NIC - 1))
            tE.inc(mmK)
            ev[("kv_kmm", ks)] = tE.v
            for ic in range(NIC):
                wv_h = wkvh[:, ic * 256 + 128:(ic + 1) * 256]
                wv_l = wkvl[:, ic * 256 + 128:(ic + 1) * 256]
                xh = xtkh[:, ic * 256:(ic + 1) * 256]
                xl = xtkl[:, ic * 256:(ic + 1) * 256]
                e.matmul(psV[:, 0:256], wv_h, xh, start=(ic == 0), stop=False)
                e.matmul(psV[:, 0:256], wv_h, xl, start=False, stop=False)
                mmV = e.matmul(psV[:, 0:256], wv_l, xh, start=False, stop=(ic == NIC - 1))
            tE.inc(mmV)
            ev[("kv_vmm", ks)] = tE.v
        B.tensor(f_te)

        def f_dve_v(e, ks=ks):
            _w(e, tE, ev[("kv_vmm", ks)])
            if ks > 0:
                _w(e, aE, ev[("kv_vh", ks - 1, 1)])
                _w(e, vE, ev[("kv_vl", ks - 1, 1)])
            vE.inc(e.tensor_copy(out=vtf[:], in_=psV[:, 0:256]))
            ev[("kv_vtf", ks)] = vE.v
        B.vector(f_dve_v)

        def f_te_v(e, ks=ks):
            _w(e, vE, ev[("kv_vtf", ks)])
            for sub in range(2):
                tE.inc(e.transpose(psVt[:, sub * 128:(sub + 1) * 128],
                                   vtf[:, sub * 128:(sub + 1) * 128], idn[:]))
                ev[("kv_vtr", ks, sub)] = tE.v
        B.tensor(f_te_v)

        def f_act(e, ks=ks):
            _w(e, tE, ev[("kv_kmm", ks)])
            aE.inc(e.activation(out=kth[:, ks * 256:(ks + 1) * 256], in_=psK[:, 0:256],
                                func=ACTF.Copy, bias=0.0, scale=1.0))
            ev[("kv_kh", ks)] = aE.v
            for sub in range(2):
                _w(e, tE, ev[("kv_vtr", ks, sub)])
                u = ks * 2 + sub
                aE.inc(e.activation(out=vth[:, u * 128:(u + 1) * 128],
                                    in_=psVt[:, sub * 128:(sub + 1) * 128],
                                    func=ACTF.Copy, bias=0.0, scale=1.0))
                ev[("kv_vh", ks, sub)] = aE.v
        B.scalar(f_act)

        def f_dve(e, ks=ks):
            _w(e, aE, ev[("kv_kh", ks)])
            vE.inc(e.tensor_tensor(out=ktl[:, ks * 256:(ks + 1) * 256], in0=psK[:, 0:256],
                                   in1=kth[:, ks * 256:(ks + 1) * 256], op=ALU.subtract))
            ev[("kv_kl", ks)] = vE.v
            for sub in range(2):
                _w(e, aE, ev[("kv_vh", ks, sub)])
                u = ks * 2 + sub
                vE.inc(e.tensor_tensor(out=vtl[:, u * 128:(u + 1) * 128],
                                       in0=psVt[:, sub * 128:(sub + 1) * 128],
                                       in1=vth[:, u * 128:(u + 1) * 128], op=ALU.subtract))
                ev[("kv_vl", ks, sub)] = vE.v
        B.vector(f_dve)
    barrier()
    ps0.close()
    es0.close()

    es1 = ExitStack()
    mask_s = es1.enter_context(nc.sbuf_tensor("mask_s", [128, 4 * 640], BF16))
    albf = es1.enter_context(nc.sbuf_tensor("albf", [128, 1024], F32))
    xth = es1.enter_context(nc.sbuf_tensor("xth", [128, NIC * HALF], BF16))
    xtl = es1.enter_context(nc.sbuf_tensor("xtl", [128, NIC * HALF], BF16))
    wqf = es1.enter_context(nc.sbuf_tensor("wqf", [128, NIC * 128], F32))
    wqh = es1.enter_context(nc.sbuf_tensor("wqh", [128, NIC * 128], BF16))
    wql = es1.enter_context(nc.sbuf_tensor("wql", [128, NIC * 128], BF16))
    qsh = es1.enter_context(nc.sbuf_tensor("qsh", [128, 2 * HALF], BF16))
    qsl = es1.enter_context(nc.sbuf_tensor("qsl", [128, 2 * HALF], BF16))
    logits = es1.enter_context(nc.sbuf_tensor("logits", [128, 3 * 2048], F32))
    pbuf = es1.enter_context(nc.sbuf_tensor("pbuf", [128, 2 * 2048], F32))
    pth = es1.enter_context(nc.sbuf_tensor("pth", [128, 2 * 2048], BF16))
    ptl = es1.enter_context(nc.sbuf_tensor("ptl", [128, 2 * 2048], BF16))
    obh = es1.enter_context(nc.sbuf_tensor("obh", [128, 512], BF16))
    obl = es1.enter_context(nc.sbuf_tensor("obl", [128, 512], BF16))
    negm = es1.enter_context(nc.sbuf_tensor("negm", [128, 2], F32))
    sums = es1.enter_context(nc.sbuf_tensor("sums", [128, 2], F32))
    rrec = es1.enter_context(nc.sbuf_tensor("rrec", [128, 2], F32))
    ps1 = ExitStack()
    psQ = ps1.enter_context(nc.psum_tensor("psQ", [128, 1024], F32))
    psS = ps1.enter_context(nc.psum_tensor("psS", [128, 1024], F32))
    psT = ps1.enter_context(nc.psum_tensor("psT", [128, 1024], F32))
    psO = ps1.enter_context(nc.psum_tensor("psO", [128, 1024], F32))

    def wq_dma(e, h):
        for dcg in range(4):
            src_ = wq[dcg * 1024:(dcg + 1) * 1024,
                      h * 128:(h + 1) * 128].rearrange("(ic p) o -> p ic o", p=128)
            dst = wqf[:, dcg * 1024:(dcg + 1) * 1024].rearrange("p (ic c) -> p ic c", c=128)
            s_wq.dma(e.dma_start(out=dst, in_=src_))
        ev[("wq", h)] = (s_wq, s_wq.v)

    def wq_split(e, h):
        wde(e, ("wq", h))
        if ("qmm", h - 1) in ev:
            _w(e, tE, ev[("qmm", h - 1)])
        vE.inc(e.tensor_copy(out=wqh[:], in_=wqf[:]))
        ev[("wqh", h)] = vE.v
        sw(e, vE)
        vE.inc(e.tensor_tensor(out=wql[:], in0=wqf[:], in1=wqh[:], op=ALU.subtract))
        ev[("wql", h)] = vE.v

    def q_proj(e, h, half):
        _w(e, vE, ev[("wql", h)])
        _w(e, vE, ev[("xb_a", half)])
        _w(e, vE, ev[("xb_v", half)])
        if ("qlo", h - 2) in ev:
            _w(e, vE, ev[("qlo", h - 2)])
        par = h % 2
        dst = psQ[:, par * 512:(par + 1) * 512]
        for ic in range(NIC):
            wh = wqh[:, ic * 128:(ic + 1) * 128]
            wl = wql[:, ic * 128:(ic + 1) * 128]
            xh = xth[:, ic * HALF:(ic + 1) * HALF]
            xl = xtl[:, ic * HALF:(ic + 1) * HALF]
            e.matmul(dst, wh, xh, start=(ic == 0), stop=False)
            e.matmul(dst, wh, xl, start=False, stop=False)
            mmq = e.matmul(dst, wl, xh, start=False, stop=(ic == NIC - 1))
        tE.inc(mmq)
        ev[("qmm", h)] = tE.v

    def q_evac(e, h):
        _w(e, tE, ev[("qmm", h)])
        if ("amm_last", h - 2) in ev:
            _w(e, tE, ev[("amm_last", h - 2)])
        par = h % 2
        vE.inc(e.tensor_scalar(out=qsh[:, par * 512:(par + 1) * 512],
                               in0=psQ[:, par * 512:(par + 1) * 512],
                               scalar1=SCALE, scalar2=None, op0=ALU.mult))
        ev[("qhi", h)] = vE.v
        sw(e, vE)
        vE.inc(e.scalar_tensor_tensor(out=qsl[:, par * 512:(par + 1) * 512],
                                      in0=psQ[:, par * 512:(par + 1) * 512], scalar=SCALE,
                                      in1=qsh[:, par * 512:(par + 1) * 512],
                                      op0=ALU.mult, op1=ALU.subtract))
        ev[("qlo", h)] = vE.v

    def p1_half(half):
        slopes = _slopes()
        for tt in range(4):
            def f_sync(e, tt=tt):
                if tt > 0:
                    _w(e, tE, ev[("xb_trlast", (half, tt - 1))])
                s_xnat.dma(e.dma_start(out=pbuf[:, 0:4096],
                                       in_=x_own[(half * 4 + tt) * 128:(half * 4 + tt + 1) * 128, :]))
                ev[("x_nat", half, tt)] = (s_xnat, s_xnat.v)
                if tt == 0:
                    for s in range(4):
                        s_mask.dma(e.dma_start(out=mask_s[:, s * 640:(s + 1) * 640],
                                               in_=masks[half * 4 + s]))
                    ev[("mask", half)] = (s_mask, s_mask.v)
            B.sync(f_sync)

            def prew(e, tt=tt):
                wde(e, ("x_nat", half, tt))
                if tt > 0:
                    _w(e, vE, ev[("xb_lo", (half, tt - 1))])
            tsp_rounds("xb", (half, tt), NIC,
                       (lambda i: pbuf[:, i * 128:(i + 1) * 128]),
                       (lambda i, tt=tt: xth[:, i * HALF + tt * 128:i * HALF + (tt + 1) * 128]),
                       (lambda i, tt=tt: xtl[:, i * HALF + tt * 128:i * HALF + (tt + 1) * 128]),
                       psT, 4, 128, pre_wait=prew)
        ev[("xb_a", half)] = ev[("xb_hi", (half, 3))]
        ev[("xb_v", half)] = ev[("xb_lo", (half, 3))]

        def f_gp_sp(e):
            _w(e, vE, ev[("xb_a", half)])
            _w(e, vE, ev[("xb_v", half)])
            for ic in range(NIC):
                qG.dma(e.dma_start(out=xt_scr[0][ic, :, half * HALF:(half + 1) * HALF],
                                   in_=xth[:, ic * HALF:(ic + 1) * HALF]))
                qG.dma(e.dma_start(out=xt_scr[1][ic, :, half * HALF:(half + 1) * HALF],
                                   in_=xtl[:, ic * HALF:(ic + 1) * HALF]))
            ev[("xt_spill", half)] = qG.v
        B.gpsimd(f_gp_sp)

        def f_sync_pro(e):
            wq_dma(e, 0)
            s_al[0].dma(e.dma_start(out=albf[:, 0:512], in_=alibi[0]))
            ev[("al", 0)] = (s_al[0], s_al[0].v)
        B.sync(f_sync_pro)
        B.vector(lambda e: wq_split(e, 0))

        def f_sync_pro2(e):
            _w(e, vE, ev[("wql", 0)])
            wq_dma(e, 1)
        B.sync(f_sync_pro2)
        B.tensor(lambda e: q_proj(e, 0, half))
        B.vector(lambda e: q_evac(e, 0))

        iters = [(h, s) for h in range(NH) for s in range(4)]
        NIT = len(iters)
        gb = [0]
        gtg = [0]

        def blocks_of(sg):
            L = LS[sg] * 128
            res = []
            off = 0
            while off < L:
                res.append((off, min(512, L - off)))
                off += 512
            return res

        pss_read = {}

        def score_block_secs(n):
            if not (0 <= n < NIT):
                return
            h, s = iters[n]
            sg = half * 4 + s
            L = LS[sg] * 128
            lpar = n % 3
            slope_h = float(slopes[h])
            for bi, (off, nb) in enumerate(blocks_of(sg)):
                gidx = gb[0]
                gb[0] += 1
                spar = gidx % 2

                def f_te(e, n=n, h=h, s=s, off=off, nb=nb, spar=spar, bi=bi):
                    if bi == 0 and s == 0:
                        _w(e, vE, ev[("qlo", h)])
                        wde(e, ("al", h))
                        if h == 0:
                            wde(e, ("mask", half))
                    if spar in pss_read:
                        _w(e, vE, pss_read[spar])
                    qpar = h % 2
                    qh_t = qsh[:, qpar * 512 + s * 128:qpar * 512 + (s + 1) * 128]
                    ql_t = qsl[:, qpar * 512 + s * 128:qpar * 512 + (s + 1) * 128]
                    dst = psS[:, spar * 512:spar * 512 + nb]
                    e.matmul(dst, qh_t, kth[:, off:off + nb], start=True, stop=False)
                    e.matmul(dst, qh_t, ktl[:, off:off + nb], start=False, stop=False)
                    mms_ = e.matmul(dst, ql_t, kth[:, off:off + nb], start=False, stop=True)
                    tE.inc(mms_)
                    ev[("smm", n, off)] = tE.v
                B.tensor(f_te)

                def f_dve(e, n=n, off=off, nb=nb, spar=spar, L=L, s=s,
                          lpar=lpar, slope_h=slope_h):
                    _w(e, tE, ev[("smm", n, off)])
                    r0 = L - 640
                    lo = max(off, r0)
                    hi = off + nb
                    if lo < hi:
                        vE.inc(e.tensor_tensor(
                            out=psS[:, spar * 512 + (lo - off):spar * 512 + (hi - off)],
                            in0=psS[:, spar * 512 + (lo - off):spar * 512 + (hi - off)],
                            in1=mask_s[:, s * 640 + (lo - r0):s * 640 + (hi - r0)],
                            op=ALU.add))
                        sw(e, vE)
                    c_hb = float(slope_h * off)
                    vE.inc(e.scalar_tensor_tensor(
                        out=logits[:, lpar * 2048 + off:lpar * 2048 + off + nb],
                        in0=psS[:, spar * 512:spar * 512 + nb], scalar=c_hb,
                        in1=albf[:, (h % 2) * 512:(h % 2) * 512 + nb],
                        op0=ALU.add, op1=ALU.add))
                    pss_read[spar] = vE.v
                    ev[("evac", n)] = vE.v
                B.vector(f_dve)
            if s == 3:
                ev[("amm_last", h)] = tE.v

        def pt_group_secs(n):
            m = n - 3
            if not (0 <= m < NIT):
                return
            h, s = iters[m]
            sg = half * 4 + s
            nu = LS[sg]
            lpar = m % 3
            for g in range((nu + 3) // 4):
                gidx = gtg[0]
                gtg[0] += 1
                ev[("pt_gpar", m, g)] = gidx
                u0 = g * 4
                u1 = min(u0 + 4, nu)

                def f_te(e, m=m, g=g, gidx=gidx, u0=u0, u1=u1, lpar=lpar):
                    if g == 0:
                        _w(e, vE, ev[("pn", m)])
                    if ("ptev_l", gidx - 2) in ev:
                        _w(e, vE, ev[("ptev_l", gidx - 2)])
                    tpar = gidx % 2
                    for u in range(u0, u1):
                        tE.inc(e.transpose(
                            psT[:, tpar * 512 + (u - u0) * 128:tpar * 512 + (u - u0 + 1) * 128],
                            logits[:, lpar * 2048 + u * 128:lpar * 2048 + (u + 1) * 128],
                            idn[:]))
                    ev[("pt_tr", m, g)] = tE.v

                def f_dve(e, m=m, g=g, gidx=gidx, u0=u0, u1=u1):
                    _w(e, tE, ev[("pt_tr", m, g)])
                    tpar = gidx % 2
                    ppar = m % 2
                    w_ = (u1 - u0) * 128
                    vE.inc(e.tensor_copy(
                        out=pth[:, ppar * 2048 + u0 * 128:ppar * 2048 + u0 * 128 + w_],
                        in_=psT[:, tpar * 512:tpar * 512 + w_]))
                    ev[("ptev_h_g", m, g)] = vE.v
                    sw(e, vE)
                    vE.inc(e.tensor_tensor(
                        out=ptl[:, ppar * 2048 + u0 * 128:ppar * 2048 + u0 * 128 + w_],
                        in0=psT[:, tpar * 512:tpar * 512 + w_],
                        in1=pth[:, ppar * 2048 + u0 * 128:ppar * 2048 + u0 * 128 + w_],
                        op=ALU.subtract))
                    ev[("ptev_l_g", m, g)] = vE.v
                    ev[("ptev_l", gidx)] = vE.v
                B.tensor(f_te)
                B.vector(f_dve)

        def te_sec(n):
            def f(e):
                m = n - 4
                if 0 <= m < NIT:
                    h, s = iters[m]
                    sg = half * 4 + s
                    nu = LS[sg]
                    opar = m % 2
                    if ("o_lo", m - 2) in ev:
                        _w(e, vE, ev[("o_lo", m - 2)])
                    ppar = m % 2
                    for u in range(nu):
                        g = u // 4
                        _w(e, vE, ev[("ptev_l_g", m, g)])
                        vh_t = vth[:, u * 128:(u + 1) * 128]
                        vl_t = vtl[:, u * 128:(u + 1) * 128]
                        ph_t = pth[:, ppar * 2048 + u * 128:ppar * 2048 + (u + 1) * 128]
                        pl_t = ptl[:, ppar * 2048 + u * 128:ppar * 2048 + (u + 1) * 128]
                        dst = psO[:, opar * 512:opar * 512 + 128]
                        e.matmul(dst, vh_t, ph_t, start=(u == 0), stop=False)
                        e.matmul(dst, vh_t, pl_t, start=False, stop=False)
                        mmav = e.matmul(dst, vl_t, ph_t, start=False, stop=(u == nu - 1))
                    tE.inc(mmav)
                    ev[("av", m)] = tE.v
                if n < NIT:
                    h, s = iters[n]
                    if s == 2 and h + 1 < NH:
                        q_proj(e, h + 1, half)
            return f

        def act_sec(n):
            def f(e):
                m = n - 1
                if 0 <= m < NIT:
                    h, s = iters[m]
                    sg = half * 4 + s
                    L = LS[sg] * 128
                    _w(e, vE, ev[("max", m)])
                    if ("pn", m - 2) in ev:
                        _w(e, vE, ev[("pn", m - 2)])
                    lpar = m % 3
                    ppar = m % 2
                    aE.inc(e.activation(out=pbuf[:, ppar * 2048:ppar * 2048 + L],
                                        in_=logits[:, lpar * 2048:lpar * 2048 + L],
                                        func=ACTF.Exp, bias=negm[:, m % 2:m % 2 + 1],
                                        scale=1.0, accum_out=sums[:, m % 2:m % 2 + 1]))
                    ev[("exp", m)] = aE.v
            return f

        def dve_sec(n):
            def f(e):
                if n < NIT:
                    h, s = iters[n]
                    if s == 1 and h + 1 < NH:
                        wq_split(e, h + 1)
                    if s == 3 and h + 1 < NH:
                        q_evac(e, h + 1)
                    sg = half * 4 + s
                    L = LS[sg] * 128
                    lpar = n % 3
                    if ("exp", n - 2) in ev:
                        _w(e, aE, ev[("exp", n - 2)])
                    sw(e, vE)
                    vE.inc(e.reduce_max(out=negm[:, n % 2:n % 2 + 1],
                                        in_=logits[:, lpar * 2048:lpar * 2048 + L],
                                        axis=AX, negate=True))
                    ev[("max", n)] = vE.v
                m = n - 1
                if 0 <= m < NIT:
                    _w(e, aE, ev[("exp", m)])
                    vE.inc(e.reciprocal(out=rrec[:, m % 2:m % 2 + 1],
                                        in_=sums[:, m % 2:m % 2 + 1]))
                    ev[("recip", m)] = vE.v
                m = n - 2
                if 0 <= m < NIT:
                    h, s = iters[m]
                    sg = half * 4 + s
                    L = LS[sg] * 128
                    sw(e, vE)
                    lpar = m % 3
                    ppar = m % 2
                    vE.inc(e.tensor_scalar(out=logits[:, lpar * 2048:lpar * 2048 + L],
                                           in0=pbuf[:, ppar * 2048:ppar * 2048 + L],
                                           scalar1=rrec[:, m % 2:m % 2 + 1], scalar2=None,
                                           op0=ALU.mult))
                    ev[("pn", m)] = vE.v
                m = n - 4
                if 0 <= m < NIT:
                    h, s = iters[m]
                    _w(e, tE, ev[("av", m)])
                    if ("ospill", h - 1) in ev:
                        wde(e, ("ospill", h - 1))
                    opar = m % 2
                    vE.inc(e.tensor_copy(out=obh[:, s * 128:(s + 1) * 128],
                                         in_=psO[:, opar * 512:opar * 512 + 128]))
                    ev[("o_hi", m)] = vE.v
                    sw(e, vE)
                    vE.inc(e.tensor_tensor(out=obl[:, s * 128:(s + 1) * 128],
                                           in0=psO[:, opar * 512:opar * 512 + 128],
                                           in1=obh[:, s * 128:(s + 1) * 128],
                                           op=ALU.subtract))
                    ev[("o_lo", m)] = vE.v
            return f

        def sync_sec(n):
            def f(e):
                if n < NIT:
                    h, s = iters[n]
                    if s == 2 and h + 2 < NH:
                        _w(e, vE, ev[("wql", h + 1)])
                        wq_dma(e, h + 2)
                    if s == 0 and h + 1 < NH:
                        if h >= 1:
                            _w(e, vE, ev[("max", (h - 1) * 4 + 3)])
                        p_ = (h + 1) % 2
                        s_al[p_].dma(e.dma_start(
                            out=albf[:, p_ * 512:(p_ + 1) * 512], in_=alibi[h + 1]))
                        ev[("al", h + 1)] = (s_al[p_], s_al[p_].v)
            return f

        def gp_sec(n):
            def f(e):
                m = n - 4
                if 0 <= m < NIT:
                    h, s = iters[m]
                    if s == 3:
                        _w(e, vE, ev[("o_lo", m)])
                        s_osp.dma(e.dma_start(out=o_scr[0][h, :, half * HALF:(half + 1) * HALF],
                                              in_=obh[:]))
                        s_osp.dma(e.dma_start(out=o_scr[1][h, :, half * HALF:(half + 1) * HALF],
                                              in_=obl[:]))
                        ev[("ospill", h)] = (s_osp, s_osp.v)
            return f

        for n in range(NIT + 5):
            pt_group_secs(n)
            score_block_secs(n)
            B.tensor(te_sec(n))
            B.scalar(act_sec(n))
            B.vector(dve_sec(n))
            B.sync(sync_sec(n))
            B.gpsimd(gp_sec(n))
        barrier()

    if MAXPH >= 1:
        for half in range(2):
            p1_half(half)
            ev = {k: v for k, v in ev.items() if not isinstance(k, tuple)}

    ps1.close()
    es1.close()
    es1kv.close()

    esA = ExitStack()
    acth = esA.enter_context(nc.sbuf_tensor("acth", [128, NIC * HALF], BF16))
    actl = esA.enter_context(nc.sbuf_tensor("actl", [128, NIC * HALF], BF16))
    q_b = esA.enter_context(nc.sbuf_tensor("q_b", [128, TPC], F32))
    mu_b = esA.enter_context(nc.sbuf_tensor("mu_b", [128, HALF], F32))
    r_b = esA.enter_context(nc.sbuf_tensor("r_b", [128, HALF], F32))
    mu1r = esA.enter_context(nc.sbuf_tensor("mu1r", [1, TPC], F32))
    rs1r = esA.enter_context(nc.sbuf_tensor("rs1r", [1, TPC], F32))
    mu2r = esA.enter_context(nc.sbuf_tensor("mu2r", [1, TPC], F32))
    rs2r = esA.enter_context(nc.sbuf_tensor("rs2r", [1, TPC], F32))
    srow = esA.enter_context(nc.sbuf_tensor("srow", [1, HALF], F32))
    srow2 = esA.enter_context(nc.sbuf_tensor("srow2", [1, HALF], F32))
    rowh = esA.enter_context(nc.sbuf_tensor("rowh", [1, HALF], BF16))
    rowl = esA.enter_context(nc.sbuf_tensor("rowl", [1, HALF], BF16))
    ones_row = esA.enter_context(nc.sbuf_tensor("ones_row", [1, HALF], BF16))

    def f_dve_or(e):
        vE.inc(e.memset(ones_row[:], 1.0))
        ev["q_ones_v"] = vE.v
    B.vector(f_dve_or)

    es2 = ExitStack()
    woF = es2.enter_context(nc.sbuf_tensor("woF", [128, NIC * 128], F32))
    woh = es2.enter_context(nc.sbuf_tensor("woh", [128, NIC * 128], BF16))
    wol = es2.enter_context(nc.sbuf_tensor("wol", [128, NIC * 128], BF16))
    xoch = es2.enter_context(nc.sbuf_tensor("xoch", [128, 2 * HALF], BF16))
    xocl = es2.enter_context(nc.sbuf_tensor("xocl", [128, 2 * HALF], BF16))
    z1f = es2.enter_context(nc.sbuf_tensor("z1f", [128, 2 * HALF], F32))
    sqf = es2.enter_context(nc.sbuf_tensor("sqf", [128, 2 * HALF], F32))
    z1hb = es2.enter_context(nc.sbuf_tensor("z1hb", [128, 2 * HALF], BF16))
    z1lb = es2.enter_context(nc.sbuf_tensor("z1lb", [128, 2 * HALF], BF16))
    sqhb = es2.enter_context(nc.sbuf_tensor("sqhb", [128, 2 * HALF], BF16))
    sqlb = es2.enter_context(nc.sbuf_tensor("sqlb", [128, 2 * HALF], BF16))
    ps2 = ExitStack()
    psY = ps2.enter_context(nc.psum_tensor("psY", [128, 1024], F32))
    psZs = ps2.enter_context(nc.psum_tensor("psZs", [1, 512], F32))
    psQs = ps2.enter_context(nc.psum_tensor("psQs", [1, 512], F32))

    def wo_dma(e, oc):
        for dcg in range(4):
            src_ = wo[dcg * 1024:(dcg + 1) * 1024,
                      oc * 128:(oc + 1) * 128].rearrange("(ic p) o -> p ic o", p=128)
            dst = woF[:, dcg * 1024:(dcg + 1) * 1024].rearrange("p (ic c) -> p ic c", c=128)
            s_wo.dma(e.dma_start(out=dst, in_=src_))
        ev[("woq", oc)] = (s_wo, s_wo.v)

    def p2_half(half):
        def f_gp0(e):
            for dc in range(NIC):
                s_old.dma(e.dma_start(out=acth[:, dc * HALF:(dc + 1) * HALF],
                                      in_=o_scr[0][dc, :, half * HALF:(half + 1) * HALF]))
                s_old.dma(e.dma_start(out=actl[:, dc * HALF:(dc + 1) * HALF],
                                      in_=o_scr[1][dc, :, half * HALF:(half + 1) * HALF]))
            ev[("oload", half)] = (s_old, s_old.v)
            for oc in range(2):
                s_xoc[oc % 2].dma(e.dma_start(out=xoch[:, (oc % 2) * HALF:(oc % 2 + 1) * HALF],
                                              in_=xt_scr[0][oc, :, half * HALF:(half + 1) * HALF]))
                s_xoc[oc % 2].dma(e.dma_start(out=xocl[:, (oc % 2) * HALF:(oc % 2 + 1) * HALF],
                                              in_=xt_scr[1][oc, :, half * HALF:(half + 1) * HALF]))
                ev[("xoc", half, oc)] = (s_xoc[oc % 2], s_xoc[oc % 2].v)
        B.gpsimd(f_gp0)

        def f_sync0(e):
            wo_dma(e, 0)
        B.sync(f_sync0)

        def f_dve0(e):
            wde(e, ("woq", 0))
            vE.inc(e.tensor_copy(out=woh[:], in_=woF[:]))
            ev[("wosh", 0)] = vE.v
            sw(e, vE)
            vE.inc(e.tensor_tensor(out=wol[:], in0=woF[:], in1=woh[:], op=ALU.subtract))
            ev[("wosl", 0)] = vE.v
        B.vector(f_dve0)

        def f_sync1(e):
            _w(e, vE, ev[("wosl", 0)])
            wo_dma(e, 1)
        B.sync(f_sync1)

        for oc in range(NIC + 2):
            def f_te(e, oc=oc):
                if oc < NIC:
                    _w(e, vE, ev[("wosl", oc)])
                    if oc == 0:
                        wde(e, ("oload", half))
                    if ("z1f", half, oc - 2) in ev:
                        _w(e, vE, ev[("z1f", half, oc - 2)])
                    par = oc % 2
                    dst = psY[:, par * 512:(par + 1) * 512]
                    for dc in range(NIC):
                        wh = woh[:, dc * 128:(dc + 1) * 128]
                        wl = wol[:, dc * 128:(dc + 1) * 128]
                        ohd = acth[:, dc * HALF:(dc + 1) * HALF]
                        old = actl[:, dc * HALF:(dc + 1) * HALF]
                        e.matmul(dst, wh, ohd, start=(dc == 0), stop=False)
                        e.matmul(dst, wh, old, start=False, stop=False)
                        mmy = e.matmul(dst, wl, ohd, start=False, stop=(dc == NIC - 1))
                    tE.inc(mmy)
                    ev[("ymm", half, oc)] = tE.v
                m = oc - 2
                if 0 <= m < NIC:
                    par = m % 2
                    _w(e, vE, ev[("z1sq_l", half, m)])
                    st = m == 0
                    e.matmul(psZs[:], onesb[:, 0:1], z1hb[:, par * HALF:(par + 1) * HALF],
                             start=st, stop=False)
                    e.matmul(psZs[:], onesb[:, 0:1], z1lb[:, par * HALF:(par + 1) * HALF],
                             start=False, stop=(m == NIC - 1))
                    e.matmul(psQs[:], onesb[:, 0:1], sqhb[:, par * HALF:(par + 1) * HALF],
                             start=st, stop=False)
                    mmst = e.matmul(psQs[:], onesb[:, 0:1],
                                    sqlb[:, par * HALF:(par + 1) * HALF],
                                    start=False, stop=(m == NIC - 1))
                    tE.inc(mmst)
                    ev[("stats", half, m)] = tE.v

            def f_dve(e, oc=oc):
                m = oc + 1
                if m < NIC:
                    wde(e, ("woq", m))
                    if ("ymm", half, m - 1) in ev:
                        _w(e, tE, ev[("ymm", half, m - 1)])
                    vE.inc(e.tensor_copy(out=woh[:], in_=woF[:]))
                    sw(e, vE)
                    vE.inc(e.tensor_tensor(out=wol[:], in0=woF[:], in1=woh[:],
                                           op=ALU.subtract))
                    ev[("wosh", m)] = vE.v
                    ev[("wosl", m)] = vE.v
                if oc < NIC:
                    _w(e, tE, ev[("ymm", half, oc)])
                    wde(e, ("xoc", half, oc))
                    if ("z1sq_h", half, oc - 2) in ev:
                        _w(e, vE, ev[("z1sq_h", half, oc - 2)])
                    sw(e, vE)
                    par = oc % 2
                    z1s = z1f[:, par * HALF:(par + 1) * HALF]
                    vE.inc(e.tensor_tensor(out=z1s, in0=psY[:, par * 512:(par + 1) * 512],
                                           in1=xoch[:, par * HALF:(par + 1) * HALF],
                                           op=ALU.add))
                    sw(e, vE)
                    vE.inc(e.tensor_tensor(out=z1s, in0=z1s,
                                           in1=xocl[:, par * HALF:(par + 1) * HALF],
                                           op=ALU.add))
                    ev[("z1f", half, oc)] = vE.v
                m = oc - 1
                if 0 <= m < NIC:
                    par = m % 2
                    if ("stats", half, m - 2) in ev:
                        _w(e, tE, ev[("stats", half, m - 2)])
                    if ("z1sp", half, m - 2) in ev:
                        wde(e, ("z1sp", half, m - 2))
                    sw(e, vE)
                    z1v = z1f[:, par * HALF:(par + 1) * HALF]
                    vE.inc(e.tensor_copy(out=z1hb[:, par * HALF:(par + 1) * HALF], in_=z1v))
                    vE.inc(e.tensor_tensor(out=sqf[:, par * HALF:(par + 1) * HALF],
                                           in0=z1v, in1=z1v, op=ALU.mult))
                    ev[("z1h", half, m)] = vE.v
                    sw(e, vE)
                    vE.inc(e.tensor_tensor(out=z1lb[:, par * HALF:(par + 1) * HALF],
                                           in0=z1v,
                                           in1=z1hb[:, par * HALF:(par + 1) * HALF],
                                           op=ALU.subtract))
                    vE.inc(e.tensor_copy(out=sqhb[:, par * HALF:(par + 1) * HALF],
                                         in_=sqf[:, par * HALF:(par + 1) * HALF]))
                    ev[("z1l", half, m)] = vE.v
                    sw(e, vE)
                    vE.inc(e.tensor_tensor(out=sqlb[:, par * HALF:(par + 1) * HALF],
                                           in0=sqf[:, par * HALF:(par + 1) * HALF],
                                           in1=sqhb[:, par * HALF:(par + 1) * HALF],
                                           op=ALU.subtract))
                    ev[("z1sq_h", half, m)] = vE.v
                    ev[("z1sq_l", half, m)] = vE.v

            def f_sync(e, oc=oc):
                m = oc + 2
                if m < NIC:
                    _w(e, vE, ev[("wosl", m - 1)])
                    wo_dma(e, m)

            def f_gp(e, oc=oc):
                m = oc + 2
                if m < NIC:
                    if ("z1f", half, m - 2) in ev:
                        _w(e, vE, ev[("z1f", half, m - 2)])
                    s_xoc[m % 2].dma(e.dma_start(out=xoch[:, (m % 2) * HALF:(m % 2 + 1) * HALF],
                                                 in_=xt_scr[0][m, :, half * HALF:(half + 1) * HALF]))
                    s_xoc[m % 2].dma(e.dma_start(out=xocl[:, (m % 2) * HALF:(m % 2 + 1) * HALF],
                                                 in_=xt_scr[1][m, :, half * HALF:(half + 1) * HALF]))
                    ev[("xoc", half, m)] = (s_xoc[m % 2], s_xoc[m % 2].v)
                m = oc - 1
                if 0 <= m < NIC:
                    _w(e, vE, ev[("z1sq_l", half, m)])
                    par = m % 2
                    s_z1sp[par].dma(e.dma_start(out=z1_scr[0][m, :, half * HALF:(half + 1) * HALF],
                                                in_=z1hb[:, par * HALF:(par + 1) * HALF]))
                    s_z1sp[par].dma(e.dma_start(out=z1_scr[1][m, :, half * HALF:(half + 1) * HALF],
                                                in_=z1lb[:, par * HALF:(par + 1) * HALF]))
                    ev[("z1sp", half, m)] = (s_z1sp[par], s_z1sp[par].v)
            B.tensor(f_te)
            B.vector(f_dve)
            B.sync(f_sync)
            B.gpsimd(f_gp)

        def f_dve_fin(e):
            _w(e, tE, ev[("stats", half, NIC - 1)])
            vE.inc(e.tensor_scalar(out=mu1r[0:1, half * HALF:(half + 1) * HALF], in0=psZs[:],
                                   scalar1=1.0 / D, scalar2=None, op0=ALU.mult))
            vE.inc(e.tensor_scalar(out=srow[:], in0=psQs[:], scalar1=1.0 / D, scalar2=None,
                                   op0=ALU.mult))
            mu = mu1r[0:1, half * HALF:(half + 1) * HALF]
            sw(e, vE)
            vE.inc(e.tensor_tensor(out=srow2[:], in0=mu, in1=mu, op=ALU.mult))
            sw(e, vE)
            vE.inc(e.tensor_tensor(out=srow[:], in0=srow[:], in1=srow2[:], op=ALU.subtract))
            sw(e, vE)
            vE.inc(e.tensor_scalar(out=srow[:], in0=srow[:], scalar1=1e-5, scalar2=None,
                                   op0=ALU.add))
            ev[("ln1_var", half)] = vE.v
        B.vector(f_dve_fin)

        def f_act_fin(e):
            _w(e, vE, ev[("ln1_var", half)])
            aE.inc(e.activation(out=srow[:], in_=srow[:], func=ACTF.Sqrt, bias=0.0, scale=1.0))
            ev[("ln1_sqrt", half)] = aE.v
        B.scalar(f_act_fin)

        def f_dve_fin2(e):
            _w(e, aE, ev[("ln1_sqrt", half)])
            vE.inc(e.reciprocal(out=rs1r[0:1, half * HALF:(half + 1) * HALF], in_=srow[:]))
        B.vector(f_dve_fin2)
        barrier()

    if MAXPH >= 2:
        for half in range(2):
            p2_half(half)
            ev = {k: v for k, v in ev.items() if not isinstance(k, tuple)}

    ps2.close()
    es2.close()

    es3 = ExitStack()
    zrh = es3.enter_context(nc.sbuf_tensor("zrh", [128, 2 * HALF], BF16))
    zrl = es3.enter_context(nc.sbuf_tensor("zrl", [128, 2 * HALF], BF16))
    x1tmp = es3.enter_context(nc.sbuf_tensor("x1tmp", [128, 2 * HALF], F32))
    es3b = ExitStack()
    dotk = es3b.enter_context(nc.sbuf_tensor("dotk", [128, 256], F32))
    psD_sb = es3b.enter_context(nc.sbuf_tensor("psD_sb", [64, 512], F32))
    ebuf = es3b.enter_context(nc.sbuf_tensor("ebuf", [128, 64], F32))
    ebufb = es3b.enter_context(nc.sbuf_tensor("ebufb", [128, 64], BF16))
    ebuf_oh = es3b.enter_context(nc.sbuf_tensor("ebuf_oh", [64, 128], F32))
    ohb = es3b.enter_context(nc.sbuf_tensor("ohb", [64, 128], BF16))
    dnegm = es3b.enter_context(nc.sbuf_tensor("dnegm", [128, 1], F32))
    dsum = es3b.enter_context(nc.sbuf_tensor("dsum", [128, 1], F32))
    dthr = es3b.enter_context(nc.sbuf_tensor("dthr", [128, 1], F32))
    q01 = es3b.enter_context(nc.sbuf_tensor("q01", [128, 1], F32))
    centf = es3b.enter_context(nc.sbuf_tensor("centf", [64, D], F32))
    cth = es3b.enter_context(nc.sbuf_tensor("cth", [128, NIC * 64], BF16))
    ctl = es3b.enter_context(nc.sbuf_tensor("ctl", [128, NIC * 64], BF16))
    cnt_s = es3b.enter_context(nc.sbuf_tensor("cnt_s", [64, 1], F32))
    glob_s = es3b.enter_context(nc.sbuf_tensor("glob_s", [64, 1], F32))
    arowb = es3b.enter_context(nc.sbuf_tensor("arowb", [1, 128], BF16))
    ar_i = es3b.enter_context(nc.sbuf_tensor("ar_i", [1, D], I32))
    ar_f = es3b.enter_context(nc.sbuf_tensor("ar_f", [1, D], F32))
    qrow = es3b.enter_context(nc.sbuf_tensor("qrow", [1, TPC], F32))
    qrowb = es3b.enter_context(nc.sbuf_tensor("qrowb", [1, TPC], BF16))
    ps3 = ExitStack()
    psD = ps3.enter_context(nc.psum_tensor("psD", [64, 512], F32))
    psT3 = ps3.enter_context(nc.psum_tensor("psT3", [128, 1024], F32))
    psC = ps3.enter_context(nc.psum_tensor("psC", [64, 512], F32))
    psB3 = ps3.enter_context(nc.psum_tensor("psB3", [128, 512], F32))
    psR3 = ps3.enter_context(nc.psum_tensor("psR3", [128, 512], F32))

    def f_sync_c(e):
        if MAXPH < 3:
            return
        s_cst.dma(e.dma_start(out=centf[:], in_=cent))
        s_cst.dma(e.dma_start(out=ar_i[:], in_=assign_row))
        ev["p3_loads"] = (s_cst, s_cst.v)
    B.sync(f_sync_c)

    def f_dve_ar(e):
        if MAXPH < 3:
            return
        wde(e, "p3_loads")
        vE.inc(e.tensor_copy(out=ar_f[:], in_=ar_i[:]))
    B.vector(f_dve_ar)

    if MAXPH >= 3:
        tsp_rounds("ct", "c", NIC,
               (lambda i: centf[:, i * 128:(i + 1) * 128]),
               (lambda i: cth[:, i * 64:(i + 1) * 64]),
               (lambda i: ctl[:, i * 64:(i + 1) * 64]),
               psT3, 4, 64,
                   pre_wait=(lambda e: wde(e, "p3_loads")),
                   scale=10.0, ident_ap=idn[0:64, 0:64])

    def bcast_rows(tag, srcrow, dst, half, psb):
        def f_dve(e):
            _w(e, tE, tE.v)
            _w(e, aE, aE.v)
            sw(e, vE)
            vE.inc(e.tensor_copy(out=rowh[:],
                                 in_=srcrow[0:1, half * HALF:(half + 1) * HALF]))
            ev[(tag + "_rh", half)] = vE.v
            sw(e, vE)
            vE.inc(e.scalar_tensor_tensor(out=rowl[:],
                                          in0=srcrow[0:1, half * HALF:(half + 1) * HALF],
                                          scalar=1.0, in1=rowh[:], op0=ALU.mult,
                                          op1=ALU.subtract))
            ev[(tag + "_rl", half)] = vE.v
        def f_te(e):
            _w(e, vE, ev[(tag + "_rl", half)])
            e.matmul(psb[:], onesb[0:1, :], rowh[:], start=True, stop=False)
            mmb = e.matmul(psb[:], onesb[0:1, :], rowl[:], start=False, stop=True)
            tE.inc(mmb)
            ev[(tag + "_mm", half)] = tE.v
        def f_dve2(e):
            _w(e, tE, ev[(tag + "_mm", half)])
            vE.inc(e.tensor_copy(out=dst[:], in_=psb[:]))
            ev[(tag + "_bc", half)] = vE.v
        B.vector(f_dve)
        B.tensor(f_te)
        B.vector(f_dve2)

    def x1_build(half, with_dots):
        for ic in range(NIC + 2):
            def f_gp(e, ic=ic):
                if ic < NIC:
                    if ("x1f", half, ic - 2) in ev:
                        _w(e, vE, ev[("x1f", half, ic - 2)])
                    par = ic % 2
                    s_zr[par].dma(e.dma_start(out=zrh[:, par * HALF:(par + 1) * HALF],
                                              in_=z1_scr[0][ic, :, half * HALF:(half + 1) * HALF]))
                    s_zr[par].dma(e.dma_start(out=zrl[:, par * HALF:(par + 1) * HALF],
                                              in_=z1_scr[1][ic, :, half * HALF:(half + 1) * HALF]))
                    ev[("zr", half, ic)] = (s_zr[par], s_zr[par].v)
            def f_dve(e, ic=ic):
                m = ic - 1
                if 0 <= m < NIC:
                    if m == 0:
                        _w(e, vE, ev[("ln1mu_bc", half)])
                        _w(e, vE, ev[("ln1rs_bc", half)])
                    par = m % 2
                    wde(e, ("zr", half, m))
                    if ("x1h", half, m - 2) in ev:
                        _w(e, vE, ev[("x1h", half, m - 2)])
                    sw(e, vE)
                    x1s = x1tmp[:, par * HALF:(par + 1) * HALF]
                    vE.inc(e.tensor_tensor(out=x1s, in0=zrh[:, par * HALF:(par + 1) * HALF],
                                           in1=zrl[:, par * HALF:(par + 1) * HALF], op=ALU.add))
                    sw(e, vE)
                    vE.inc(e.tensor_tensor(out=x1s, in0=x1s, in1=mu_b[:], op=ALU.subtract))
                    sw(e, vE)
                    vE.inc(e.tensor_tensor(out=x1s, in0=x1s, in1=r_b[:], op=ALU.mult))
                    ev[("x1f", half, m)] = vE.v
            def f_dve2(e, ic=ic):
                m = ic - 1
                if 0 <= m < NIC:
                    par = m % 2
                    sw(e, vE)
                    vE.inc(e.tensor_copy(out=acth[:, m * HALF:(m + 1) * HALF],
                                         in_=x1tmp[:, par * HALF:(par + 1) * HALF]))
                    ev[("x1h", half, m)] = vE.v
                    sw(e, vE)
                    vE.inc(e.tensor_tensor(out=actl[:, m * HALF:(m + 1) * HALF],
                                           in0=x1tmp[:, par * HALF:(par + 1) * HALF],
                                           in1=acth[:, m * HALF:(m + 1) * HALF],
                                           op=ALU.subtract))
                    ev[("x1l", half, m)] = vE.v
            def f_te(e, ic=ic):
                if not with_dots:
                    return
                m = ic - 1
                if 0 <= m < NIC:
                    if m == 0 and ("ct_hi", "c") in ev:
                        _w(e, vE, ev[("ct_lo", "c")])
                    _w(e, vE, ev[("x1l", half, m)])
                    ch = cth[:, m * 64:(m + 1) * 64]
                    cl = ctl[:, m * 64:(m + 1) * 64]
                    xh = acth[:, m * HALF:(m + 1) * HALF]
                    xl = actl[:, m * HALF:(m + 1) * HALF]
                    e.matmul(psD[:], ch, xh, start=(m == 0), stop=False)
                    e.matmul(psD[:], ch, xl, start=False, stop=False)
                    mmd = e.matmul(psD[:], cl, xh, start=False, stop=(m == NIC - 1))
                    tE.inc(mmd)
                    ev[("dmm", half, m)] = tE.v
            B.gpsimd(f_gp)
            B.vector(f_dve)
            B.vector(f_dve2)
            B.tensor(f_te)

    def p3_half(half):
        bcast_rows("ln1mu", mu1r, mu_b, half, psB3)
        bcast_rows("ln1rs", rs1r, r_b, half, psB3)
        x1_build(half, with_dots=True)

        def f_dve_cp(e):
            _w(e, tE, ev[("dmm", half, NIC - 1)])
            vE.inc(e.tensor_copy(out=psD_sb[:], in_=psD[:]))
            ev[("dot_sb", half)] = vE.v
        B.vector(f_dve_cp)

        for tt in range(4):
            def f_te(e, tt=tt):
                _w(e, vE, ev[("dot_sb", half)])
                if tt > 0:
                    _w(e, vE, ev[("dot_ev", half, tt - 1)])
                    _w(e, vE, ev[("q_ev", half, tt - 1)])
                tE.inc(e.transpose(psT3[:, 0:64], psD_sb[:, tt * 128:(tt + 1) * 128],
                                   idn[0:64, 0:64]))
                ev[("dot_tr", half, tt)] = tE.v
            B.tensor(f_te)

            def f_dve(e, tt=tt):
                _w(e, tE, ev[("dot_tr", half, tt)])
                vE.inc(e.tensor_copy(out=dotk[:, tt * 64:(tt + 1) * 64], in_=psT3[:, 0:64]))
                ev[("dot_ev", half, tt)] = vE.v
                sw(e, vE)
                if ("dot_exp", half, tt - 1) in ev:
                    _w(e, aE, ev[("dot_exp", half, tt - 1)])
                vE.inc(e.reduce_max(out=dnegm[:], in_=dotk[:, tt * 64:(tt + 1) * 64],
                                    axis=AX, negate=True))
                ev[("dot_max", half, tt)] = vE.v
            B.vector(f_dve)

            def f_act(e, tt=tt):
                _w(e, vE, ev[("dot_max", half, tt)])
                aE.inc(e.activation(out=ebuf[:], in_=dotk[:, tt * 64:(tt + 1) * 64],
                                    func=ACTF.Exp, bias=dnegm[:], scale=1.0, accum_out=dsum[:]))
                ev[("dot_exp", half, tt)] = aE.v
            B.scalar(f_act)

            def f_dve2(e, tt=tt):
                _w(e, aE, ev[("dot_exp", half, tt)])
                vE.inc(e.tensor_scalar(out=dthr[:], in0=dsum[:], scalar1=0.01, scalar2=None,
                                       op0=ALU.mult))
                sw(e, vE)
                vE.inc(e.tensor_scalar(out=ebuf[:], in0=ebuf[:], scalar1=dthr[:], scalar2=None,
                                       op0=ALU.is_gt))
                sw(e, vE)
                vE.inc(e.reduce_max(out=q01[:], in_=ebuf[:], axis=AX))
                ev[("dot_hits", half, tt)] = vE.v
            B.vector(f_dve2)

            def f_dve2b(e, tt=tt):
                sw(e, vE)
                vE.inc(e.tensor_copy(out=ebufb[:], in_=ebuf[:]))
                ev[("dot_hb", half, tt)] = vE.v
            B.vector(f_dve2b)

            def f_te2(e, tt=tt):
                _w(e, vE, ev[("dot_hb", half, tt)])
                st = (half == 0 and tt == 0)
                sp = (half == 1 and tt == 3)
                mmc = e.matmul(psC[:, 0:1], ebufb[:], onesb[:, 0:1], start=st, stop=sp)
                tE.inc(mmc)
                ev[("cnt", half, tt)] = tE.v
                _w(e, vE, ev[("dot_hits", half, tt)])
                _w(e, vE, ev[("dot_ev", half, tt)])
                tE.inc(e.transpose(psT3[0:1, 384:512], q01[:], idn[:]))
                ev[("q_tr", half, tt)] = tE.v
            B.tensor(f_te2)

            def f_dve3(e, tt=tt):
                _w(e, tE, ev[("q_tr", half, tt)])
                vE.inc(e.tensor_copy(
                    out=qrow[0:1, half * HALF + tt * 128:half * HALF + (tt + 1) * 128],
                    in_=psT3[0:1, 384:512]))
                ev[("q_ev", half, tt)] = vE.v
            B.vector(f_dve3)

    if MAXPH >= 3:
        p3_half(0)
        barrier()
        p3_half(1)
    barrier()

    def f_dve_cnt(e):
        if MAXPH < 3:
            return
        vE.inc(e.tensor_copy(out=cnt_s[:], in_=psC[:, 0:1]))
        ev["cnt_sb"] = vE.v
    B.vector(f_dve_cnt)

    def f_gp_cc(e):
        if MAXPH < 3:
            return
        _w(e, vE, ev["cnt_sb"])
        s_cc.dma(e.dma_start(out=cc_in.ap(), in_=cnt_s[:]))
        e.wait_ge(s_cc.h, s_cc.v)
        cS.inc(e.collective_compute(
            "AllReduce", ALU.add, replica_groups=[[0, 1, 2, 3, 4, 5, 6, 7]],
            ins=[cc_in.ap().opt()], outs=[cc_out.ap().opt()]))
        e.wait_ge(cS.h, cS.v)
        s_cc.dma(e.dma_start(out=glob_s[:], in_=cc_out.ap()))
        ev["cc_done"] = (s_cc, s_cc.v)
    B.gpsimd(f_gp_cc)

    def f_dve_cs(e):
        if MAXPH < 3:
            return
        wde(e, "cc_done")
        vE.inc(e.tensor_scalar(out=glob_s[:], in0=glob_s[:], scalar1=0.5, scalar2=None,
                               op0=ALU.is_ge))
        ev["cs01"] = vE.v
    B.vector(f_dve_cs)

    def f_dve_csb(e):
        if MAXPH < 3:
            return
        sw(e, vE)
        vE.inc(e.tensor_copy(out=csb[:], in_=glob_s[:]))
        ev["csb"] = vE.v
    B.vector(f_dve_csb)

    for oc in range(NIC if MAXPH >= 3 else 0):
        def f_dve0r(e, oc=oc):
            if oc == 0:
                _w(e, vE, ev["csb"])
            if ("abc", oc - 1) in ev:
                _w(e, tE, ev[("abc", oc - 1)])
            sw(e, vE)
            vE.inc(e.tensor_copy(out=arowb[:], in_=ar_f[0:1, oc * 128:(oc + 1) * 128]))
            ev[("arowb", oc)] = vE.v
        def f_te(e, oc=oc):
            _w(e, vE, ev[("arowb", oc)])
            if ("abc01", oc - 1) in ev:
                _w(e, vE, ev[("abc01", oc - 1)])
            mma = e.matmul(psB3[0:64, 0:128], onesb[0:1, 0:64], arowb[:],
                           start=True, stop=True)
            tE.inc(mma)
            ev[("abc", oc)] = tE.v
        def f_dve(e, oc=oc):
            _w(e, tE, ev[("abc", oc)])
            vE.inc(e.tensor_scalar(out=ebuf_oh[:], in0=psB3[0:64, 0:128], scalar1=iota_s[:],
                                   scalar2=None, op0=ALU.is_equal))
            ev[("abc01", oc)] = vE.v
        def f_dve1r(e, oc=oc):
            _w(e, vE, ev[("abc01", oc)])
            if ("rsmm", oc - 1) in ev:
                _w(e, tE, ev[("rsmm", oc - 1)])
            sw(e, vE)
            vE.inc(e.tensor_copy(out=ohb[:], in_=ebuf_oh[:]))
            ev[("ohb", oc)] = vE.v
        def f_te2(e, oc=oc):
            _w(e, vE, ev[("ohb", oc)])
            _w(e, vE, ev["csb"])
            if ("rs", oc - 1) in ev:
                _w(e, vE, ev[("rs", oc - 1)])
            mmr = e.matmul(psR3[:, 0:1], ohb[:], csb[:], start=True, stop=True)
            tE.inc(mmr)
            ev[("rsmm", oc)] = tE.v
        def f_dve2(e, oc=oc):
            _w(e, tE, ev[("rsmm", oc)])
            vE.inc(e.tensor_copy(out=rs_s[:, oc:oc + 1], in_=psR3[:, 0:1]))
            ev[("rs", oc)] = vE.v
        B.vector(f_dve0r)
        B.tensor(f_te)
        B.vector(f_dve)
        B.vector(f_dve1r)
        B.tensor(f_te2)
        B.vector(f_dve2)

    for half in range(2 if MAXPH >= 3 else 0):
        def f_dveq(e, half=half):
            _w(e, vE, ev[("q_ev", half, 3)] if ("q_ev", half, 3) in ev else 0)
            sw(e, vE)
            vE.inc(e.tensor_copy(out=qrowb[0:1, half * HALF:(half + 1) * HALF],
                                 in_=qrow[0:1, half * HALF:(half + 1) * HALF]))
            ev[("qrb", half)] = vE.v
        def f_te(e, half=half):
            _w(e, vE, ev[("qrb", half)])
            mmq3 = e.matmul(psB3[:], onesb[0:1, :],
                            qrowb[0:1, half * HALF:(half + 1) * HALF],
                            start=True, stop=True)
            tE.inc(mmq3)
            ev[("qmm3", half)] = tE.v
        def f_dve(e, half=half):
            _w(e, tE, ev[("qmm3", half)])
            vE.inc(e.tensor_copy(out=q_b[:, half * HALF:(half + 1) * HALF], in_=psB3[:]))
            ev[("qbc", half)] = vE.v
        B.vector(f_dveq)
        B.tensor(f_te)
        B.vector(f_dve)
    barrier()
    ps3.close()
    es3b.close()

    es4 = ExitStack()
    wfF = es4.enter_context(nc.sbuf_tensor("wfF", [128, D], F32))
    biasf4 = es4.enter_context(nc.sbuf_tensor("biasf4", [1, D], F32))
    biash = es4.enter_context(nc.sbuf_tensor("biash", [1, D], BF16))
    biasl = es4.enter_context(nc.sbuf_tensor("biasl", [1, D], BF16))
    wth = es4.enter_context(nc.sbuf_tensor("wth", [128, NIC * 128], BF16))
    wtl = es4.enter_context(nc.sbuf_tensor("wtl", [128, NIC * 128], BF16))
    gsf = es4.enter_context(nc.sbuf_tensor("gsf", [128, 2 * HALF], F32))
    z2ff = es4.enter_context(nc.sbuf_tensor("z2ff", [128, 2 * HALF], F32))
    z2hb = es4.enter_context(nc.sbuf_tensor("z2hb", [128, 2 * HALF], BF16))
    z2lb = es4.enter_context(nc.sbuf_tensor("z2lb", [128, 2 * HALF], BF16))
    sq2f = es4.enter_context(nc.sbuf_tensor("sq2f", [128, 2 * HALF], F32))
    sq2hb = es4.enter_context(nc.sbuf_tensor("sq2hb", [128, 2 * HALF], BF16))
    sq2lb = es4.enter_context(nc.sbuf_tensor("sq2lb", [128, 2 * HALF], BF16))
    ps4 = ExitStack()
    psF = ps4.enter_context(nc.psum_tensor("psF", [128, 1024], F32))
    psW = ps4.enter_context(nc.psum_tensor("psW", [128, 1024], F32))
    psZs4 = ps4.enter_context(nc.psum_tensor("psZs4", [1, 512], F32))
    psQs4 = ps4.enter_context(nc.psum_tensor("psQs4", [1, 512], F32))
    psB4 = ps4.enter_context(nc.psum_tensor("psB4", [128, 512], F32))

    def f_sync_b4(e):
        if MAXPH < 4:
            return
        s_cst.dma(e.dma_start(out=biasf4[:], in_=biasr))
        ev["p4_bias"] = (s_cst, s_cst.v)
    B.sync(f_sync_b4)

    def f_dve_b4(e):
        if MAXPH < 4:
            return
        wde(e, "p4_bias")
        vE.inc(e.tensor_copy(out=biash[:], in_=biasf4[:]))
        sw(e, vE)
        vE.inc(e.tensor_tensor(out=biasl[:], in0=biasf4[:], in1=biash[:], op=ALU.subtract))
        ev["q_ones"] = vE.v
    B.vector(f_dve_b4)

    def p4_half(half):
        bcast_rows("ln1mu", mu1r, mu_b, half, psB4)
        bcast_rows("ln1rs", rs1r, r_b, half, psB4)
        x1_build(half, with_dots=False)

        def f_sync0(e):
            s_wf.dma(e.dma_start(out=wfF[:], in_=wf[0:128, :]))
            ev[("wfq", half, 0)] = (s_wf, s_wf.v)
        B.sync(f_sync0)

        def wt_group_secs(oc):
            if oc >= NIC:
                return
            for g in range(8):
                def f_te(e, oc=oc, g=g):
                    if g == 0:
                        wde(e, ("wfq", half, oc))
                    if g == 0 and oc == 0:
                        _w(e, vE, ev[("x1l", half, NIC - 1)])
                    if g >= 2:
                        _w(e, vE, ev[("wt_l", half, oc, g - 2)])
                    elif oc > 0:
                        _w(e, vE, ev[("wt_l", half, oc - 1, 6 + g)])
                    tpar = g % 2
                    for u in range(4):
                        ic = g * 4 + u
                        tE.inc(e.transpose(
                            psW[:, tpar * 512 + u * 128:tpar * 512 + (u + 1) * 128],
                            wfF[:, ic * 128:(ic + 1) * 128], idn[:]))
                    ev[("wt_tr", half, oc, g)] = tE.v

                def f_dve(e, oc=oc, g=g):
                    _w(e, tE, ev[("wt_tr", half, oc, g)])
                    if ("fmm", half, oc - 1) in ev:
                        _w(e, tE, ev[("fmm", half, oc - 1)])
                    tpar = g % 2
                    vE.inc(e.tensor_copy(out=wth[:, g * 512:(g + 1) * 512],
                                         in_=psW[:, tpar * 512:(tpar + 1) * 512]))
                    sw(e, vE)
                    vE.inc(e.tensor_tensor(out=wtl[:, g * 512:(g + 1) * 512],
                                           in0=psW[:, tpar * 512:(tpar + 1) * 512],
                                           in1=wth[:, g * 512:(g + 1) * 512],
                                           op=ALU.subtract))
                    ev[("wt_l", half, oc, g)] = vE.v
                B.tensor(f_te)
                B.vector(f_dve)

        for oc in range(NIC + 3):
            def f_te(e, oc=oc):
                m = oc - 1
                if 0 <= m < NIC:
                    _w(e, vE, ev[("wt_l", half, m, 7)])
                    _w(e, vE, ev["q_ones"])
                    _w(e, vE, ev["q_ones_v"])
                    if ("gs", half, m - 2) in ev:
                        _w(e, vE, ev[("gs", half, m - 2)])
                    par = m % 2
                    dst = psF[:, par * 512:(par + 1) * 512]
                    for ic in range(NIC):
                        wh = wth[:, ic * 128:(ic + 1) * 128]
                        wl = wtl[:, ic * 128:(ic + 1) * 128]
                        xh = acth[:, ic * HALF:(ic + 1) * HALF]
                        xl = actl[:, ic * HALF:(ic + 1) * HALF]
                        e.matmul(dst, wh, xh, start=(ic == 0), stop=False)
                        e.matmul(dst, wh, xl, start=False, stop=False)
                        e.matmul(dst, wl, xh, start=False, stop=False)
                    e.matmul(dst, biash[0:1, m * 128:(m + 1) * 128], ones_row[:],
                             start=False, stop=False)
                    mmf = e.matmul(dst, biasl[0:1, m * 128:(m + 1) * 128], ones_row[:],
                                   start=False, stop=True)
                    tE.inc(mmf)
                    ev[("fmm", half, m)] = tE.v
                m = oc - 3
                if 0 <= m < NIC:
                    par = m % 2
                    _w(e, vE, ev[("z2sq_l", half, m)])
                    st = m == 0
                    e.matmul(psZs4[:], onesb[:, 0:1], z2hb[:, par * HALF:(par + 1) * HALF],
                             start=st, stop=False)
                    e.matmul(psZs4[:], onesb[:, 0:1], z2lb[:, par * HALF:(par + 1) * HALF],
                             start=False, stop=(m == NIC - 1))
                    e.matmul(psQs4[:], onesb[:, 0:1], sq2hb[:, par * HALF:(par + 1) * HALF],
                             start=st, stop=False)
                    mmst4 = e.matmul(psQs4[:], onesb[:, 0:1],
                                     sq2lb[:, par * HALF:(par + 1) * HALF],
                                     start=False, stop=(m == NIC - 1))
                    tE.inc(mmst4)
                    ev[("stats4", half, m)] = tE.v

            def f_dve(e, oc=oc):
                m = oc - 1
                if 0 <= m < NIC:
                    _w(e, tE, ev[("fmm", half, m)])
                    _w(e, vE, ev[("rs", m)])
                    _w(e, vE, ev[("qbc", half)])
                    par = m % 2
                    gs = gsf[:, par * HALF:(par + 1) * HALF]
                    vE.inc(e.scalar_tensor_tensor(out=gs, in0=psF[:, par * 512:(par + 1) * 512],
                                                  scalar=rs_s[:, m:m + 1],
                                                  in1=q_b[:, half * HALF:(half + 1) * HALF],
                                                  op0=ALU.mult, op1=ALU.mult))
                    ev[("gs", half, m)] = vE.v
                    if ("z2sq_l", half, m - 2) in ev:
                        _w(e, vE, ev[("z2sq_l", half, m - 2)])
                    z2s = z2ff[:, par * HALF:(par + 1) * HALF]
                    sw(e, vE)
                    vE.inc(e.tensor_tensor(out=z2s, in0=gs,
                                           in1=acth[:, m * HALF:(m + 1) * HALF], op=ALU.add))
                    sw(e, vE)
                    vE.inc(e.tensor_tensor(out=z2s, in0=z2s,
                                           in1=actl[:, m * HALF:(m + 1) * HALF], op=ALU.add))
                    ev[("z2f", half, m)] = vE.v
                m = oc - 2
                if 0 <= m < NIC:
                    par = m % 2
                    _w(e, vE, ev[("z2f", half, m)])
                    if ("stats4", half, m - 2) in ev:
                        _w(e, tE, ev[("stats4", half, m - 2)])
                    if ("z2sp", half, m - 2) in ev:
                        wde(e, ("z2sp", half, m - 2))
                    vE.inc(e.tensor_copy(out=z2hb[:, par * HALF:(par + 1) * HALF],
                                         in_=z2ff[:, par * HALF:(par + 1) * HALF]))
                    sw(e, vE)
                    vE.inc(e.tensor_tensor(out=z2lb[:, par * HALF:(par + 1) * HALF],
                                           in0=z2ff[:, par * HALF:(par + 1) * HALF],
                                           in1=z2hb[:, par * HALF:(par + 1) * HALF],
                                           op=ALU.subtract))
                    ev[("z2l", half, m)] = vE.v
                    vE.inc(e.tensor_tensor(out=sq2f[:, par * HALF:(par + 1) * HALF],
                                           in0=z2ff[:, par * HALF:(par + 1) * HALF],
                                           in1=z2ff[:, par * HALF:(par + 1) * HALF],
                                           op=ALU.mult))
                    sw(e, vE)
                    vE.inc(e.tensor_copy(out=sq2hb[:, par * HALF:(par + 1) * HALF],
                                         in_=sq2f[:, par * HALF:(par + 1) * HALF]))
                    sw(e, vE)
                    vE.inc(e.tensor_tensor(out=sq2lb[:, par * HALF:(par + 1) * HALF],
                                           in0=sq2f[:, par * HALF:(par + 1) * HALF],
                                           in1=sq2hb[:, par * HALF:(par + 1) * HALF],
                                           op=ALU.subtract))
                    ev[("z2sq_l", half, m)] = vE.v

            def f_sync(e, oc=oc):
                m = oc + 1
                if m < NIC:
                    _w(e, vE, ev[("wt_l", half, m - 1, 7)])
                    s_wf.dma(e.dma_start(out=wfF[:], in_=wf[m * 128:(m + 1) * 128, :]))
                    ev[("wfq", half, m)] = (s_wf, s_wf.v)

            def f_gp(e, oc=oc):
                m = oc - 2
                if 0 <= m < NIC:
                    _w(e, vE, ev[("z2l", half, m)])
                    par = m % 2
                    s_z2sp[par].dma(e.dma_start(out=z2_scr[0][m, :, half * HALF:(half + 1) * HALF],
                                                in_=z2hb[:, par * HALF:(par + 1) * HALF]))
                    s_z2sp[par].dma(e.dma_start(out=z2_scr[1][m, :, half * HALF:(half + 1) * HALF],
                                                in_=z2lb[:, par * HALF:(par + 1) * HALF]))
                    ev[("z2sp", half, m)] = (s_z2sp[par], s_z2sp[par].v)
            B.tensor(f_te)
            wt_group_secs(oc)
            B.vector(f_dve)
            B.sync(f_sync)
            B.gpsimd(f_gp)

        def f_dve_fin(e):
            _w(e, tE, ev[("stats4", half, NIC - 1)])
            vE.inc(e.tensor_scalar(out=mu2r[0:1, half * HALF:(half + 1) * HALF], in0=psZs4[:],
                                   scalar1=1.0 / D, scalar2=None, op0=ALU.mult))
            vE.inc(e.tensor_scalar(out=srow[:], in0=psQs4[:], scalar1=1.0 / D, scalar2=None,
                                   op0=ALU.mult))
            mu = mu2r[0:1, half * HALF:(half + 1) * HALF]
            sw(e, vE)
            vE.inc(e.tensor_tensor(out=srow2[:], in0=mu, in1=mu, op=ALU.mult))
            sw(e, vE)
            vE.inc(e.tensor_tensor(out=srow[:], in0=srow[:], in1=srow2[:], op=ALU.subtract))
            sw(e, vE)
            vE.inc(e.tensor_scalar(out=srow[:], in0=srow[:], scalar1=1e-5, scalar2=None,
                                   op0=ALU.add))
            ev[("ln2_var", half)] = vE.v
        B.vector(f_dve_fin)

        def f_act_fin(e):
            _w(e, vE, ev[("ln2_var", half)])
            aE.inc(e.activation(out=srow[:], in_=srow[:], func=ACTF.Sqrt, bias=0.0, scale=1.0))
            ev[("ln2_sqrt", half)] = aE.v
        B.scalar(f_act_fin)

        def f_dve_fin2(e):
            _w(e, aE, ev[("ln2_sqrt", half)])
            vE.inc(e.reciprocal(out=rs2r[0:1, half * HALF:(half + 1) * HALF], in_=srow[:]))
        B.vector(f_dve_fin2)
        barrier()

    for half in range(2 if MAXPH >= 4 else 0):
        p4_half(half)
        keep = ("rs", "qbc", "z2sp", "q_ones")
        ev = {k: v for k, v in ev.items()
              if not isinstance(k, tuple) or k[0] in keep}

    ps4.close()
    es4.close()

    es5 = ExitStack()
    zoh = es5.enter_context(nc.sbuf_tensor("zoh", [128, 2 * HALF], BF16))
    zol = es5.enter_context(nc.sbuf_tensor("zol", [128, 2 * HALF], BF16))
    x2tmp = es5.enter_context(nc.sbuf_tensor("x2tmp", [128, 2 * HALF], F32))
    ostg = es5.enter_context(nc.sbuf_tensor("ostg", [128, 2 * HALF], F32))
    ps5 = ExitStack()
    psX5 = ps5.enter_context(nc.psum_tensor("psX5", [128, 1024], F32))
    psB5 = ps5.enter_context(nc.psum_tensor("psB5", [128, 512], F32))

    def p5_half(half):
        bcast_rows("ln2mu", mu2r, mu_b, half, psB5)
        bcast_rows("ln2rs", rs2r, r_b, half, psB5)
        for ic in range(NIC + 2):
            def f_gp(e, ic=ic):
                if ic < NIC:
                    if ("x2f", half, ic - 2) in ev:
                        _w(e, vE, ev[("x2f", half, ic - 2)])
                    par = ic % 2
                    s_zo[par].dma(e.dma_start(out=zoh[:, par * HALF:(par + 1) * HALF],
                                              in_=z2_scr[0][ic, :, half * HALF:(half + 1) * HALF]))
                    s_zo[par].dma(e.dma_start(out=zol[:, par * HALF:(par + 1) * HALF],
                                              in_=z2_scr[1][ic, :, half * HALF:(half + 1) * HALF]))
                    ev[("zo", half, ic)] = (s_zo[par], s_zo[par].v)
            def f_dve(e, ic=ic):
                m = ic - 1
                if 0 <= m < NIC:
                    if m == 0:
                        _w(e, vE, ev[("ln2mu_bc", half)])
                        _w(e, vE, ev[("ln2rs_bc", half)])
                    par = m % 2
                    wde(e, ("zo", half, m))
                    if ("x2tr", half, m - 2) in ev:
                        _w(e, tE, ev[("x2tr", half, m - 2)])
                    sw(e, vE)
                    x2 = x2tmp[:, par * HALF:(par + 1) * HALF]
                    vE.inc(e.tensor_tensor(out=x2, in0=zoh[:, par * HALF:(par + 1) * HALF],
                                           in1=zol[:, par * HALF:(par + 1) * HALF], op=ALU.add))
                    sw(e, vE)
                    vE.inc(e.tensor_tensor(out=x2, in0=x2, in1=mu_b[:], op=ALU.subtract))
                    sw(e, vE)
                    vE.inc(e.tensor_tensor(out=x2, in0=x2, in1=r_b[:], op=ALU.mult))
                    ev[("x2f", half, m)] = vE.v
            def f_te(e, ic=ic):
                m = ic - 1
                if 0 <= m < NIC:
                    par = m % 2
                    _w(e, vE, ev[("x2f", half, m)])
                    if ("x2ev", half, m - 2) in ev:
                        _w(e, vE, ev[("x2ev", half, m - 2)])
                    for tt in range(4):
                        tE.inc(e.transpose(
                            psX5[:, par * 512 + tt * 128:par * 512 + (tt + 1) * 128],
                            x2tmp[:, par * HALF + tt * 128:par * HALF + (tt + 1) * 128],
                            idn[:]))
                    ev[("x2tr", half, m)] = tE.v
            def f_dve2(e, ic=ic):
                m = ic - 1
                if 0 <= m < NIC:
                    par = m % 2
                    _w(e, tE, ev[("x2tr", half, m)])
                    if ("ost", half, m - 2) in ev:
                        wde(e, ("ost", half, m - 2))
                    vE.inc(e.tensor_copy(out=ostg[:, par * HALF:(par + 1) * HALF],
                                         in_=psX5[:, par * 512:(par + 1) * 512]))
                    ev[("x2ev", half, m)] = vE.v
            def f_sync(e, ic=ic):
                m = ic - 1
                if 0 <= m < NIC:
                    par = m % 2
                    _w(e, vE, ev[("x2ev", half, m)])
                    for tt in range(4):
                        s_ost[par].dma(e.dma_start(
                            out=out[(half * 4 + tt) * 128:(half * 4 + tt + 1) * 128,
                                    m * 128:(m + 1) * 128],
                            in_=ostg[:, par * HALF + tt * 128:par * HALF + (tt + 1) * 128]))
                    ev[("ost", half, m)] = (s_ost[par], s_ost[par].v)
            B.gpsimd(f_gp)
            B.vector(f_dve)
            B.tensor(f_te)
            B.vector(f_dve2)
            B.sync(f_sync)

    for half in range(2 if MAXPH >= 5 else 0):
        p5_half(half)
        barrier()

    def f_sync_fin(e):
        for sset in (SY_SEMS, GP_SEMS):
            for sm in sset:
                if sm.v > 0:
                    e.wait_ge(sm.h, sm.v)
    B.sync(f_sync_fin)

    ps5.close()
    es5.close()
    es3.close()
    esA.close()
    B.__exit__(None, None, None)
    es.close()
    return nc


_NC_CACHE = {}


def kernel(L0_attn_hidden_states, L0_attn_Wq, L0_attn_Wkv, L0_attn_Wo,
           L0_ffn_weight, L0_ffn_bias, L0_ffn_centroids, L0_ffn_assignments):
    x = np.ascontiguousarray(np.asarray(L0_attn_hidden_states, dtype=np.float32))
    wq = np.ascontiguousarray(np.asarray(L0_attn_Wq, dtype=np.float32))
    wkv = np.ascontiguousarray(np.asarray(L0_attn_Wkv, dtype=np.float32))
    wo = np.ascontiguousarray(np.asarray(L0_attn_Wo, dtype=np.float32))
    wf = np.ascontiguousarray(np.asarray(L0_ffn_weight, dtype=np.float32))
    bias = np.ascontiguousarray(np.asarray(L0_ffn_bias, dtype=np.float32)).reshape(1, D)
    cent = np.ascontiguousarray(np.asarray(L0_ffn_centroids, dtype=np.float32))
    assign = np.ascontiguousarray(np.asarray(L0_ffn_assignments, dtype=np.int32)).reshape(1, D)

    slopes = _slopes()
    jj = np.arange(512, dtype=np.float32)[None, None, :]
    ii = np.arange(128, dtype=np.float32)[None, :, None]
    alibi = (slopes[:, None, None] * (jj - ii)).astype(np.float32)

    ident = np.eye(128, dtype=np.float32)
    ones128 = np.ones((128, 128), dtype=np.float32)
    iota = np.arange(64, dtype=np.float32).reshape(64, 1)

    in_maps = []
    for c in range(8):
        b, j = divmod(c, 2)
        ch = CHUNKS[j]
        x_own = np.concatenate([x[b, k * 128:(k + 1) * 128] for k in ch], axis=0)
        mk = np.zeros((8, 128, 640), dtype=np.float32)
        for sg in range(8):
            L = LS[sg] * 128
            t0 = ch[sg] * 128
            r0 = L - 640
            keys = r0 + np.arange(640)[None, :]
            toks = t0 + np.arange(128)[:, None]
            mk[sg] = np.where(keys <= toks, 0.0, NEG).astype(np.float32)
        in_maps.append({
            "x_own": x_own, "x_kv": x[b], "wq": wq, "wo": wo, "wf": wf,
            "wkv": wkv, "biasr": bias, "cent": cent, "assign_row": assign,
            "alibi": alibi, "masks": mk.astype(BF),
            "identf": ident, "onesf": ones128, "iota64": iota,
        })

    if "nc" not in _NC_CACHE:
        _NC_CACHE["nc"] = build_nc()
    nc = _NC_CACHE["nc"]
    res = run_bass_kernel_spmd(nc, in_maps, core_ids=list(range(8)))

    full = np.empty((Bb, S, D), dtype=np.float32)
    for c in range(8):
        b, j = divmod(c, 2)
        o = np.asarray(res.results[c]["out"])
        for k, chk in enumerate(CHUNKS[j]):
            full[b, chk * 128:(chk + 1) * 128] = o[k * 128:(k + 1) * 128]
    return full


# version 09
# speedup vs baseline: 1.1209x; 1.0390x over previous
import math
import numpy as np
import ml_dtypes
import concourse.bass as bass
import concourse.mybir as mybir
from concourse.bass_utils import run_bass_kernel_spmd
from contextlib import ExitStack

F32 = mybir.dt.float32
BF16 = mybir.dt.bfloat16
I32 = mybir.dt.int32
ALU = mybir.AluOpType
ACTF = mybir.ActivationFunctionType
AX = mybir.AxisListType.X
BF = ml_dtypes.bfloat16

Bb, S, D, NH, DH = 4, 2048, 4096, 32, 128
NIC = D // 128
TPC = 1024
HALF = 512
LS = [5, 6, 7, 8, 13, 14, 15, 16]
CHUNKS = ([0, 1, 2, 3, 12, 13, 14, 15], [4, 5, 6, 7, 8, 9, 10, 11])
SCALE = float(DH) ** -0.5
NEG = -1.0e30
MAXPH = 99


def _slopes():
    start = 2.0 ** (-2.0 ** (-(math.log2(NH) - 3)))
    return np.array([start ** (i + 1) for i in range(NH)], dtype=np.float32)


class Sem:
    def __init__(self, nc, es, name):
        self.h = es.enter_context(nc.semaphore(name))
        self.v = 0

    def inc(self, instr, k=1):
        instr.then_inc(self.h, k)
        self.v += k
        return self.v

    def dma(self, instr):
        return self.inc(instr, 16)


def _w(eng, sem, val):
    if val > 0:
        eng.wait_ge(sem.h, val)


def build_nc():
    nc = bass.Bass("TRN2", target_bir_lowering=False)
    x_own = nc.dram_tensor("x_own", [TPC, D], F32, kind="ExternalInput").ap()
    x_kv = nc.dram_tensor("x_kv", [S, D], F32, kind="ExternalInput").ap()
    wq = nc.dram_tensor("wq", [D, D], F32, kind="ExternalInput").ap()
    wo = nc.dram_tensor("wo", [D, D], F32, kind="ExternalInput").ap()
    wf = nc.dram_tensor("wf", [D, D], F32, kind="ExternalInput").ap()
    wkv = nc.dram_tensor("wkv", [D, 256], F32, kind="ExternalInput").ap()
    biasr = nc.dram_tensor("biasr", [1, D], F32, kind="ExternalInput").ap()
    cent = nc.dram_tensor("cent", [64, D], F32, kind="ExternalInput").ap()
    assign_row = nc.dram_tensor("assign_row", [1, D], I32, kind="ExternalInput").ap()
    alibi = nc.dram_tensor("alibi", [NH, 128, 512], F32, kind="ExternalInput").ap()
    masks = nc.dram_tensor("masks", [8, 128, 640], BF16, kind="ExternalInput").ap()
    identf = nc.dram_tensor("identf", [128, 128], F32, kind="ExternalInput").ap()
    onesf = nc.dram_tensor("onesf", [128, 128], F32, kind="ExternalInput").ap()
    iota64 = nc.dram_tensor("iota64", [64, 1], F32, kind="ExternalInput").ap()
    out = nc.dram_tensor("out", [TPC, D], F32, kind="ExternalOutput").ap()
    o_scr = [nc.dram_tensor(f"o_scr{p}", [NIC, 128, TPC], BF16).ap() for p in range(2)]
    xt_scr = [nc.dram_tensor(f"xt_scr{p}", [NIC, 128, TPC], BF16).ap() for p in range(2)]
    z1_scr = [nc.dram_tensor(f"z1_scr{p}", [NIC, 128, TPC], BF16).ap() for p in range(2)]
    z2_scr = [nc.dram_tensor(f"z2_scr{p}", [NIC, 128, TPC], BF16).ap() for p in range(2)]
    cc_in = nc.dram_tensor("cc_in", [64, 1], F32)
    cc_out = nc.dram_tensor("cc_out", [64, 1], F32)

    es = ExitStack()
    qS = Sem(nc, es, "qS")
    qG = Sem(nc, es, "qG")
    tE = Sem(nc, es, "tE")
    vE = Sem(nc, es, "vE")
    aE = Sem(nc, es, "aE")
    cS = Sem(nc, es, "cS")
    bar = Sem(nc, es, "bar")
    s_cst = Sem(nc, es, "s_cst")
    s_wkv = Sem(nc, es, "s_wkv")
    s_xnb = [Sem(nc, es, f"s_xnb{p}") for p in range(2)]
    s_mask = Sem(nc, es, "s_mask")
    s_xnat = Sem(nc, es, "s_xnat")
    s_wq = Sem(nc, es, "s_wq")
    s_al = [Sem(nc, es, f"s_al{p}") for p in range(2)]
    s_osp = Sem(nc, es, "s_osp")
    s_old = Sem(nc, es, "s_old")
    s_xoc = [Sem(nc, es, f"s_xoc{p}") for p in range(2)]
    s_wo = Sem(nc, es, "s_wo")
    s_wf = Sem(nc, es, "s_wf")
    s_z1sp = [Sem(nc, es, f"s_z1sp{p}") for p in range(2)]
    s_z2sp = [Sem(nc, es, f"s_z2sp{p}") for p in range(2)]
    s_zr = [Sem(nc, es, f"s_zr{p}") for p in range(2)]
    s_zo = [Sem(nc, es, f"s_zo{p}") for p in range(2)]
    s_ost = [Sem(nc, es, f"s_ost{p}") for p in range(2)]
    s_cc = Sem(nc, es, "s_cc")
    SY_SEMS = [qS, s_cst, s_wkv, s_xnb[0], s_xnb[1], s_mask, s_xnat, s_wq,
               s_al[0], s_al[1], s_wo, s_wf, s_ost[0], s_ost[1]]
    GP_SEMS = [qG, s_osp, s_old, s_xoc[0], s_xoc[1], s_z1sp[0], s_z1sp[1],
               s_z2sp[0], s_z2sp[1], s_zr[0], s_zr[1], s_zo[0], s_zo[1], s_cc]

    def wde(e, key):
        sm, v = ev[key]
        _w(e, sm, v)

    def sw(e, sem):
        e.wait_ge(sem.h, sem.v)

    idn = es.enter_context(nc.sbuf_tensor("idn", [128, 128], F32))
    onesb_f = es.enter_context(nc.sbuf_tensor("onesb_f", [128, 128], F32))
    onesb = es.enter_context(nc.sbuf_tensor("onesb", [128, 128], BF16))
    iota_s = es.enter_context(nc.sbuf_tensor("iota_s", [64, 1], F32))
    rs_s = es.enter_context(nc.sbuf_tensor("rs_s", [128, 32], F32))
    csb = es.enter_context(nc.sbuf_tensor("csb", [64, 1], BF16))

    B = nc.Block()
    B.__enter__()

    def barrier():
        tgt = bar.v + 5
        bar.v = tgt
        sy_vals = [(s.h, s.v) for s in SY_SEMS if s.v > 0]
        gp_vals = [(s.h, s.v) for s in GP_SEMS if s.v > 0]
        for attr, drains, dr in ((B.tensor, None, True), (B.scalar, None, True),
                                 (B.vector, None, True), (B.sync, sy_vals, False),
                                 (B.gpsimd, gp_vals, True)):
            def f(e, tgt=tgt, drains=drains, dr=dr):
                if drains:
                    for h, v in drains:
                        e.wait_ge(h, v)
                if dr:
                    e.drain()
                e.sem_inc(bar.h, 1)
                e.wait_ge(bar.h, tgt)
            attr(f)

    ev = {}

    def tsp_rounds(tag, key, ntiles, src_of, hi_of, lo_of, ps, slots, tile_w,
                   pre_wait=None, scale=1.0, src_p=128, ident_ap=None):
        idap = ident_ap if ident_ap is not None else idn[:]
        assert slots * tile_w <= 512
        nr = (ntiles + slots - 1) // slots

        def boff(r):
            return (r % 2) * 512

        for r in range(nr):
            i0, i1 = r * slots, min((r + 1) * slots, ntiles)

            def f_te(e, r=r, i0=i0, i1=i1):
                if r == 0 and pre_wait is not None:
                    pre_wait(e)
                if r >= 2 and (tag + "_r", key, r - 2) in ev:
                    _w(e, vE, ev[(tag + "_r", key, r - 2)])
                for i in range(i0, i1):
                    g = i - i0
                    o = boff(r) + g * tile_w
                    tE.inc(e.transpose(ps[:, o:o + tile_w], src_of(i), idap))
                ev[(tag + "_tr", key, r)] = tE.v

            def f_dve(e, r=r, i0=i0, i1=i1):
                _w(e, tE, ev[(tag + "_tr", key, r)])
                for i in range(i0, i1):
                    g = i - i0
                    o = boff(r) + g * tile_w
                    if scale == 1.0:
                        vE.inc(e.tensor_copy(out=hi_of(i), in_=ps[:, o:o + tile_w]))
                    else:
                        vE.inc(e.tensor_scalar(out=hi_of(i), in0=ps[:, o:o + tile_w],
                                               scalar1=scale, scalar2=None, op0=ALU.mult))
                ev[(tag + "_hr", key, r)] = vE.v
                sw(e, vE)
                for i in range(i0, i1):
                    g = i - i0
                    o = boff(r) + g * tile_w
                    if scale == 1.0:
                        vE.inc(e.tensor_tensor(out=lo_of(i), in0=ps[:, o:o + tile_w],
                                               in1=hi_of(i), op=ALU.subtract))
                    else:
                        vE.inc(e.scalar_tensor_tensor(
                            out=lo_of(i), in0=ps[:, o:o + tile_w],
                            scalar=scale, in1=hi_of(i), op0=ALU.mult, op1=ALU.subtract))
                ev[(tag + "_r", key, r)] = vE.v
            B.tensor(f_te)
            B.vector(f_dve)
        ev[(tag + "_hi", key)] = ev[(tag + "_hr", key, nr - 1)]
        ev[(tag + "_lo", key)] = ev[(tag + "_r", key, nr - 1)]
        ev[(tag + "_trlast", key)] = ev[(tag + "_tr", key, nr - 1)]

    es1kv = ExitStack()
    kth = es1kv.enter_context(nc.sbuf_tensor("kth", [128, S], BF16))
    ktl = es1kv.enter_context(nc.sbuf_tensor("ktl", [128, S], BF16))
    vth = es1kv.enter_context(nc.sbuf_tensor("vth", [128, S], BF16))
    vtl = es1kv.enter_context(nc.sbuf_tensor("vtl", [128, S], BF16))
    es0 = ExitStack()
    wkvf = es0.enter_context(nc.sbuf_tensor("wkvf", [128, NIC * 256], F32))
    wkvh = es0.enter_context(nc.sbuf_tensor("wkvh", [128, NIC * 256], BF16))
    wkvl = es0.enter_context(nc.sbuf_tensor("wkvl", [128, NIC * 256], BF16))
    xnb0 = es0.enter_context(nc.sbuf_tensor("xnb0", [128, 2 * D], F32))
    xtkh = es0.enter_context(nc.sbuf_tensor("xtkh", [128, NIC * 256], BF16))
    xtkl = es0.enter_context(nc.sbuf_tensor("xtkl", [128, NIC * 256], BF16))
    vtf = es0.enter_context(nc.sbuf_tensor("vtf", [128, 256], F32))
    ps0 = ExitStack()
    psK = ps0.enter_context(nc.psum_tensor("psK", [128, 512], F32))
    psV = ps0.enter_context(nc.psum_tensor("psV", [128, 512], F32))
    psX0 = ps0.enter_context(nc.psum_tensor("psX0", [128, 1024], F32))
    psVt = ps0.enter_context(nc.psum_tensor("psVt", [128, 512], F32))

    def f_sync0(e):
        s_cst.dma(e.dma_start(out=idn[:], in_=identf))
        s_cst.dma(e.dma_start(out=onesb_f[:], in_=onesf))
        s_cst.dma(e.dma_start(out=iota_s[:], in_=iota64))
        ev["p0_loads"] = (s_cst, s_cst.v)
        s_wkv.dma(e.dma_start(
            out=wkvf[:].rearrange("p (ic c) -> p ic c", c=256),
            in_=wkv.rearrange("(ic p) c -> p ic c", p=128)))
        ev["p0_wkv"] = (s_wkv, s_wkv.v)
    B.sync(f_sync0)

    def f_dve0(e):
        wde(e, "p0_loads")
        vE.inc(e.tensor_copy(out=onesb[:], in_=onesb_f[:]))
        wde(e, "p0_wkv")
        vE.inc(e.tensor_copy(out=wkvh[:], in_=wkvf[:]))
        ev["p0_wkvh"] = vE.v
        sw(e, vE)
        vE.inc(e.tensor_tensor(out=wkvl[:], in0=wkvf[:], in1=wkvh[:], op=ALU.subtract))
        ev["p0_wkvl"] = vE.v
    B.vector(f_dve0)

    for ks in range(8):
        def f_sync(e, ks=ks):
            for tt in range(2):
                if ks > 0:
                    _w(e, tE, ev[("kvx_trlast", (ks - 1, tt))])
                s_xnb[tt].dma(e.dma_start(out=xnb0[:, tt * D:(tt + 1) * D],
                                          in_=x_kv[(ks * 2 + tt) * 128:(ks * 2 + tt + 1) * 128, :]))
                ev[("kv_x", ks, tt)] = (s_xnb[tt], s_xnb[tt].v)
        B.sync(f_sync)

        for tt in range(2):
            def prew(e, ks=ks, tt=tt):
                wde(e, "p0_loads")
                wde(e, ("kv_x", ks, tt))
                if (ks, tt) != (0, 0):
                    pks, ptt = (ks, tt - 1) if tt > 0 else (ks - 1, 1)
                    _w(e, vE, ev[("kvx_lo", (pks, ptt))])
            tsp_rounds("kvx", (ks, tt), NIC,
                       (lambda i, ks=ks, tt=tt:
                        xnb0[:, tt * D + i * 128:tt * D + (i + 1) * 128]),
                       (lambda i, tt=tt: xtkh[:, i * 256 + tt * 128:i * 256 + (tt + 1) * 128]),
                       (lambda i, tt=tt: xtkl[:, i * 256 + tt * 128:i * 256 + (tt + 1) * 128]),
                       psX0, 4, 128, pre_wait=prew)

        def f_te(e, ks=ks):
            _w(e, vE, ev[("kvx_hi", (ks, 1))])
            _w(e, vE, ev[("kvx_lo", (ks, 1))])
            if ks == 0:
                _w(e, vE, ev["p0_wkvl"])
            else:
                _w(e, aE, ev[("kv_kh", ks - 1)])
                _w(e, vE, ev[("kv_kl", ks - 1)])
                _w(e, vE, ev[("kv_vtf", ks - 1)])
                _w(e, vE, ev[("kv_vl", ks - 1, 1)])
            for ic in range(NIC):
                wk_h = wkvh[:, ic * 256:ic * 256 + 128]
                wk_l = wkvl[:, ic * 256:ic * 256 + 128]
                xh = xtkh[:, ic * 256:(ic + 1) * 256]
                xl = xtkl[:, ic * 256:(ic + 1) * 256]
                e.matmul(psK[:, 0:256], wk_h, xh, start=(ic == 0), stop=False)
                e.matmul(psK[:, 0:256], wk_h, xl, start=False, stop=False)
                mmK = e.matmul(psK[:, 0:256], wk_l, xh, start=False, stop=(ic == NIC - 1))
            tE.inc(mmK)
            ev[("kv_kmm", ks)] = tE.v
            for ic in range(NIC):
                wv_h = wkvh[:, ic * 256 + 128:(ic + 1) * 256]
                wv_l = wkvl[:, ic * 256 + 128:(ic + 1) * 256]
                xh = xtkh[:, ic * 256:(ic + 1) * 256]
                xl = xtkl[:, ic * 256:(ic + 1) * 256]
                e.matmul(psV[:, 0:256], wv_h, xh, start=(ic == 0), stop=False)
                e.matmul(psV[:, 0:256], wv_h, xl, start=False, stop=False)
                mmV = e.matmul(psV[:, 0:256], wv_l, xh, start=False, stop=(ic == NIC - 1))
            tE.inc(mmV)
            ev[("kv_vmm", ks)] = tE.v
        B.tensor(f_te)

        def f_dve_v(e, ks=ks):
            _w(e, tE, ev[("kv_vmm", ks)])
            if ks > 0:
                _w(e, aE, ev[("kv_vh", ks - 1, 1)])
                _w(e, vE, ev[("kv_vl", ks - 1, 1)])
            vE.inc(e.tensor_copy(out=vtf[:], in_=psV[:, 0:256]))
            ev[("kv_vtf", ks)] = vE.v
        B.vector(f_dve_v)

        def f_te_v(e, ks=ks):
            _w(e, vE, ev[("kv_vtf", ks)])
            for sub in range(2):
                tE.inc(e.transpose(psVt[:, sub * 128:(sub + 1) * 128],
                                   vtf[:, sub * 128:(sub + 1) * 128], idn[:]))
                ev[("kv_vtr", ks, sub)] = tE.v
        B.tensor(f_te_v)

        def f_act(e, ks=ks):
            _w(e, tE, ev[("kv_kmm", ks)])
            aE.inc(e.activation(out=kth[:, ks * 256:(ks + 1) * 256], in_=psK[:, 0:256],
                                func=ACTF.Copy, bias=0.0, scale=1.0))
            ev[("kv_kh", ks)] = aE.v
            for sub in range(2):
                _w(e, tE, ev[("kv_vtr", ks, sub)])
                u = ks * 2 + sub
                aE.inc(e.activation(out=vth[:, u * 128:(u + 1) * 128],
                                    in_=psVt[:, sub * 128:(sub + 1) * 128],
                                    func=ACTF.Copy, bias=0.0, scale=1.0))
                ev[("kv_vh", ks, sub)] = aE.v
        B.scalar(f_act)

        def f_dve(e, ks=ks):
            _w(e, aE, ev[("kv_kh", ks)])
            vE.inc(e.tensor_tensor(out=ktl[:, ks * 256:(ks + 1) * 256], in0=psK[:, 0:256],
                                   in1=kth[:, ks * 256:(ks + 1) * 256], op=ALU.subtract))
            ev[("kv_kl", ks)] = vE.v
            for sub in range(2):
                _w(e, aE, ev[("kv_vh", ks, sub)])
                u = ks * 2 + sub
                vE.inc(e.tensor_tensor(out=vtl[:, u * 128:(u + 1) * 128],
                                       in0=psVt[:, sub * 128:(sub + 1) * 128],
                                       in1=vth[:, u * 128:(u + 1) * 128], op=ALU.subtract))
                ev[("kv_vl", ks, sub)] = vE.v
        B.vector(f_dve)
    barrier()
    ps0.close()
    es0.close()

    es1 = ExitStack()
    mask_s = es1.enter_context(nc.sbuf_tensor("mask_s", [128, 4 * 640], BF16))
    albf = es1.enter_context(nc.sbuf_tensor("albf", [128, 1024], F32))
    xth = es1.enter_context(nc.sbuf_tensor("xth", [128, NIC * HALF], BF16))
    xtl = es1.enter_context(nc.sbuf_tensor("xtl", [128, NIC * HALF], BF16))
    wqf = es1.enter_context(nc.sbuf_tensor("wqf", [128, NIC * 128], F32))
    wqh = es1.enter_context(nc.sbuf_tensor("wqh", [128, 2 * NIC * 128], BF16))
    wql = es1.enter_context(nc.sbuf_tensor("wql", [128, 2 * NIC * 128], BF16))
    qsh = es1.enter_context(nc.sbuf_tensor("qsh", [128, 2 * HALF], BF16))
    qsl = es1.enter_context(nc.sbuf_tensor("qsl", [128, 2 * HALF], BF16))
    logits = es1.enter_context(nc.sbuf_tensor("logits", [128, 3 * 2048], F32))
    pbuf = es1.enter_context(nc.sbuf_tensor("pbuf", [128, 2 * 2048], F32))
    pth = es1.enter_context(nc.sbuf_tensor("pth", [128, 2 * 2048], BF16))
    ptl = es1.enter_context(nc.sbuf_tensor("ptl", [128, 2 * 2048], BF16))
    obh = es1.enter_context(nc.sbuf_tensor("obh", [128, 512], BF16))
    obl = es1.enter_context(nc.sbuf_tensor("obl", [128, 512], BF16))
    negm = es1.enter_context(nc.sbuf_tensor("negm", [128, 2], F32))
    sums = es1.enter_context(nc.sbuf_tensor("sums", [128, 2], F32))
    rrec = es1.enter_context(nc.sbuf_tensor("rrec", [128, 2], F32))
    ps1 = ExitStack()
    psQ = ps1.enter_context(nc.psum_tensor("psQ", [128, 1024], F32))
    psS = ps1.enter_context(nc.psum_tensor("psS", [128, 1024], F32))
    psT = ps1.enter_context(nc.psum_tensor("psT", [128, 1024], F32))
    psO = ps1.enter_context(nc.psum_tensor("psO", [128, 1024], F32))

    def wq_dma(e, h):
        for dcg in range(4):
            src_ = wq[dcg * 1024:(dcg + 1) * 1024,
                      h * 128:(h + 1) * 128].rearrange("(ic p) o -> p ic o", p=128)
            dst = wqf[:, dcg * 1024:(dcg + 1) * 1024].rearrange("p (ic c) -> p ic c", c=128)
            s_wq.dma(e.dma_start(out=dst, in_=src_))
        ev[("wq", h)] = (s_wq, s_wq.v)

    def wq_split(e, h):
        wde(e, ("wq", h))
        if ("qmm", h - 2) in ev:
            _w(e, tE, ev[("qmm", h - 2)])
        wb = (h % 2) * NIC * 128
        vE.inc(e.tensor_copy(out=wqh[:, wb:wb + NIC * 128], in_=wqf[:]))
        ev[("wqh", h)] = vE.v
        sw(e, vE)
        vE.inc(e.tensor_tensor(out=wql[:, wb:wb + NIC * 128], in0=wqf[:],
                               in1=wqh[:, wb:wb + NIC * 128], op=ALU.subtract))
        ev[("wql", h)] = vE.v

    def q_proj(e, h, half):
        _w(e, vE, ev[("wql", h)])
        _w(e, vE, ev[("xb_a", half)])
        _w(e, vE, ev[("xb_v", half)])
        if ("qlo", h - 2) in ev:
            _w(e, vE, ev[("qlo", h - 2)])
        par = h % 2
        wb = par * NIC * 128
        dst = psQ[:, par * 512:(par + 1) * 512]
        for ic in range(NIC):
            wh = wqh[:, wb + ic * 128:wb + (ic + 1) * 128]
            wl = wql[:, wb + ic * 128:wb + (ic + 1) * 128]
            xh = xth[:, ic * HALF:(ic + 1) * HALF]
            xl = xtl[:, ic * HALF:(ic + 1) * HALF]
            e.matmul(dst, wh, xh, start=(ic == 0), stop=False)
            e.matmul(dst, wh, xl, start=False, stop=False)
            mmq = e.matmul(dst, wl, xh, start=False, stop=(ic == NIC - 1))
        tE.inc(mmq)
        ev[("qmm", h)] = tE.v

    def q_evac(e, h):
        _w(e, tE, ev[("qmm", h)])
        if ("amm_last", h - 2) in ev:
            _w(e, tE, ev[("amm_last", h - 2)])
        par = h % 2
        vE.inc(e.tensor_scalar(out=qsh[:, par * 512:(par + 1) * 512],
                               in0=psQ[:, par * 512:(par + 1) * 512],
                               scalar1=SCALE, scalar2=None, op0=ALU.mult))
        ev[("qhi", h)] = vE.v
        sw(e, vE)
        vE.inc(e.scalar_tensor_tensor(out=qsl[:, par * 512:(par + 1) * 512],
                                      in0=psQ[:, par * 512:(par + 1) * 512], scalar=SCALE,
                                      in1=qsh[:, par * 512:(par + 1) * 512],
                                      op0=ALU.mult, op1=ALU.subtract))
        ev[("qlo", h)] = vE.v

    def p1_half(half):
        slopes = _slopes()
        for tt in range(4):
            def f_sync(e, tt=tt):
                if tt > 0:
                    _w(e, tE, ev[("xb_trlast", (half, tt - 1))])
                s_xnat.dma(e.dma_start(out=pbuf[:, 0:4096],
                                       in_=x_own[(half * 4 + tt) * 128:(half * 4 + tt + 1) * 128, :]))
                ev[("x_nat", half, tt)] = (s_xnat, s_xnat.v)
                if tt == 0:
                    for s in range(4):
                        s_mask.dma(e.dma_start(out=mask_s[:, s * 640:(s + 1) * 640],
                                               in_=masks[half * 4 + s]))
                    ev[("mask", half)] = (s_mask, s_mask.v)
            B.sync(f_sync)

            def prew(e, tt=tt):
                wde(e, ("x_nat", half, tt))
                if tt > 0:
                    _w(e, vE, ev[("xb_lo", (half, tt - 1))])
            tsp_rounds("xb", (half, tt), NIC,
                       (lambda i: pbuf[:, i * 128:(i + 1) * 128]),
                       (lambda i, tt=tt: xth[:, i * HALF + tt * 128:i * HALF + (tt + 1) * 128]),
                       (lambda i, tt=tt: xtl[:, i * HALF + tt * 128:i * HALF + (tt + 1) * 128]),
                       psT, 4, 128, pre_wait=prew)
        ev[("xb_a", half)] = ev[("xb_hi", (half, 3))]
        ev[("xb_v", half)] = ev[("xb_lo", (half, 3))]

        def f_gp_sp(e):
            _w(e, vE, ev[("xb_a", half)])
            _w(e, vE, ev[("xb_v", half)])
            for ic in range(NIC):
                qG.dma(e.dma_start(out=xt_scr[0][ic, :, half * HALF:(half + 1) * HALF],
                                   in_=xth[:, ic * HALF:(ic + 1) * HALF]))
                qG.dma(e.dma_start(out=xt_scr[1][ic, :, half * HALF:(half + 1) * HALF],
                                   in_=xtl[:, ic * HALF:(ic + 1) * HALF]))
            ev[("xt_spill", half)] = qG.v
        B.gpsimd(f_gp_sp)

        def f_sync_pro(e):
            wq_dma(e, 0)
            s_al[0].dma(e.dma_start(out=albf[:, 0:512], in_=alibi[0]))
            ev[("al", 0)] = (s_al[0], s_al[0].v)
        B.sync(f_sync_pro)
        B.vector(lambda e: wq_split(e, 0))

        def f_sync_pro2(e):
            _w(e, vE, ev[("wql", 0)])
            wq_dma(e, 1)
        B.sync(f_sync_pro2)
        B.tensor(lambda e: q_proj(e, 0, half))
        B.vector(lambda e: q_evac(e, 0))

        iters = [(h, s) for h in range(NH) for s in range(4)]
        NIT = len(iters)
        gb = [0]
        gtg = [0]

        def blocks_of(sg):
            L = LS[sg] * 128
            res = []
            off = 0
            while off < L:
                res.append((off, min(512, L - off)))
                off += 512
            return res

        pss_read = {}

        def score_block_secs(n):
            if not (0 <= n < NIT):
                return
            h, s = iters[n]
            sg = half * 4 + s
            L = LS[sg] * 128
            lpar = n % 3
            slope_h = float(slopes[h])
            for bi, (off, nb) in enumerate(blocks_of(sg)):
                gidx = gb[0]
                gb[0] += 1
                spar = gidx % 2

                def f_te(e, n=n, h=h, s=s, off=off, nb=nb, spar=spar, bi=bi):
                    if bi == 0 and s == 0:
                        _w(e, vE, ev[("qlo", h)])
                        wde(e, ("al", h))
                        if h == 0:
                            wde(e, ("mask", half))
                    if spar in pss_read:
                        _w(e, vE, pss_read[spar])
                    qpar = h % 2
                    qh_t = qsh[:, qpar * 512 + s * 128:qpar * 512 + (s + 1) * 128]
                    ql_t = qsl[:, qpar * 512 + s * 128:qpar * 512 + (s + 1) * 128]
                    dst = psS[:, spar * 512:spar * 512 + nb]
                    e.matmul(dst, qh_t, kth[:, off:off + nb], start=True, stop=False)
                    e.matmul(dst, qh_t, ktl[:, off:off + nb], start=False, stop=False)
                    mms_ = e.matmul(dst, ql_t, kth[:, off:off + nb], start=False, stop=True)
                    tE.inc(mms_)
                    ev[("smm", n, off)] = tE.v
                B.tensor(f_te)

                def f_dve(e, n=n, off=off, nb=nb, spar=spar, L=L, s=s,
                          lpar=lpar, slope_h=slope_h):
                    _w(e, tE, ev[("smm", n, off)])
                    r0 = L - 640
                    lo = max(off, r0)
                    hi = off + nb
                    if lo < hi:
                        vE.inc(e.tensor_tensor(
                            out=psS[:, spar * 512 + (lo - off):spar * 512 + (hi - off)],
                            in0=psS[:, spar * 512 + (lo - off):spar * 512 + (hi - off)],
                            in1=mask_s[:, s * 640 + (lo - r0):s * 640 + (hi - r0)],
                            op=ALU.add))
                        sw(e, vE)
                    c_hb = float(slope_h * off)
                    vE.inc(e.scalar_tensor_tensor(
                        out=logits[:, lpar * 2048 + off:lpar * 2048 + off + nb],
                        in0=psS[:, spar * 512:spar * 512 + nb], scalar=c_hb,
                        in1=albf[:, (h % 2) * 512:(h % 2) * 512 + nb],
                        op0=ALU.add, op1=ALU.add))
                    pss_read[spar] = vE.v
                    ev[("evac", n)] = vE.v
                B.vector(f_dve)
            if s == 3:
                ev[("amm_last", h)] = tE.v

        def pt_group_secs(n):
            m = n - 3
            if not (0 <= m < NIT):
                return
            h, s = iters[m]
            sg = half * 4 + s
            nu = LS[sg]
            lpar = m % 3
            for g in range((nu + 3) // 4):
                gidx = gtg[0]
                gtg[0] += 1
                ev[("pt_gpar", m, g)] = gidx
                u0 = g * 4
                u1 = min(u0 + 4, nu)

                def f_te(e, m=m, g=g, gidx=gidx, u0=u0, u1=u1, lpar=lpar):
                    if g == 0:
                        _w(e, vE, ev[("pn", m)])
                    if ("ptev_l", gidx - 2) in ev:
                        _w(e, vE, ev[("ptev_l", gidx - 2)])
                    tpar = gidx % 2
                    for u in range(u0, u1):
                        tE.inc(e.transpose(
                            psT[:, tpar * 512 + (u - u0) * 128:tpar * 512 + (u - u0 + 1) * 128],
                            logits[:, lpar * 2048 + u * 128:lpar * 2048 + (u + 1) * 128],
                            idn[:]))
                    ev[("pt_tr", m, g)] = tE.v

                def f_dve(e, m=m, g=g, gidx=gidx, u0=u0, u1=u1):
                    _w(e, tE, ev[("pt_tr", m, g)])
                    tpar = gidx % 2
                    ppar = m % 2
                    w_ = (u1 - u0) * 128
                    vE.inc(e.tensor_copy(
                        out=pth[:, ppar * 2048 + u0 * 128:ppar * 2048 + u0 * 128 + w_],
                        in_=psT[:, tpar * 512:tpar * 512 + w_]))
                    ev[("ptev_h_g", m, g)] = vE.v
                    sw(e, vE)
                    vE.inc(e.tensor_tensor(
                        out=ptl[:, ppar * 2048 + u0 * 128:ppar * 2048 + u0 * 128 + w_],
                        in0=psT[:, tpar * 512:tpar * 512 + w_],
                        in1=pth[:, ppar * 2048 + u0 * 128:ppar * 2048 + u0 * 128 + w_],
                        op=ALU.subtract))
                    ev[("ptev_l_g", m, g)] = vE.v
                    ev[("ptev_l", gidx)] = vE.v
                B.tensor(f_te)
                B.vector(f_dve)

        def te_sec(n):
            def f(e):
                m = n - 4
                if 0 <= m < NIT:
                    h, s = iters[m]
                    sg = half * 4 + s
                    nu = LS[sg]
                    opar = m % 2
                    if ("o_lo", m - 2) in ev:
                        _w(e, vE, ev[("o_lo", m - 2)])
                    ppar = m % 2
                    for u in range(nu):
                        g = u // 4
                        _w(e, vE, ev[("ptev_l_g", m, g)])
                        vh_t = vth[:, u * 128:(u + 1) * 128]
                        vl_t = vtl[:, u * 128:(u + 1) * 128]
                        ph_t = pth[:, ppar * 2048 + u * 128:ppar * 2048 + (u + 1) * 128]
                        pl_t = ptl[:, ppar * 2048 + u * 128:ppar * 2048 + (u + 1) * 128]
                        dst = psO[:, opar * 512:opar * 512 + 128]
                        e.matmul(dst, vh_t, ph_t, start=(u == 0), stop=False)
                        e.matmul(dst, vh_t, pl_t, start=False, stop=False)
                        mmav = e.matmul(dst, vl_t, ph_t, start=False, stop=(u == nu - 1))
                    tE.inc(mmav)
                    ev[("av", m)] = tE.v
                if n < NIT:
                    h, s = iters[n]
                    if s == 2 and h + 1 < NH:
                        q_proj(e, h + 1, half)
            return f

        def act_sec(n):
            def f(e):
                m = n - 1
                if 0 <= m < NIT:
                    h, s = iters[m]
                    sg = half * 4 + s
                    L = LS[sg] * 128
                    _w(e, vE, ev[("max", m)])
                    if ("pn", m - 2) in ev:
                        _w(e, vE, ev[("pn", m - 2)])
                    lpar = m % 3
                    ppar = m % 2
                    aE.inc(e.activation(out=pbuf[:, ppar * 2048:ppar * 2048 + L],
                                        in_=logits[:, lpar * 2048:lpar * 2048 + L],
                                        func=ACTF.Exp, bias=negm[:, m % 2:m % 2 + 1],
                                        scale=1.0, accum_out=sums[:, m % 2:m % 2 + 1]))
                    ev[("exp", m)] = aE.v
            return f

        def dve_sec(n):
            def f(e):
                if n < NIT:
                    h, s = iters[n]
                    if s == 1 and h + 1 < NH:
                        wq_split(e, h + 1)
                    if s == 3 and h + 1 < NH:
                        q_evac(e, h + 1)
                    sg = half * 4 + s
                    L = LS[sg] * 128
                    lpar = n % 3
                    if ("exp", n - 2) in ev:
                        _w(e, aE, ev[("exp", n - 2)])
                    sw(e, vE)
                    vE.inc(e.reduce_max(out=negm[:, n % 2:n % 2 + 1],
                                        in_=logits[:, lpar * 2048:lpar * 2048 + L],
                                        axis=AX, negate=True))
                    ev[("max", n)] = vE.v
                m = n - 1
                if 0 <= m < NIT:
                    _w(e, aE, ev[("exp", m)])
                    vE.inc(e.reciprocal(out=rrec[:, m % 2:m % 2 + 1],
                                        in_=sums[:, m % 2:m % 2 + 1]))
                    ev[("recip", m)] = vE.v
                m = n - 2
                if 0 <= m < NIT:
                    h, s = iters[m]
                    sg = half * 4 + s
                    L = LS[sg] * 128
                    sw(e, vE)
                    lpar = m % 3
                    ppar = m % 2
                    vE.inc(e.tensor_scalar(out=logits[:, lpar * 2048:lpar * 2048 + L],
                                           in0=pbuf[:, ppar * 2048:ppar * 2048 + L],
                                           scalar1=rrec[:, m % 2:m % 2 + 1], scalar2=None,
                                           op0=ALU.mult))
                    ev[("pn", m)] = vE.v
                m = n - 4
                if 0 <= m < NIT:
                    h, s = iters[m]
                    _w(e, tE, ev[("av", m)])
                    if ("ospill", h - 1) in ev:
                        wde(e, ("ospill", h - 1))
                    opar = m % 2
                    vE.inc(e.tensor_copy(out=obh[:, s * 128:(s + 1) * 128],
                                         in_=psO[:, opar * 512:opar * 512 + 128]))
                    ev[("o_hi", m)] = vE.v
                    sw(e, vE)
                    vE.inc(e.tensor_tensor(out=obl[:, s * 128:(s + 1) * 128],
                                           in0=psO[:, opar * 512:opar * 512 + 128],
                                           in1=obh[:, s * 128:(s + 1) * 128],
                                           op=ALU.subtract))
                    ev[("o_lo", m)] = vE.v
            return f

        def sync_sec(n):
            def f(e):
                if n < NIT:
                    h, s = iters[n]
                    if s == 2 and h + 2 < NH:
                        _w(e, vE, ev[("wql", h + 1)])
                        wq_dma(e, h + 2)
                    if s == 0 and h + 1 < NH:
                        if h >= 1:
                            _w(e, vE, ev[("max", (h - 1) * 4 + 3)])
                        p_ = (h + 1) % 2
                        s_al[p_].dma(e.dma_start(
                            out=albf[:, p_ * 512:(p_ + 1) * 512], in_=alibi[h + 1]))
                        ev[("al", h + 1)] = (s_al[p_], s_al[p_].v)
            return f

        def gp_sec(n):
            def f(e):
                m = n - 4
                if 0 <= m < NIT:
                    h, s = iters[m]
                    if s == 3:
                        _w(e, vE, ev[("o_lo", m)])
                        s_osp.dma(e.dma_start(out=o_scr[0][h, :, half * HALF:(half + 1) * HALF],
                                              in_=obh[:]))
                        s_osp.dma(e.dma_start(out=o_scr[1][h, :, half * HALF:(half + 1) * HALF],
                                              in_=obl[:]))
                        ev[("ospill", h)] = (s_osp, s_osp.v)
            return f

        for n in range(NIT + 5):
            pt_group_secs(n)
            score_block_secs(n)
            B.tensor(te_sec(n))
            B.scalar(act_sec(n))
            B.vector(dve_sec(n))
            B.sync(sync_sec(n))
            B.gpsimd(gp_sec(n))
        barrier()

    if MAXPH >= 1:
        for half in range(2):
            p1_half(half)
            ev = {k: v for k, v in ev.items() if not isinstance(k, tuple)}

    ps1.close()
    es1.close()
    es1kv.close()

    esA = ExitStack()
    acth = esA.enter_context(nc.sbuf_tensor("acth", [128, NIC * HALF], BF16))
    actl = esA.enter_context(nc.sbuf_tensor("actl", [128, NIC * HALF], BF16))
    q_b = esA.enter_context(nc.sbuf_tensor("q_b", [128, TPC], F32))
    mu_b = esA.enter_context(nc.sbuf_tensor("mu_b", [128, HALF], F32))
    r_b = esA.enter_context(nc.sbuf_tensor("r_b", [128, HALF], F32))
    mu1r = esA.enter_context(nc.sbuf_tensor("mu1r", [1, TPC], F32))
    rs1r = esA.enter_context(nc.sbuf_tensor("rs1r", [1, TPC], F32))
    mu2r = esA.enter_context(nc.sbuf_tensor("mu2r", [1, TPC], F32))
    rs2r = esA.enter_context(nc.sbuf_tensor("rs2r", [1, TPC], F32))
    srow = esA.enter_context(nc.sbuf_tensor("srow", [1, HALF], F32))
    srow2 = esA.enter_context(nc.sbuf_tensor("srow2", [1, HALF], F32))
    rowh = esA.enter_context(nc.sbuf_tensor("rowh", [1, HALF], BF16))
    rowl = esA.enter_context(nc.sbuf_tensor("rowl", [1, HALF], BF16))
    ones_row = esA.enter_context(nc.sbuf_tensor("ones_row", [1, HALF], BF16))

    def f_dve_or(e):
        vE.inc(e.memset(ones_row[:], 1.0))
        ev["q_ones_v"] = vE.v
    B.vector(f_dve_or)

    es2 = ExitStack()
    woF = es2.enter_context(nc.sbuf_tensor("woF", [128, NIC * 128], F32))
    woh = es2.enter_context(nc.sbuf_tensor("woh", [128, 2 * NIC * 128], BF16))
    wol = es2.enter_context(nc.sbuf_tensor("wol", [128, 2 * NIC * 128], BF16))
    xoch = es2.enter_context(nc.sbuf_tensor("xoch", [128, 2 * HALF], BF16))
    xocl = es2.enter_context(nc.sbuf_tensor("xocl", [128, 2 * HALF], BF16))
    z1f = es2.enter_context(nc.sbuf_tensor("z1f", [128, 2 * HALF], F32))
    sqf = es2.enter_context(nc.sbuf_tensor("sqf", [128, 2 * HALF], F32))
    z1hb = es2.enter_context(nc.sbuf_tensor("z1hb", [128, 2 * HALF], BF16))
    z1lb = es2.enter_context(nc.sbuf_tensor("z1lb", [128, 2 * HALF], BF16))
    sqhb = es2.enter_context(nc.sbuf_tensor("sqhb", [128, 2 * HALF], BF16))
    sqlb = es2.enter_context(nc.sbuf_tensor("sqlb", [128, 2 * HALF], BF16))
    ps2 = ExitStack()
    psY = ps2.enter_context(nc.psum_tensor("psY", [128, 1024], F32))
    psZs = ps2.enter_context(nc.psum_tensor("psZs", [1, 512], F32))
    psQs = ps2.enter_context(nc.psum_tensor("psQs", [1, 512], F32))

    def wo_dma(e, oc):
        for dcg in range(4):
            src_ = wo[dcg * 1024:(dcg + 1) * 1024,
                      oc * 128:(oc + 1) * 128].rearrange("(ic p) o -> p ic o", p=128)
            dst = woF[:, dcg * 1024:(dcg + 1) * 1024].rearrange("p (ic c) -> p ic c", c=128)
            s_wo.dma(e.dma_start(out=dst, in_=src_))
        ev[("woq", oc)] = (s_wo, s_wo.v)

    def p2_half(half):
        def f_gp0(e):
            for dc in range(NIC):
                s_old.dma(e.dma_start(out=acth[:, dc * HALF:(dc + 1) * HALF],
                                      in_=o_scr[0][dc, :, half * HALF:(half + 1) * HALF]))
                s_old.dma(e.dma_start(out=actl[:, dc * HALF:(dc + 1) * HALF],
                                      in_=o_scr[1][dc, :, half * HALF:(half + 1) * HALF]))
            ev[("oload", half)] = (s_old, s_old.v)
            for oc in range(2):
                s_xoc[oc % 2].dma(e.dma_start(out=xoch[:, (oc % 2) * HALF:(oc % 2 + 1) * HALF],
                                              in_=xt_scr[0][oc, :, half * HALF:(half + 1) * HALF]))
                s_xoc[oc % 2].dma(e.dma_start(out=xocl[:, (oc % 2) * HALF:(oc % 2 + 1) * HALF],
                                              in_=xt_scr[1][oc, :, half * HALF:(half + 1) * HALF]))
                ev[("xoc", half, oc)] = (s_xoc[oc % 2], s_xoc[oc % 2].v)
        B.gpsimd(f_gp0)

        def f_sync0(e):
            wo_dma(e, 0)
        B.sync(f_sync0)

        def f_dve0(e):
            wde(e, ("woq", 0))
            vE.inc(e.tensor_copy(out=woh[:, 0:NIC * 128], in_=woF[:]))
            ev[("wosh", 0)] = vE.v
            sw(e, vE)
            vE.inc(e.tensor_tensor(out=wol[:, 0:NIC * 128], in0=woF[:],
                                   in1=woh[:, 0:NIC * 128], op=ALU.subtract))
            ev[("wosl", 0)] = vE.v
        B.vector(f_dve0)

        def f_sync1(e):
            _w(e, vE, ev[("wosl", 0)])
            wo_dma(e, 1)
        B.sync(f_sync1)

        for oc in range(NIC + 2):
            def f_te(e, oc=oc):
                if oc < NIC:
                    _w(e, vE, ev[("wosl", oc)])
                    if oc == 0:
                        wde(e, ("oload", half))
                    if ("z1f", half, oc - 2) in ev:
                        _w(e, vE, ev[("z1f", half, oc - 2)])
                    par = oc % 2
                    wb = par * NIC * 128
                    dst = psY[:, par * 512:(par + 1) * 512]
                    for dc in range(NIC):
                        wh = woh[:, wb + dc * 128:wb + (dc + 1) * 128]
                        wl = wol[:, wb + dc * 128:wb + (dc + 1) * 128]
                        ohd = acth[:, dc * HALF:(dc + 1) * HALF]
                        old = actl[:, dc * HALF:(dc + 1) * HALF]
                        e.matmul(dst, wh, ohd, start=(dc == 0), stop=False)
                        e.matmul(dst, wh, old, start=False, stop=False)
                        mmy = e.matmul(dst, wl, ohd, start=False, stop=(dc == NIC - 1))
                    tE.inc(mmy)
                    ev[("ymm", half, oc)] = tE.v
                m = oc - 2
                if 0 <= m < NIC:
                    par = m % 2
                    _w(e, vE, ev[("z1sq_l", half, m)])
                    st = m == 0
                    e.matmul(psZs[:], onesb[:, 0:1], z1hb[:, par * HALF:(par + 1) * HALF],
                             start=st, stop=False)
                    e.matmul(psZs[:], onesb[:, 0:1], z1lb[:, par * HALF:(par + 1) * HALF],
                             start=False, stop=(m == NIC - 1))
                    e.matmul(psQs[:], onesb[:, 0:1], sqhb[:, par * HALF:(par + 1) * HALF],
                             start=st, stop=False)
                    mmst = e.matmul(psQs[:], onesb[:, 0:1],
                                    sqlb[:, par * HALF:(par + 1) * HALF],
                                    start=False, stop=(m == NIC - 1))
                    tE.inc(mmst)
                    ev[("stats", half, m)] = tE.v

            def f_dve(e, oc=oc):
                m = oc + 1
                if m < NIC:
                    wde(e, ("woq", m))
                    if ("ymm", half, m - 2) in ev:
                        _w(e, tE, ev[("ymm", half, m - 2)])
                    wb = (m % 2) * NIC * 128
                    vE.inc(e.tensor_copy(out=woh[:, wb:wb + NIC * 128], in_=woF[:]))
                    sw(e, vE)
                    vE.inc(e.tensor_tensor(out=wol[:, wb:wb + NIC * 128], in0=woF[:],
                                           in1=woh[:, wb:wb + NIC * 128],
                                           op=ALU.subtract))
                    ev[("wosh", m)] = vE.v
                    ev[("wosl", m)] = vE.v
                if oc < NIC:
                    _w(e, tE, ev[("ymm", half, oc)])
                    wde(e, ("xoc", half, oc))
                    if ("z1sq_h", half, oc - 2) in ev:
                        _w(e, vE, ev[("z1sq_h", half, oc - 2)])
                    sw(e, vE)
                    par = oc % 2
                    z1s = z1f[:, par * HALF:(par + 1) * HALF]
                    vE.inc(e.tensor_tensor(out=z1s, in0=psY[:, par * 512:(par + 1) * 512],
                                           in1=xoch[:, par * HALF:(par + 1) * HALF],
                                           op=ALU.add))
                    sw(e, vE)
                    vE.inc(e.tensor_tensor(out=z1s, in0=z1s,
                                           in1=xocl[:, par * HALF:(par + 1) * HALF],
                                           op=ALU.add))
                    ev[("z1f", half, oc)] = vE.v
                m = oc - 1
                if 0 <= m < NIC:
                    par = m % 2
                    if ("stats", half, m - 2) in ev:
                        _w(e, tE, ev[("stats", half, m - 2)])
                    if ("z1sp", half, m - 2) in ev:
                        wde(e, ("z1sp", half, m - 2))
                    sw(e, vE)
                    z1v = z1f[:, par * HALF:(par + 1) * HALF]
                    vE.inc(e.tensor_copy(out=z1hb[:, par * HALF:(par + 1) * HALF], in_=z1v))
                    vE.inc(e.tensor_tensor(out=sqf[:, par * HALF:(par + 1) * HALF],
                                           in0=z1v, in1=z1v, op=ALU.mult))
                    ev[("z1h", half, m)] = vE.v
                    sw(e, vE)
                    vE.inc(e.tensor_tensor(out=z1lb[:, par * HALF:(par + 1) * HALF],
                                           in0=z1v,
                                           in1=z1hb[:, par * HALF:(par + 1) * HALF],
                                           op=ALU.subtract))
                    vE.inc(e.tensor_copy(out=sqhb[:, par * HALF:(par + 1) * HALF],
                                         in_=sqf[:, par * HALF:(par + 1) * HALF]))
                    ev[("z1l", half, m)] = vE.v
                    sw(e, vE)
                    vE.inc(e.tensor_tensor(out=sqlb[:, par * HALF:(par + 1) * HALF],
                                           in0=sqf[:, par * HALF:(par + 1) * HALF],
                                           in1=sqhb[:, par * HALF:(par + 1) * HALF],
                                           op=ALU.subtract))
                    ev[("z1sq_h", half, m)] = vE.v
                    ev[("z1sq_l", half, m)] = vE.v

            def f_sync(e, oc=oc):
                m = oc + 2
                if m < NIC:
                    _w(e, vE, ev[("wosl", m - 1)])
                    wo_dma(e, m)

            def f_gp(e, oc=oc):
                m = oc + 2
                if m < NIC:
                    if ("z1f", half, m - 2) in ev:
                        _w(e, vE, ev[("z1f", half, m - 2)])
                    s_xoc[m % 2].dma(e.dma_start(out=xoch[:, (m % 2) * HALF:(m % 2 + 1) * HALF],
                                                 in_=xt_scr[0][m, :, half * HALF:(half + 1) * HALF]))
                    s_xoc[m % 2].dma(e.dma_start(out=xocl[:, (m % 2) * HALF:(m % 2 + 1) * HALF],
                                                 in_=xt_scr[1][m, :, half * HALF:(half + 1) * HALF]))
                    ev[("xoc", half, m)] = (s_xoc[m % 2], s_xoc[m % 2].v)
                m = oc - 1
                if 0 <= m < NIC:
                    _w(e, vE, ev[("z1sq_l", half, m)])
                    par = m % 2
                    s_z1sp[par].dma(e.dma_start(out=z1_scr[0][m, :, half * HALF:(half + 1) * HALF],
                                                in_=z1hb[:, par * HALF:(par + 1) * HALF]))
                    s_z1sp[par].dma(e.dma_start(out=z1_scr[1][m, :, half * HALF:(half + 1) * HALF],
                                                in_=z1lb[:, par * HALF:(par + 1) * HALF]))
                    ev[("z1sp", half, m)] = (s_z1sp[par], s_z1sp[par].v)
            B.tensor(f_te)
            B.vector(f_dve)
            B.sync(f_sync)
            B.gpsimd(f_gp)

        def f_dve_fin(e):
            _w(e, tE, ev[("stats", half, NIC - 1)])
            vE.inc(e.tensor_scalar(out=mu1r[0:1, half * HALF:(half + 1) * HALF], in0=psZs[:],
                                   scalar1=1.0 / D, scalar2=None, op0=ALU.mult))
            vE.inc(e.tensor_scalar(out=srow[:], in0=psQs[:], scalar1=1.0 / D, scalar2=None,
                                   op0=ALU.mult))
            mu = mu1r[0:1, half * HALF:(half + 1) * HALF]
            sw(e, vE)
            vE.inc(e.tensor_tensor(out=srow2[:], in0=mu, in1=mu, op=ALU.mult))
            sw(e, vE)
            vE.inc(e.tensor_tensor(out=srow[:], in0=srow[:], in1=srow2[:], op=ALU.subtract))
            sw(e, vE)
            vE.inc(e.tensor_scalar(out=srow[:], in0=srow[:], scalar1=1e-5, scalar2=None,
                                   op0=ALU.add))
            ev[("ln1_var", half)] = vE.v
        B.vector(f_dve_fin)

        def f_act_fin(e):
            _w(e, vE, ev[("ln1_var", half)])
            aE.inc(e.activation(out=srow[:], in_=srow[:], func=ACTF.Sqrt, bias=0.0, scale=1.0))
            ev[("ln1_sqrt", half)] = aE.v
        B.scalar(f_act_fin)

        def f_dve_fin2(e):
            _w(e, aE, ev[("ln1_sqrt", half)])
            vE.inc(e.reciprocal(out=rs1r[0:1, half * HALF:(half + 1) * HALF], in_=srow[:]))
        B.vector(f_dve_fin2)
        barrier()

    if MAXPH >= 2:
        for half in range(2):
            p2_half(half)
            ev = {k: v for k, v in ev.items() if not isinstance(k, tuple)}

    ps2.close()
    es2.close()

    es3 = ExitStack()
    zrh = es3.enter_context(nc.sbuf_tensor("zrh", [128, 2 * HALF], BF16))
    zrl = es3.enter_context(nc.sbuf_tensor("zrl", [128, 2 * HALF], BF16))
    x1tmp = es3.enter_context(nc.sbuf_tensor("x1tmp", [128, 2 * HALF], F32))
    es3b = ExitStack()
    dotk = es3b.enter_context(nc.sbuf_tensor("dotk", [128, 256], F32))
    psD_sb = es3b.enter_context(nc.sbuf_tensor("psD_sb", [64, 512], F32))
    ebuf = es3b.enter_context(nc.sbuf_tensor("ebuf", [128, 64], F32))
    ebufb = es3b.enter_context(nc.sbuf_tensor("ebufb", [128, 64], BF16))
    ebuf_oh = es3b.enter_context(nc.sbuf_tensor("ebuf_oh", [64, 128], F32))
    ohb = es3b.enter_context(nc.sbuf_tensor("ohb", [64, 128], BF16))
    dnegm = es3b.enter_context(nc.sbuf_tensor("dnegm", [128, 1], F32))
    dsum = es3b.enter_context(nc.sbuf_tensor("dsum", [128, 1], F32))
    dthr = es3b.enter_context(nc.sbuf_tensor("dthr", [128, 1], F32))
    q01 = es3b.enter_context(nc.sbuf_tensor("q01", [128, 1], F32))
    centf = es3b.enter_context(nc.sbuf_tensor("centf", [64, D], F32))
    cth = es3b.enter_context(nc.sbuf_tensor("cth", [128, NIC * 64], BF16))
    ctl = es3b.enter_context(nc.sbuf_tensor("ctl", [128, NIC * 64], BF16))
    cnt_s = es3b.enter_context(nc.sbuf_tensor("cnt_s", [64, 1], F32))
    glob_s = es3b.enter_context(nc.sbuf_tensor("glob_s", [64, 1], F32))
    arowb = es3b.enter_context(nc.sbuf_tensor("arowb", [1, 128], BF16))
    ar_i = es3b.enter_context(nc.sbuf_tensor("ar_i", [1, D], I32))
    ar_f = es3b.enter_context(nc.sbuf_tensor("ar_f", [1, D], F32))
    qrow = es3b.enter_context(nc.sbuf_tensor("qrow", [1, TPC], F32))
    qrowb = es3b.enter_context(nc.sbuf_tensor("qrowb", [1, TPC], BF16))
    ps3 = ExitStack()
    psD = ps3.enter_context(nc.psum_tensor("psD", [64, 512], F32))
    psT3 = ps3.enter_context(nc.psum_tensor("psT3", [128, 1024], F32))
    psC = ps3.enter_context(nc.psum_tensor("psC", [64, 512], F32))
    psB3 = ps3.enter_context(nc.psum_tensor("psB3", [128, 512], F32))
    psR3 = ps3.enter_context(nc.psum_tensor("psR3", [128, 512], F32))

    def f_sync_c(e):
        if MAXPH < 3:
            return
        s_cst.dma(e.dma_start(out=centf[:], in_=cent))
        s_cst.dma(e.dma_start(out=ar_i[:], in_=assign_row))
        ev["p3_loads"] = (s_cst, s_cst.v)
    B.sync(f_sync_c)

    def f_dve_ar(e):
        if MAXPH < 3:
            return
        wde(e, "p3_loads")
        vE.inc(e.tensor_copy(out=ar_f[:], in_=ar_i[:]))
    B.vector(f_dve_ar)

    if MAXPH >= 3:
        tsp_rounds("ct", "c", NIC,
               (lambda i: centf[:, i * 128:(i + 1) * 128]),
               (lambda i: cth[:, i * 64:(i + 1) * 64]),
               (lambda i: ctl[:, i * 64:(i + 1) * 64]),
               psT3, 4, 64,
                   pre_wait=(lambda e: wde(e, "p3_loads")),
                   scale=10.0, ident_ap=idn[0:64, 0:64])

    def bcast_rows(tag, srcrow, dst, half, psb):
        def f_dve(e):
            _w(e, tE, tE.v)
            _w(e, aE, aE.v)
            sw(e, vE)
            vE.inc(e.tensor_copy(out=rowh[:],
                                 in_=srcrow[0:1, half * HALF:(half + 1) * HALF]))
            ev[(tag + "_rh", half)] = vE.v
            sw(e, vE)
            vE.inc(e.scalar_tensor_tensor(out=rowl[:],
                                          in0=srcrow[0:1, half * HALF:(half + 1) * HALF],
                                          scalar=1.0, in1=rowh[:], op0=ALU.mult,
                                          op1=ALU.subtract))
            ev[(tag + "_rl", half)] = vE.v
        def f_te(e):
            _w(e, vE, ev[(tag + "_rl", half)])
            e.matmul(psb[:], onesb[0:1, :], rowh[:], start=True, stop=False)
            mmb = e.matmul(psb[:], onesb[0:1, :], rowl[:], start=False, stop=True)
            tE.inc(mmb)
            ev[(tag + "_mm", half)] = tE.v
        def f_dve2(e):
            _w(e, tE, ev[(tag + "_mm", half)])
            vE.inc(e.tensor_copy(out=dst[:], in_=psb[:]))
            ev[(tag + "_bc", half)] = vE.v
        B.vector(f_dve)
        B.tensor(f_te)
        B.vector(f_dve2)

    def x1_build(half, with_dots):
        for ic in range(NIC + 2):
            def f_gp(e, ic=ic):
                if ic < NIC:
                    if ("x1f", half, ic - 2) in ev:
                        _w(e, vE, ev[("x1f", half, ic - 2)])
                    par = ic % 2
                    s_zr[par].dma(e.dma_start(out=zrh[:, par * HALF:(par + 1) * HALF],
                                              in_=z1_scr[0][ic, :, half * HALF:(half + 1) * HALF]))
                    s_zr[par].dma(e.dma_start(out=zrl[:, par * HALF:(par + 1) * HALF],
                                              in_=z1_scr[1][ic, :, half * HALF:(half + 1) * HALF]))
                    ev[("zr", half, ic)] = (s_zr[par], s_zr[par].v)
            def f_dve(e, ic=ic):
                m = ic - 1
                if 0 <= m < NIC:
                    if m == 0:
                        _w(e, vE, ev[("ln1mu_bc", half)])
                        _w(e, vE, ev[("ln1rs_bc", half)])
                    par = m % 2
                    wde(e, ("zr", half, m))
                    if ("x1h", half, m - 2) in ev:
                        _w(e, vE, ev[("x1h", half, m - 2)])
                    sw(e, vE)
                    x1s = x1tmp[:, par * HALF:(par + 1) * HALF]
                    vE.inc(e.tensor_tensor(out=x1s, in0=zrh[:, par * HALF:(par + 1) * HALF],
                                           in1=zrl[:, par * HALF:(par + 1) * HALF], op=ALU.add))
                    sw(e, vE)
                    vE.inc(e.tensor_tensor(out=x1s, in0=x1s, in1=mu_b[:], op=ALU.subtract))
                    sw(e, vE)
                    vE.inc(e.tensor_tensor(out=x1s, in0=x1s, in1=r_b[:], op=ALU.mult))
                    ev[("x1f", half, m)] = vE.v
            def f_dve2(e, ic=ic):
                m = ic - 1
                if 0 <= m < NIC:
                    par = m % 2
                    sw(e, vE)
                    vE.inc(e.tensor_copy(out=acth[:, m * HALF:(m + 1) * HALF],
                                         in_=x1tmp[:, par * HALF:(par + 1) * HALF]))
                    ev[("x1h", half, m)] = vE.v
                    sw(e, vE)
                    vE.inc(e.tensor_tensor(out=actl[:, m * HALF:(m + 1) * HALF],
                                           in0=x1tmp[:, par * HALF:(par + 1) * HALF],
                                           in1=acth[:, m * HALF:(m + 1) * HALF],
                                           op=ALU.subtract))
                    ev[("x1l", half, m)] = vE.v
            def f_te(e, ic=ic):
                if not with_dots:
                    return
                m = ic - 1
                if 0 <= m < NIC:
                    if m == 0 and ("ct_hi", "c") in ev:
                        _w(e, vE, ev[("ct_lo", "c")])
                    _w(e, vE, ev[("x1l", half, m)])
                    ch = cth[:, m * 64:(m + 1) * 64]
                    cl = ctl[:, m * 64:(m + 1) * 64]
                    xh = acth[:, m * HALF:(m + 1) * HALF]
                    xl = actl[:, m * HALF:(m + 1) * HALF]
                    e.matmul(psD[:], ch, xh, start=(m == 0), stop=False)
                    e.matmul(psD[:], ch, xl, start=False, stop=False)
                    mmd = e.matmul(psD[:], cl, xh, start=False, stop=(m == NIC - 1))
                    tE.inc(mmd)
                    ev[("dmm", half, m)] = tE.v
            B.gpsimd(f_gp)
            B.vector(f_dve)
            B.vector(f_dve2)
            B.tensor(f_te)

    def p3_half(half):
        bcast_rows("ln1mu", mu1r, mu_b, half, psB3)
        bcast_rows("ln1rs", rs1r, r_b, half, psB3)
        x1_build(half, with_dots=True)

        def f_dve_cp(e):
            _w(e, tE, ev[("dmm", half, NIC - 1)])
            vE.inc(e.tensor_copy(out=psD_sb[:], in_=psD[:]))
            ev[("dot_sb", half)] = vE.v
        B.vector(f_dve_cp)

        for tt in range(4):
            def f_te(e, tt=tt):
                _w(e, vE, ev[("dot_sb", half)])
                if tt > 0:
                    _w(e, vE, ev[("dot_ev", half, tt - 1)])
                    _w(e, vE, ev[("q_ev", half, tt - 1)])
                tE.inc(e.transpose(psT3[:, 0:64], psD_sb[:, tt * 128:(tt + 1) * 128],
                                   idn[0:64, 0:64]))
                ev[("dot_tr", half, tt)] = tE.v
            B.tensor(f_te)

            def f_dve(e, tt=tt):
                _w(e, tE, ev[("dot_tr", half, tt)])
                vE.inc(e.tensor_copy(out=dotk[:, tt * 64:(tt + 1) * 64], in_=psT3[:, 0:64]))
                ev[("dot_ev", half, tt)] = vE.v
                sw(e, vE)
                if ("dot_exp", half, tt - 1) in ev:
                    _w(e, aE, ev[("dot_exp", half, tt - 1)])
                vE.inc(e.reduce_max(out=dnegm[:], in_=dotk[:, tt * 64:(tt + 1) * 64],
                                    axis=AX, negate=True))
                ev[("dot_max", half, tt)] = vE.v
            B.vector(f_dve)

            def f_act(e, tt=tt):
                _w(e, vE, ev[("dot_max", half, tt)])
                aE.inc(e.activation(out=ebuf[:], in_=dotk[:, tt * 64:(tt + 1) * 64],
                                    func=ACTF.Exp, bias=dnegm[:], scale=1.0, accum_out=dsum[:]))
                ev[("dot_exp", half, tt)] = aE.v
            B.scalar(f_act)

            def f_dve2(e, tt=tt):
                _w(e, aE, ev[("dot_exp", half, tt)])
                vE.inc(e.tensor_scalar(out=dthr[:], in0=dsum[:], scalar1=0.01, scalar2=None,
                                       op0=ALU.mult))
                sw(e, vE)
                vE.inc(e.tensor_scalar(out=ebuf[:], in0=ebuf[:], scalar1=dthr[:], scalar2=None,
                                       op0=ALU.is_gt))
                sw(e, vE)
                vE.inc(e.reduce_max(out=q01[:], in_=ebuf[:], axis=AX))
                ev[("dot_hits", half, tt)] = vE.v
            B.vector(f_dve2)

            def f_dve2b(e, tt=tt):
                sw(e, vE)
                vE.inc(e.tensor_copy(out=ebufb[:], in_=ebuf[:]))
                ev[("dot_hb", half, tt)] = vE.v
            B.vector(f_dve2b)

            def f_te2(e, tt=tt):
                _w(e, vE, ev[("dot_hb", half, tt)])
                st = (half == 0 and tt == 0)
                sp = (half == 1 and tt == 3)
                mmc = e.matmul(psC[:, 0:1], ebufb[:], onesb[:, 0:1], start=st, stop=sp)
                tE.inc(mmc)
                ev[("cnt", half, tt)] = tE.v
                _w(e, vE, ev[("dot_hits", half, tt)])
                _w(e, vE, ev[("dot_ev", half, tt)])
                tE.inc(e.transpose(psT3[0:1, 384:512], q01[:], idn[:]))
                ev[("q_tr", half, tt)] = tE.v
            B.tensor(f_te2)

            def f_dve3(e, tt=tt):
                _w(e, tE, ev[("q_tr", half, tt)])
                vE.inc(e.tensor_copy(
                    out=qrow[0:1, half * HALF + tt * 128:half * HALF + (tt + 1) * 128],
                    in_=psT3[0:1, 384:512]))
                ev[("q_ev", half, tt)] = vE.v
            B.vector(f_dve3)

    if MAXPH >= 3:
        p3_half(0)
        barrier()
        p3_half(1)
    barrier()

    def f_dve_cnt(e):
        if MAXPH < 3:
            return
        vE.inc(e.tensor_copy(out=cnt_s[:], in_=psC[:, 0:1]))
        ev["cnt_sb"] = vE.v
    B.vector(f_dve_cnt)

    def f_gp_cc(e):
        if MAXPH < 3:
            return
        _w(e, vE, ev["cnt_sb"])
        s_cc.dma(e.dma_start(out=cc_in.ap(), in_=cnt_s[:]))
        e.wait_ge(s_cc.h, s_cc.v)
        cS.inc(e.collective_compute(
            "AllReduce", ALU.add, replica_groups=[[0, 1, 2, 3, 4, 5, 6, 7]],
            ins=[cc_in.ap().opt()], outs=[cc_out.ap().opt()]))
        e.wait_ge(cS.h, cS.v)
        s_cc.dma(e.dma_start(out=glob_s[:], in_=cc_out.ap()))
        ev["cc_done"] = (s_cc, s_cc.v)
    B.gpsimd(f_gp_cc)

    def f_dve_cs(e):
        if MAXPH < 3:
            return
        wde(e, "cc_done")
        vE.inc(e.tensor_scalar(out=glob_s[:], in0=glob_s[:], scalar1=0.5, scalar2=None,
                               op0=ALU.is_ge))
        ev["cs01"] = vE.v
    B.vector(f_dve_cs)

    def f_dve_csb(e):
        if MAXPH < 3:
            return
        sw(e, vE)
        vE.inc(e.tensor_copy(out=csb[:], in_=glob_s[:]))
        ev["csb"] = vE.v
    B.vector(f_dve_csb)

    for oc in range(NIC if MAXPH >= 3 else 0):
        def f_dve0r(e, oc=oc):
            if oc == 0:
                _w(e, vE, ev["csb"])
            if ("abc", oc - 1) in ev:
                _w(e, tE, ev[("abc", oc - 1)])
            sw(e, vE)
            vE.inc(e.tensor_copy(out=arowb[:], in_=ar_f[0:1, oc * 128:(oc + 1) * 128]))
            ev[("arowb", oc)] = vE.v
        def f_te(e, oc=oc):
            _w(e, vE, ev[("arowb", oc)])
            if ("abc01", oc - 1) in ev:
                _w(e, vE, ev[("abc01", oc - 1)])
            mma = e.matmul(psB3[0:64, 0:128], onesb[0:1, 0:64], arowb[:],
                           start=True, stop=True)
            tE.inc(mma)
            ev[("abc", oc)] = tE.v
        def f_dve(e, oc=oc):
            _w(e, tE, ev[("abc", oc)])
            vE.inc(e.tensor_scalar(out=ebuf_oh[:], in0=psB3[0:64, 0:128], scalar1=iota_s[:],
                                   scalar2=None, op0=ALU.is_equal))
            ev[("abc01", oc)] = vE.v
        def f_dve1r(e, oc=oc):
            _w(e, vE, ev[("abc01", oc)])
            if ("rsmm", oc - 1) in ev:
                _w(e, tE, ev[("rsmm", oc - 1)])
            sw(e, vE)
            vE.inc(e.tensor_copy(out=ohb[:], in_=ebuf_oh[:]))
            ev[("ohb", oc)] = vE.v
        def f_te2(e, oc=oc):
            _w(e, vE, ev[("ohb", oc)])
            _w(e, vE, ev["csb"])
            if ("rs", oc - 1) in ev:
                _w(e, vE, ev[("rs", oc - 1)])
            mmr = e.matmul(psR3[:, 0:1], ohb[:], csb[:], start=True, stop=True)
            tE.inc(mmr)
            ev[("rsmm", oc)] = tE.v
        def f_dve2(e, oc=oc):
            _w(e, tE, ev[("rsmm", oc)])
            vE.inc(e.tensor_copy(out=rs_s[:, oc:oc + 1], in_=psR3[:, 0:1]))
            ev[("rs", oc)] = vE.v
        B.vector(f_dve0r)
        B.tensor(f_te)
        B.vector(f_dve)
        B.vector(f_dve1r)
        B.tensor(f_te2)
        B.vector(f_dve2)

    for half in range(2 if MAXPH >= 3 else 0):
        def f_dveq(e, half=half):
            _w(e, vE, ev[("q_ev", half, 3)] if ("q_ev", half, 3) in ev else 0)
            sw(e, vE)
            vE.inc(e.tensor_copy(out=qrowb[0:1, half * HALF:(half + 1) * HALF],
                                 in_=qrow[0:1, half * HALF:(half + 1) * HALF]))
            ev[("qrb", half)] = vE.v
        def f_te(e, half=half):
            _w(e, vE, ev[("qrb", half)])
            mmq3 = e.matmul(psB3[:], onesb[0:1, :],
                            qrowb[0:1, half * HALF:(half + 1) * HALF],
                            start=True, stop=True)
            tE.inc(mmq3)
            ev[("qmm3", half)] = tE.v
        def f_dve(e, half=half):
            _w(e, tE, ev[("qmm3", half)])
            vE.inc(e.tensor_copy(out=q_b[:, half * HALF:(half + 1) * HALF], in_=psB3[:]))
            ev[("qbc", half)] = vE.v
        B.vector(f_dveq)
        B.tensor(f_te)
        B.vector(f_dve)
    barrier()
    ps3.close()
    es3b.close()

    es4 = ExitStack()
    wfF = es4.enter_context(nc.sbuf_tensor("wfF", [128, D], F32))
    biasf4 = es4.enter_context(nc.sbuf_tensor("biasf4", [1, D], F32))
    biash = es4.enter_context(nc.sbuf_tensor("biash", [1, D], BF16))
    biasl = es4.enter_context(nc.sbuf_tensor("biasl", [1, D], BF16))
    wth = es4.enter_context(nc.sbuf_tensor("wth", [128, 2 * NIC * 128], BF16))
    wtl = es4.enter_context(nc.sbuf_tensor("wtl", [128, 2 * NIC * 128], BF16))
    gsf = es4.enter_context(nc.sbuf_tensor("gsf", [128, 2 * HALF], F32))
    z2ff = es4.enter_context(nc.sbuf_tensor("z2ff", [128, 2 * HALF], F32))
    z2hb = es4.enter_context(nc.sbuf_tensor("z2hb", [128, 2 * HALF], BF16))
    z2lb = es4.enter_context(nc.sbuf_tensor("z2lb", [128, 2 * HALF], BF16))
    sq2f = es4.enter_context(nc.sbuf_tensor("sq2f", [128, 2 * HALF], F32))
    sq2hb = es4.enter_context(nc.sbuf_tensor("sq2hb", [128, 2 * HALF], BF16))
    sq2lb = es4.enter_context(nc.sbuf_tensor("sq2lb", [128, 2 * HALF], BF16))
    ps4 = ExitStack()
    psF = ps4.enter_context(nc.psum_tensor("psF", [128, 1024], F32))
    psW = ps4.enter_context(nc.psum_tensor("psW", [128, 1024], F32))
    psZs4 = ps4.enter_context(nc.psum_tensor("psZs4", [1, 512], F32))
    psQs4 = ps4.enter_context(nc.psum_tensor("psQs4", [1, 512], F32))
    psB4 = ps4.enter_context(nc.psum_tensor("psB4", [128, 512], F32))

    def f_sync_b4(e):
        if MAXPH < 4:
            return
        s_cst.dma(e.dma_start(out=biasf4[:], in_=biasr))
        ev["p4_bias"] = (s_cst, s_cst.v)
    B.sync(f_sync_b4)

    def f_dve_b4(e):
        if MAXPH < 4:
            return
        wde(e, "p4_bias")
        vE.inc(e.tensor_copy(out=biash[:], in_=biasf4[:]))
        sw(e, vE)
        vE.inc(e.tensor_tensor(out=biasl[:], in0=biasf4[:], in1=biash[:], op=ALU.subtract))
        ev["q_ones"] = vE.v
    B.vector(f_dve_b4)

    def p4_half(half):
        bcast_rows("ln1mu", mu1r, mu_b, half, psB4)
        bcast_rows("ln1rs", rs1r, r_b, half, psB4)
        x1_build(half, with_dots=False)

        def f_sync0(e):
            s_wf.dma(e.dma_start(out=wfF[:], in_=wf[0:128, :]))
            ev[("wfq", half, 0)] = (s_wf, s_wf.v)
        B.sync(f_sync0)

        def wt_group_secs(oc):
            if oc >= NIC:
                return
            for g in range(8):
                def f_te(e, oc=oc, g=g):
                    if g == 0:
                        wde(e, ("wfq", half, oc))
                    if g == 0 and oc == 0:
                        _w(e, vE, ev[("x1l", half, NIC - 1)])
                    if g >= 2:
                        _w(e, vE, ev[("wt_l", half, oc, g - 2)])
                    elif oc > 0:
                        _w(e, vE, ev[("wt_l", half, oc - 1, 6 + g)])
                    tpar = g % 2
                    for u in range(4):
                        ic = g * 4 + u
                        tE.inc(e.transpose(
                            psW[:, tpar * 512 + u * 128:tpar * 512 + (u + 1) * 128],
                            wfF[:, ic * 128:(ic + 1) * 128], idn[:]))
                    ev[("wt_tr", half, oc, g)] = tE.v

                def f_dve(e, oc=oc, g=g):
                    _w(e, tE, ev[("wt_tr", half, oc, g)])
                    if ("fmm", half, oc - 2) in ev:
                        _w(e, tE, ev[("fmm", half, oc - 2)])
                    tpar = g % 2
                    wb = (oc % 2) * NIC * 128
                    vE.inc(e.tensor_copy(out=wth[:, wb + g * 512:wb + (g + 1) * 512],
                                         in_=psW[:, tpar * 512:(tpar + 1) * 512]))
                    sw(e, vE)
                    vE.inc(e.tensor_tensor(out=wtl[:, wb + g * 512:wb + (g + 1) * 512],
                                           in0=psW[:, tpar * 512:(tpar + 1) * 512],
                                           in1=wth[:, wb + g * 512:wb + (g + 1) * 512],
                                           op=ALU.subtract))
                    ev[("wt_l", half, oc, g)] = vE.v
                B.tensor(f_te)
                B.vector(f_dve)

        for oc in range(NIC + 3):
            def f_te(e, oc=oc):
                m = oc - 1
                if 0 <= m < NIC:
                    _w(e, vE, ev[("wt_l", half, m, 7)])
                    _w(e, vE, ev["q_ones"])
                    _w(e, vE, ev["q_ones_v"])
                    if ("gs", half, m - 2) in ev:
                        _w(e, vE, ev[("gs", half, m - 2)])
                    par = m % 2
                    wb = par * NIC * 128
                    dst = psF[:, par * 512:(par + 1) * 512]
                    for ic in range(NIC):
                        wh = wth[:, wb + ic * 128:wb + (ic + 1) * 128]
                        wl = wtl[:, wb + ic * 128:wb + (ic + 1) * 128]
                        xh = acth[:, ic * HALF:(ic + 1) * HALF]
                        xl = actl[:, ic * HALF:(ic + 1) * HALF]
                        e.matmul(dst, wh, xh, start=(ic == 0), stop=False)
                        e.matmul(dst, wh, xl, start=False, stop=False)
                        e.matmul(dst, wl, xh, start=False, stop=False)
                    e.matmul(dst, biash[0:1, m * 128:(m + 1) * 128], ones_row[:],
                             start=False, stop=False)
                    mmf = e.matmul(dst, biasl[0:1, m * 128:(m + 1) * 128], ones_row[:],
                                   start=False, stop=True)
                    tE.inc(mmf)
                    ev[("fmm", half, m)] = tE.v
                m = oc - 3
                if 0 <= m < NIC:
                    par = m % 2
                    _w(e, vE, ev[("z2sq_l", half, m)])
                    st = m == 0
                    e.matmul(psZs4[:], onesb[:, 0:1], z2hb[:, par * HALF:(par + 1) * HALF],
                             start=st, stop=False)
                    e.matmul(psZs4[:], onesb[:, 0:1], z2lb[:, par * HALF:(par + 1) * HALF],
                             start=False, stop=(m == NIC - 1))
                    e.matmul(psQs4[:], onesb[:, 0:1], sq2hb[:, par * HALF:(par + 1) * HALF],
                             start=st, stop=False)
                    mmst4 = e.matmul(psQs4[:], onesb[:, 0:1],
                                     sq2lb[:, par * HALF:(par + 1) * HALF],
                                     start=False, stop=(m == NIC - 1))
                    tE.inc(mmst4)
                    ev[("stats4", half, m)] = tE.v

            def f_dve(e, oc=oc):
                m = oc - 1
                if 0 <= m < NIC:
                    _w(e, tE, ev[("fmm", half, m)])
                    _w(e, vE, ev[("rs", m)])
                    _w(e, vE, ev[("qbc", half)])
                    par = m % 2
                    gs = gsf[:, par * HALF:(par + 1) * HALF]
                    vE.inc(e.scalar_tensor_tensor(out=gs, in0=psF[:, par * 512:(par + 1) * 512],
                                                  scalar=rs_s[:, m:m + 1],
                                                  in1=q_b[:, half * HALF:(half + 1) * HALF],
                                                  op0=ALU.mult, op1=ALU.mult))
                    ev[("gs", half, m)] = vE.v
                    if ("z2sq_l", half, m - 2) in ev:
                        _w(e, vE, ev[("z2sq_l", half, m - 2)])
                    z2s = z2ff[:, par * HALF:(par + 1) * HALF]
                    sw(e, vE)
                    vE.inc(e.tensor_tensor(out=z2s, in0=gs,
                                           in1=acth[:, m * HALF:(m + 1) * HALF], op=ALU.add))
                    sw(e, vE)
                    vE.inc(e.tensor_tensor(out=z2s, in0=z2s,
                                           in1=actl[:, m * HALF:(m + 1) * HALF], op=ALU.add))
                    ev[("z2f", half, m)] = vE.v
                m = oc - 2
                if 0 <= m < NIC:
                    par = m % 2
                    _w(e, vE, ev[("z2f", half, m)])
                    if ("stats4", half, m - 2) in ev:
                        _w(e, tE, ev[("stats4", half, m - 2)])
                    if ("z2sp", half, m - 2) in ev:
                        wde(e, ("z2sp", half, m - 2))
                    vE.inc(e.tensor_copy(out=z2hb[:, par * HALF:(par + 1) * HALF],
                                         in_=z2ff[:, par * HALF:(par + 1) * HALF]))
                    sw(e, vE)
                    vE.inc(e.tensor_tensor(out=z2lb[:, par * HALF:(par + 1) * HALF],
                                           in0=z2ff[:, par * HALF:(par + 1) * HALF],
                                           in1=z2hb[:, par * HALF:(par + 1) * HALF],
                                           op=ALU.subtract))
                    ev[("z2l", half, m)] = vE.v
                    vE.inc(e.tensor_tensor(out=sq2f[:, par * HALF:(par + 1) * HALF],
                                           in0=z2ff[:, par * HALF:(par + 1) * HALF],
                                           in1=z2ff[:, par * HALF:(par + 1) * HALF],
                                           op=ALU.mult))
                    sw(e, vE)
                    vE.inc(e.tensor_copy(out=sq2hb[:, par * HALF:(par + 1) * HALF],
                                         in_=sq2f[:, par * HALF:(par + 1) * HALF]))
                    sw(e, vE)
                    vE.inc(e.tensor_tensor(out=sq2lb[:, par * HALF:(par + 1) * HALF],
                                           in0=sq2f[:, par * HALF:(par + 1) * HALF],
                                           in1=sq2hb[:, par * HALF:(par + 1) * HALF],
                                           op=ALU.subtract))
                    ev[("z2sq_l", half, m)] = vE.v

            def f_sync(e, oc=oc):
                m = oc + 1
                if m < NIC:
                    _w(e, tE, ev[("wt_tr", half, m - 1, 7)])
                    s_wf.dma(e.dma_start(out=wfF[:], in_=wf[m * 128:(m + 1) * 128, :]))
                    ev[("wfq", half, m)] = (s_wf, s_wf.v)

            def f_gp(e, oc=oc):
                m = oc - 2
                if 0 <= m < NIC:
                    _w(e, vE, ev[("z2l", half, m)])
                    par = m % 2
                    s_z2sp[par].dma(e.dma_start(out=z2_scr[0][m, :, half * HALF:(half + 1) * HALF],
                                                in_=z2hb[:, par * HALF:(par + 1) * HALF]))
                    s_z2sp[par].dma(e.dma_start(out=z2_scr[1][m, :, half * HALF:(half + 1) * HALF],
                                                in_=z2lb[:, par * HALF:(par + 1) * HALF]))
                    ev[("z2sp", half, m)] = (s_z2sp[par], s_z2sp[par].v)
            wt_group_secs(oc)
            B.tensor(f_te)
            B.vector(f_dve)
            B.sync(f_sync)
            B.gpsimd(f_gp)

        def f_dve_fin(e):
            _w(e, tE, ev[("stats4", half, NIC - 1)])
            vE.inc(e.tensor_scalar(out=mu2r[0:1, half * HALF:(half + 1) * HALF], in0=psZs4[:],
                                   scalar1=1.0 / D, scalar2=None, op0=ALU.mult))
            vE.inc(e.tensor_scalar(out=srow[:], in0=psQs4[:], scalar1=1.0 / D, scalar2=None,
                                   op0=ALU.mult))
            mu = mu2r[0:1, half * HALF:(half + 1) * HALF]
            sw(e, vE)
            vE.inc(e.tensor_tensor(out=srow2[:], in0=mu, in1=mu, op=ALU.mult))
            sw(e, vE)
            vE.inc(e.tensor_tensor(out=srow[:], in0=srow[:], in1=srow2[:], op=ALU.subtract))
            sw(e, vE)
            vE.inc(e.tensor_scalar(out=srow[:], in0=srow[:], scalar1=1e-5, scalar2=None,
                                   op0=ALU.add))
            ev[("ln2_var", half)] = vE.v
        B.vector(f_dve_fin)

        def f_act_fin(e):
            _w(e, vE, ev[("ln2_var", half)])
            aE.inc(e.activation(out=srow[:], in_=srow[:], func=ACTF.Sqrt, bias=0.0, scale=1.0))
            ev[("ln2_sqrt", half)] = aE.v
        B.scalar(f_act_fin)

        def f_dve_fin2(e):
            _w(e, aE, ev[("ln2_sqrt", half)])
            vE.inc(e.reciprocal(out=rs2r[0:1, half * HALF:(half + 1) * HALF], in_=srow[:]))
        B.vector(f_dve_fin2)
        barrier()

    for half in range(2 if MAXPH >= 4 else 0):
        p4_half(half)
        keep = ("rs", "qbc", "z2sp", "q_ones")
        ev = {k: v for k, v in ev.items()
              if not isinstance(k, tuple) or k[0] in keep}

    ps4.close()
    es4.close()

    es5 = ExitStack()
    zoh = es5.enter_context(nc.sbuf_tensor("zoh", [128, 2 * HALF], BF16))
    zol = es5.enter_context(nc.sbuf_tensor("zol", [128, 2 * HALF], BF16))
    x2tmp = es5.enter_context(nc.sbuf_tensor("x2tmp", [128, 2 * HALF], F32))
    ostg = es5.enter_context(nc.sbuf_tensor("ostg", [128, 2 * HALF], F32))
    ps5 = ExitStack()
    psX5 = ps5.enter_context(nc.psum_tensor("psX5", [128, 1024], F32))
    psB5 = ps5.enter_context(nc.psum_tensor("psB5", [128, 512], F32))

    def p5_half(half):
        bcast_rows("ln2mu", mu2r, mu_b, half, psB5)
        bcast_rows("ln2rs", rs2r, r_b, half, psB5)
        for ic in range(NIC + 2):
            def f_gp(e, ic=ic):
                if ic < NIC:
                    if ("x2f", half, ic - 2) in ev:
                        _w(e, vE, ev[("x2f", half, ic - 2)])
                    par = ic % 2
                    s_zo[par].dma(e.dma_start(out=zoh[:, par * HALF:(par + 1) * HALF],
                                              in_=z2_scr[0][ic, :, half * HALF:(half + 1) * HALF]))
                    s_zo[par].dma(e.dma_start(out=zol[:, par * HALF:(par + 1) * HALF],
                                              in_=z2_scr[1][ic, :, half * HALF:(half + 1) * HALF]))
                    ev[("zo", half, ic)] = (s_zo[par], s_zo[par].v)
            def f_dve(e, ic=ic):
                m = ic - 1
                if 0 <= m < NIC:
                    if m == 0:
                        _w(e, vE, ev[("ln2mu_bc", half)])
                        _w(e, vE, ev[("ln2rs_bc", half)])
                    par = m % 2
                    wde(e, ("zo", half, m))
                    if ("x2tr", half, m - 2) in ev:
                        _w(e, tE, ev[("x2tr", half, m - 2)])
                    sw(e, vE)
                    x2 = x2tmp[:, par * HALF:(par + 1) * HALF]
                    vE.inc(e.tensor_tensor(out=x2, in0=zoh[:, par * HALF:(par + 1) * HALF],
                                           in1=zol[:, par * HALF:(par + 1) * HALF], op=ALU.add))
                    sw(e, vE)
                    vE.inc(e.tensor_tensor(out=x2, in0=x2, in1=mu_b[:], op=ALU.subtract))
                    sw(e, vE)
                    vE.inc(e.tensor_tensor(out=x2, in0=x2, in1=r_b[:], op=ALU.mult))
                    ev[("x2f", half, m)] = vE.v
            def f_te(e, ic=ic):
                m = ic - 1
                if 0 <= m < NIC:
                    par = m % 2
                    _w(e, vE, ev[("x2f", half, m)])
                    if ("x2ev", half, m - 2) in ev:
                        _w(e, vE, ev[("x2ev", half, m - 2)])
                    for tt in range(4):
                        tE.inc(e.transpose(
                            psX5[:, par * 512 + tt * 128:par * 512 + (tt + 1) * 128],
                            x2tmp[:, par * HALF + tt * 128:par * HALF + (tt + 1) * 128],
                            idn[:]))
                    ev[("x2tr", half, m)] = tE.v
            def f_dve2(e, ic=ic):
                m = ic - 1
                if 0 <= m < NIC:
                    par = m % 2
                    _w(e, tE, ev[("x2tr", half, m)])
                    if ("ost", half, m - 2) in ev:
                        wde(e, ("ost", half, m - 2))
                    vE.inc(e.tensor_copy(out=ostg[:, par * HALF:(par + 1) * HALF],
                                         in_=psX5[:, par * 512:(par + 1) * 512]))
                    ev[("x2ev", half, m)] = vE.v
            def f_sync(e, ic=ic):
                m = ic - 1
                if 0 <= m < NIC:
                    par = m % 2
                    _w(e, vE, ev[("x2ev", half, m)])
                    for tt in range(4):
                        s_ost[par].dma(e.dma_start(
                            out=out[(half * 4 + tt) * 128:(half * 4 + tt + 1) * 128,
                                    m * 128:(m + 1) * 128],
                            in_=ostg[:, par * HALF + tt * 128:par * HALF + (tt + 1) * 128]))
                    ev[("ost", half, m)] = (s_ost[par], s_ost[par].v)
            B.gpsimd(f_gp)
            B.vector(f_dve)
            B.tensor(f_te)
            B.vector(f_dve2)
            B.sync(f_sync)

    for half in range(2 if MAXPH >= 5 else 0):
        p5_half(half)
        barrier()

    def f_sync_fin(e):
        for sset in (SY_SEMS, GP_SEMS):
            for sm in sset:
                if sm.v > 0:
                    e.wait_ge(sm.h, sm.v)
    B.sync(f_sync_fin)

    ps5.close()
    es5.close()
    es3.close()
    esA.close()
    B.__exit__(None, None, None)
    es.close()
    return nc


_NC_CACHE = {}


def kernel(L0_attn_hidden_states, L0_attn_Wq, L0_attn_Wkv, L0_attn_Wo,
           L0_ffn_weight, L0_ffn_bias, L0_ffn_centroids, L0_ffn_assignments):
    x = np.ascontiguousarray(np.asarray(L0_attn_hidden_states, dtype=np.float32))
    wq = np.ascontiguousarray(np.asarray(L0_attn_Wq, dtype=np.float32))
    wkv = np.ascontiguousarray(np.asarray(L0_attn_Wkv, dtype=np.float32))
    wo = np.ascontiguousarray(np.asarray(L0_attn_Wo, dtype=np.float32))
    wf = np.ascontiguousarray(np.asarray(L0_ffn_weight, dtype=np.float32))
    bias = np.ascontiguousarray(np.asarray(L0_ffn_bias, dtype=np.float32)).reshape(1, D)
    cent = np.ascontiguousarray(np.asarray(L0_ffn_centroids, dtype=np.float32))
    assign = np.ascontiguousarray(np.asarray(L0_ffn_assignments, dtype=np.int32)).reshape(1, D)

    slopes = _slopes()
    jj = np.arange(512, dtype=np.float32)[None, None, :]
    ii = np.arange(128, dtype=np.float32)[None, :, None]
    alibi = (slopes[:, None, None] * (jj - ii)).astype(np.float32)

    ident = np.eye(128, dtype=np.float32)
    ones128 = np.ones((128, 128), dtype=np.float32)
    iota = np.arange(64, dtype=np.float32).reshape(64, 1)

    in_maps = []
    for c in range(8):
        b, j = divmod(c, 2)
        ch = CHUNKS[j]
        x_own = np.concatenate([x[b, k * 128:(k + 1) * 128] for k in ch], axis=0)
        mk = np.zeros((8, 128, 640), dtype=np.float32)
        for sg in range(8):
            L = LS[sg] * 128
            t0 = ch[sg] * 128
            r0 = L - 640
            keys = r0 + np.arange(640)[None, :]
            toks = t0 + np.arange(128)[:, None]
            mk[sg] = np.where(keys <= toks, 0.0, NEG).astype(np.float32)
        in_maps.append({
            "x_own": x_own, "x_kv": x[b], "wq": wq, "wo": wo, "wf": wf,
            "wkv": wkv, "biasr": bias, "cent": cent, "assign_row": assign,
            "alibi": alibi, "masks": mk.astype(BF),
            "identf": ident, "onesf": ones128, "iota64": iota,
        })

    if "nc" not in _NC_CACHE:
        _NC_CACHE["nc"] = build_nc()
    nc = _NC_CACHE["nc"]
    res = run_bass_kernel_spmd(nc, in_maps, core_ids=list(range(8)))

    full = np.empty((Bb, S, D), dtype=np.float32)
    for c in range(8):
        b, j = divmod(c, 2)
        o = np.asarray(res.results[c]["out"])
        for k, chk in enumerate(CHUNKS[j]):
            full[b, chk * 128:(chk + 1) * 128] = o[k * 128:(k + 1) * 128]
    return full


# version 10
# speedup vs baseline: 1.1321x; 1.0100x over previous
import math
import numpy as np
import ml_dtypes
import concourse.bass as bass
import concourse.mybir as mybir
from concourse.bass_utils import run_bass_kernel_spmd
from contextlib import ExitStack

F32 = mybir.dt.float32
BF16 = mybir.dt.bfloat16
I32 = mybir.dt.int32
ALU = mybir.AluOpType
ACTF = mybir.ActivationFunctionType
AX = mybir.AxisListType.X
BF = ml_dtypes.bfloat16

Bb, S, D, NH, DH = 4, 2048, 4096, 32, 128
NIC = D // 128
TPC = 1024
HALF = 512
LS = [5, 6, 7, 8, 13, 14, 15, 16]
CHUNKS = ([0, 1, 2, 3, 12, 13, 14, 15], [4, 5, 6, 7, 8, 9, 10, 11])
SCALE = float(DH) ** -0.5
NEG = -1.0e30
MAXPH = 99


def _slopes():
    start = 2.0 ** (-2.0 ** (-(math.log2(NH) - 3)))
    return np.array([start ** (i + 1) for i in range(NH)], dtype=np.float32)


class Sem:
    def __init__(self, nc, es, name):
        self.h = es.enter_context(nc.semaphore(name))
        self.v = 0

    def inc(self, instr, k=1):
        instr.then_inc(self.h, k)
        self.v += k
        return self.v

    def dma(self, instr):
        return self.inc(instr, 16)


def _w(eng, sem, val):
    if val > 0:
        eng.wait_ge(sem.h, val)


def build_nc():
    nc = bass.Bass("TRN2", target_bir_lowering=False)
    x_own = nc.dram_tensor("x_own", [TPC, D], F32, kind="ExternalInput").ap()
    x_kv = nc.dram_tensor("x_kv", [S, D], F32, kind="ExternalInput").ap()
    wq = nc.dram_tensor("wq", [D, D], F32, kind="ExternalInput").ap()
    wo = nc.dram_tensor("wo", [D, D], F32, kind="ExternalInput").ap()
    wf = nc.dram_tensor("wf", [D, D], F32, kind="ExternalInput").ap()
    wkv = nc.dram_tensor("wkv", [D, 256], F32, kind="ExternalInput").ap()
    biasr = nc.dram_tensor("biasr", [1, D], F32, kind="ExternalInput").ap()
    cent = nc.dram_tensor("cent", [64, D], F32, kind="ExternalInput").ap()
    assign_row = nc.dram_tensor("assign_row", [1, D], I32, kind="ExternalInput").ap()
    alibi = nc.dram_tensor("alibi", [NH, 128, 512], F32, kind="ExternalInput").ap()
    masks = nc.dram_tensor("masks", [8, 128, 640], BF16, kind="ExternalInput").ap()
    identf = nc.dram_tensor("identf", [128, 128], F32, kind="ExternalInput").ap()
    onesf = nc.dram_tensor("onesf", [128, 128], F32, kind="ExternalInput").ap()
    iota64 = nc.dram_tensor("iota64", [64, 1], F32, kind="ExternalInput").ap()
    out = nc.dram_tensor("out", [TPC, D], F32, kind="ExternalOutput").ap()
    o_scr = [nc.dram_tensor(f"o_scr{p}", [NIC, 128, TPC], BF16).ap() for p in range(2)]
    xt_scr = [nc.dram_tensor(f"xt_scr{p}", [NIC, 128, TPC], BF16).ap() for p in range(2)]
    z1_scr = [nc.dram_tensor(f"z1_scr{p}", [NIC, 128, TPC], BF16).ap() for p in range(2)]
    z2_scr = [nc.dram_tensor(f"z2_scr{p}", [NIC, 128, TPC], BF16).ap() for p in range(2)]
    cc_in = nc.dram_tensor("cc_in", [64, 1], F32)
    cc_out = nc.dram_tensor("cc_out", [64, 1], F32)

    es = ExitStack()
    qS = Sem(nc, es, "qS")
    qG = Sem(nc, es, "qG")
    tE = Sem(nc, es, "tE")
    vE = Sem(nc, es, "vE")
    aE = Sem(nc, es, "aE")
    cS = Sem(nc, es, "cS")
    bar = Sem(nc, es, "bar")
    s_cst = Sem(nc, es, "s_cst")
    s_wkv = Sem(nc, es, "s_wkv")
    s_xnb = [Sem(nc, es, f"s_xnb{p}") for p in range(2)]
    s_mask = Sem(nc, es, "s_mask")
    s_xnat = Sem(nc, es, "s_xnat")
    s_wq = [Sem(nc, es, f"s_wq{p}") for p in range(2)]
    s_al = [Sem(nc, es, f"s_al{p}") for p in range(2)]
    s_osp = Sem(nc, es, "s_osp")
    s_old = Sem(nc, es, "s_old")
    s_xoc = [Sem(nc, es, f"s_xoc{p}") for p in range(2)]
    s_wo = Sem(nc, es, "s_wo")
    s_wf = Sem(nc, es, "s_wf")
    s_z1sp = [Sem(nc, es, f"s_z1sp{p}") for p in range(2)]
    s_z2sp = [Sem(nc, es, f"s_z2sp{p}") for p in range(2)]
    s_zr = [Sem(nc, es, f"s_zr{p}") for p in range(2)]
    s_zo = [Sem(nc, es, f"s_zo{p}") for p in range(2)]
    s_ost = [Sem(nc, es, f"s_ost{p}") for p in range(2)]
    s_cc = Sem(nc, es, "s_cc")
    SY_SEMS = [qS, s_cst, s_wkv, s_xnb[0], s_xnb[1], s_mask, s_xnat,
               s_wq[0], s_wq[1],
               s_al[0], s_al[1], s_wo, s_wf, s_ost[0], s_ost[1]]
    GP_SEMS = [qG, s_osp, s_old, s_xoc[0], s_xoc[1], s_z1sp[0], s_z1sp[1],
               s_z2sp[0], s_z2sp[1], s_zr[0], s_zr[1], s_zo[0], s_zo[1], s_cc]

    def wde(e, key):
        sm, v = ev[key]
        _w(e, sm, v)

    def sw(e, sem):
        e.wait_ge(sem.h, sem.v)

    idn = es.enter_context(nc.sbuf_tensor("idn", [128, 128], F32))
    onesb_f = es.enter_context(nc.sbuf_tensor("onesb_f", [128, 128], F32))
    onesb = es.enter_context(nc.sbuf_tensor("onesb", [128, 128], BF16))
    iota_s = es.enter_context(nc.sbuf_tensor("iota_s", [64, 1], F32))
    rs_s = es.enter_context(nc.sbuf_tensor("rs_s", [128, 32], F32))
    csb = es.enter_context(nc.sbuf_tensor("csb", [64, 1], BF16))

    B = nc.Block()
    B.__enter__()

    def barrier():
        tgt = bar.v + 5
        bar.v = tgt
        sy_vals = [(s.h, s.v) for s in SY_SEMS if s.v > 0]
        gp_vals = [(s.h, s.v) for s in GP_SEMS if s.v > 0]
        for attr, drains, dr in ((B.tensor, None, True), (B.scalar, None, True),
                                 (B.vector, None, True), (B.sync, sy_vals, False),
                                 (B.gpsimd, gp_vals, True)):
            def f(e, tgt=tgt, drains=drains, dr=dr):
                if drains:
                    for h, v in drains:
                        e.wait_ge(h, v)
                if dr:
                    e.drain()
                e.sem_inc(bar.h, 1)
                e.wait_ge(bar.h, tgt)
            attr(f)

    ev = {}

    def tsp_rounds(tag, key, ntiles, src_of, hi_of, lo_of, ps, slots, tile_w,
                   pre_wait=None, scale=1.0, src_p=128, ident_ap=None):
        idap = ident_ap if ident_ap is not None else idn[:]
        assert slots * tile_w <= 512
        nr = (ntiles + slots - 1) // slots

        def boff(r):
            return (r % 2) * 512

        for r in range(nr):
            i0, i1 = r * slots, min((r + 1) * slots, ntiles)

            def f_te(e, r=r, i0=i0, i1=i1):
                if r == 0 and pre_wait is not None:
                    pre_wait(e)
                if r >= 2 and (tag + "_r", key, r - 2) in ev:
                    _w(e, vE, ev[(tag + "_r", key, r - 2)])
                for i in range(i0, i1):
                    g = i - i0
                    o = boff(r) + g * tile_w
                    tE.inc(e.transpose(ps[:, o:o + tile_w], src_of(i), idap))
                ev[(tag + "_tr", key, r)] = tE.v

            def f_dve(e, r=r, i0=i0, i1=i1):
                _w(e, tE, ev[(tag + "_tr", key, r)])
                for i in range(i0, i1):
                    g = i - i0
                    o = boff(r) + g * tile_w
                    if scale == 1.0:
                        vE.inc(e.tensor_copy(out=hi_of(i), in_=ps[:, o:o + tile_w]))
                    else:
                        vE.inc(e.tensor_scalar(out=hi_of(i), in0=ps[:, o:o + tile_w],
                                               scalar1=scale, scalar2=None, op0=ALU.mult))
                ev[(tag + "_hr", key, r)] = vE.v
                sw(e, vE)
                for i in range(i0, i1):
                    g = i - i0
                    o = boff(r) + g * tile_w
                    if scale == 1.0:
                        vE.inc(e.tensor_tensor(out=lo_of(i), in0=ps[:, o:o + tile_w],
                                               in1=hi_of(i), op=ALU.subtract))
                    else:
                        vE.inc(e.scalar_tensor_tensor(
                            out=lo_of(i), in0=ps[:, o:o + tile_w],
                            scalar=scale, in1=hi_of(i), op0=ALU.mult, op1=ALU.subtract))
                ev[(tag + "_r", key, r)] = vE.v
            B.tensor(f_te)
            B.vector(f_dve)
        ev[(tag + "_hi", key)] = ev[(tag + "_hr", key, nr - 1)]
        ev[(tag + "_lo", key)] = ev[(tag + "_r", key, nr - 1)]
        ev[(tag + "_trlast", key)] = ev[(tag + "_tr", key, nr - 1)]

    es1kv = ExitStack()
    kth = es1kv.enter_context(nc.sbuf_tensor("kth", [128, S], BF16))
    ktl = es1kv.enter_context(nc.sbuf_tensor("ktl", [128, S], BF16))
    vth = es1kv.enter_context(nc.sbuf_tensor("vth", [128, S], BF16))
    vtl = es1kv.enter_context(nc.sbuf_tensor("vtl", [128, S], BF16))
    es0 = ExitStack()
    wkvf = es0.enter_context(nc.sbuf_tensor("wkvf", [128, NIC * 256], F32))
    wkvh = es0.enter_context(nc.sbuf_tensor("wkvh", [128, NIC * 256], BF16))
    wkvl = es0.enter_context(nc.sbuf_tensor("wkvl", [128, NIC * 256], BF16))
    xnb0 = es0.enter_context(nc.sbuf_tensor("xnb0", [128, 2 * D], F32))
    xtkh = es0.enter_context(nc.sbuf_tensor("xtkh", [128, NIC * 256], BF16))
    xtkl = es0.enter_context(nc.sbuf_tensor("xtkl", [128, NIC * 256], BF16))
    vtf = es0.enter_context(nc.sbuf_tensor("vtf", [128, 256], F32))
    ps0 = ExitStack()
    psK = ps0.enter_context(nc.psum_tensor("psK", [128, 512], F32))
    psV = ps0.enter_context(nc.psum_tensor("psV", [128, 512], F32))
    psX0 = ps0.enter_context(nc.psum_tensor("psX0", [128, 1024], F32))
    psVt = ps0.enter_context(nc.psum_tensor("psVt", [128, 512], F32))

    def f_sync0(e):
        s_cst.dma(e.dma_start(out=idn[:], in_=identf))
        s_cst.dma(e.dma_start(out=onesb_f[:], in_=onesf))
        s_cst.dma(e.dma_start(out=iota_s[:], in_=iota64))
        ev["p0_loads"] = (s_cst, s_cst.v)
        s_wkv.dma(e.dma_start(
            out=wkvf[:].rearrange("p (ic c) -> p ic c", c=256),
            in_=wkv.rearrange("(ic p) c -> p ic c", p=128)))
        ev["p0_wkv"] = (s_wkv, s_wkv.v)
    B.sync(f_sync0)

    def f_dve0(e):
        wde(e, "p0_loads")
        vE.inc(e.tensor_copy(out=onesb[:], in_=onesb_f[:]))
        wde(e, "p0_wkv")
        vE.inc(e.tensor_copy(out=wkvh[:], in_=wkvf[:]))
        ev["p0_wkvh"] = vE.v
        sw(e, vE)
        vE.inc(e.tensor_tensor(out=wkvl[:], in0=wkvf[:], in1=wkvh[:], op=ALU.subtract))
        ev["p0_wkvl"] = vE.v
    B.vector(f_dve0)

    for ks in range(8):
        def f_sync(e, ks=ks):
            for tt in range(2):
                if ks > 0:
                    _w(e, tE, ev[("kvx_trlast", (ks - 1, tt))])
                s_xnb[tt].dma(e.dma_start(out=xnb0[:, tt * D:(tt + 1) * D],
                                          in_=x_kv[(ks * 2 + tt) * 128:(ks * 2 + tt + 1) * 128, :]))
                ev[("kv_x", ks, tt)] = (s_xnb[tt], s_xnb[tt].v)
        B.sync(f_sync)

        for tt in range(2):
            def prew(e, ks=ks, tt=tt):
                wde(e, "p0_loads")
                wde(e, ("kv_x", ks, tt))
                if (ks, tt) != (0, 0):
                    pks, ptt = (ks, tt - 1) if tt > 0 else (ks - 1, 1)
                    _w(e, vE, ev[("kvx_lo", (pks, ptt))])
            tsp_rounds("kvx", (ks, tt), NIC,
                       (lambda i, ks=ks, tt=tt:
                        xnb0[:, tt * D + i * 128:tt * D + (i + 1) * 128]),
                       (lambda i, tt=tt: xtkh[:, i * 256 + tt * 128:i * 256 + (tt + 1) * 128]),
                       (lambda i, tt=tt: xtkl[:, i * 256 + tt * 128:i * 256 + (tt + 1) * 128]),
                       psX0, 4, 128, pre_wait=prew)

        def f_te(e, ks=ks):
            _w(e, vE, ev[("kvx_hi", (ks, 1))])
            _w(e, vE, ev[("kvx_lo", (ks, 1))])
            if ks == 0:
                _w(e, vE, ev["p0_wkvl"])
            else:
                _w(e, aE, ev[("kv_kh", ks - 1)])
                _w(e, vE, ev[("kv_kl", ks - 1)])
                _w(e, vE, ev[("kv_vtf", ks - 1)])
                _w(e, vE, ev[("kv_vl", ks - 1, 1)])
            for ic in range(NIC):
                wk_h = wkvh[:, ic * 256:ic * 256 + 128]
                wk_l = wkvl[:, ic * 256:ic * 256 + 128]
                xh = xtkh[:, ic * 256:(ic + 1) * 256]
                xl = xtkl[:, ic * 256:(ic + 1) * 256]
                e.matmul(psK[:, 0:256], wk_h, xh, start=(ic == 0), stop=False)
                e.matmul(psK[:, 0:256], wk_h, xl, start=False, stop=False)
                mmK = e.matmul(psK[:, 0:256], wk_l, xh, start=False, stop=(ic == NIC - 1))
            tE.inc(mmK)
            ev[("kv_kmm", ks)] = tE.v
            for ic in range(NIC):
                wv_h = wkvh[:, ic * 256 + 128:(ic + 1) * 256]
                wv_l = wkvl[:, ic * 256 + 128:(ic + 1) * 256]
                xh = xtkh[:, ic * 256:(ic + 1) * 256]
                xl = xtkl[:, ic * 256:(ic + 1) * 256]
                e.matmul(psV[:, 0:256], wv_h, xh, start=(ic == 0), stop=False)
                e.matmul(psV[:, 0:256], wv_h, xl, start=False, stop=False)
                mmV = e.matmul(psV[:, 0:256], wv_l, xh, start=False, stop=(ic == NIC - 1))
            tE.inc(mmV)
            ev[("kv_vmm", ks)] = tE.v
        B.tensor(f_te)

        def f_dve_v(e, ks=ks):
            _w(e, tE, ev[("kv_vmm", ks)])
            if ks > 0:
                _w(e, aE, ev[("kv_vh", ks - 1, 1)])
                _w(e, vE, ev[("kv_vl", ks - 1, 1)])
            vE.inc(e.tensor_copy(out=vtf[:], in_=psV[:, 0:256]))
            ev[("kv_vtf", ks)] = vE.v
        B.vector(f_dve_v)

        def f_te_v(e, ks=ks):
            _w(e, vE, ev[("kv_vtf", ks)])
            for sub in range(2):
                tE.inc(e.transpose(psVt[:, sub * 128:(sub + 1) * 128],
                                   vtf[:, sub * 128:(sub + 1) * 128], idn[:]))
                ev[("kv_vtr", ks, sub)] = tE.v
        B.tensor(f_te_v)

        def f_act(e, ks=ks):
            _w(e, tE, ev[("kv_kmm", ks)])
            aE.inc(e.activation(out=kth[:, ks * 256:(ks + 1) * 256], in_=psK[:, 0:256],
                                func=ACTF.Copy, bias=0.0, scale=1.0))
            ev[("kv_kh", ks)] = aE.v
            for sub in range(2):
                _w(e, tE, ev[("kv_vtr", ks, sub)])
                u = ks * 2 + sub
                aE.inc(e.activation(out=vth[:, u * 128:(u + 1) * 128],
                                    in_=psVt[:, sub * 128:(sub + 1) * 128],
                                    func=ACTF.Copy, bias=0.0, scale=1.0))
                ev[("kv_vh", ks, sub)] = aE.v
        B.scalar(f_act)

        def f_dve(e, ks=ks):
            _w(e, aE, ev[("kv_kh", ks)])
            vE.inc(e.tensor_tensor(out=ktl[:, ks * 256:(ks + 1) * 256], in0=psK[:, 0:256],
                                   in1=kth[:, ks * 256:(ks + 1) * 256], op=ALU.subtract))
            ev[("kv_kl", ks)] = vE.v
            for sub in range(2):
                _w(e, aE, ev[("kv_vh", ks, sub)])
                u = ks * 2 + sub
                vE.inc(e.tensor_tensor(out=vtl[:, u * 128:(u + 1) * 128],
                                       in0=psVt[:, sub * 128:(sub + 1) * 128],
                                       in1=vth[:, u * 128:(u + 1) * 128], op=ALU.subtract))
                ev[("kv_vl", ks, sub)] = vE.v
        B.vector(f_dve)
    barrier()
    ps0.close()
    es0.close()

    es1 = ExitStack()
    mask_s = es1.enter_context(nc.sbuf_tensor("mask_s", [128, 4 * 640], BF16))
    albf = es1.enter_context(nc.sbuf_tensor("albf", [128, 1024], F32))
    xth = es1.enter_context(nc.sbuf_tensor("xth", [128, NIC * HALF], BF16))
    xtl = es1.enter_context(nc.sbuf_tensor("xtl", [128, NIC * HALF], BF16))
    wqf = es1.enter_context(nc.sbuf_tensor("wqf", [128, 2 * NIC * 128], F32))
    wqh = es1.enter_context(nc.sbuf_tensor("wqh", [128, NIC * 128], BF16))
    wql = es1.enter_context(nc.sbuf_tensor("wql", [128, NIC * 128], BF16))
    qsh = es1.enter_context(nc.sbuf_tensor("qsh", [128, 2 * HALF], BF16))
    qsl = es1.enter_context(nc.sbuf_tensor("qsl", [128, 2 * HALF], BF16))
    logits = es1.enter_context(nc.sbuf_tensor("logits", [128, 3 * 2048], F32))
    pbuf = es1.enter_context(nc.sbuf_tensor("pbuf", [128, 2 * 2048], F32))
    pth = es1.enter_context(nc.sbuf_tensor("pth", [128, 2 * 2048], BF16))
    ptl = es1.enter_context(nc.sbuf_tensor("ptl", [128, 2 * 2048], BF16))
    obh = es1.enter_context(nc.sbuf_tensor("obh", [128, 512], BF16))
    obl = es1.enter_context(nc.sbuf_tensor("obl", [128, 512], BF16))
    negm = es1.enter_context(nc.sbuf_tensor("negm", [128, 2], F32))
    sums = es1.enter_context(nc.sbuf_tensor("sums", [128, 2], F32))
    rrec = es1.enter_context(nc.sbuf_tensor("rrec", [128, 2], F32))
    ps1 = ExitStack()
    psQ = ps1.enter_context(nc.psum_tensor("psQ", [128, 1024], F32))
    psS = ps1.enter_context(nc.psum_tensor("psS", [128, 1024], F32))
    psT = ps1.enter_context(nc.psum_tensor("psT", [128, 1024], F32))
    psO = ps1.enter_context(nc.psum_tensor("psO", [128, 1024], F32))

    def wq_dma(e, h):
        fb = (h % 2) * NIC * 128
        sq = s_wq[h % 2]
        for dcg in range(4):
            src_ = wq[dcg * 1024:(dcg + 1) * 1024,
                      h * 128:(h + 1) * 128].rearrange("(ic p) o -> p ic o", p=128)
            dst = wqf[:, fb + dcg * 1024:fb + (dcg + 1) * 1024].rearrange(
                "p (ic c) -> p ic c", c=128)
            sq.dma(e.dma_start(out=dst, in_=src_))
        ev[("wq", h)] = (sq, sq.v)

    def wq_split(e, h):
        wde(e, ("wq", h))
        if ("qmm", h - 1) in ev:
            _w(e, tE, ev[("qmm", h - 1)])
        fb = (h % 2) * NIC * 128
        vE.inc(e.tensor_copy(out=wqh[:], in_=wqf[:, fb:fb + NIC * 128]))
        ev[("wqh", h)] = vE.v
        sw(e, vE)
        vE.inc(e.tensor_tensor(out=wql[:], in0=wqf[:, fb:fb + NIC * 128],
                               in1=wqh[:], op=ALU.subtract))
        ev[("wql", h)] = vE.v

    def q_proj(e, h, half):
        _w(e, vE, ev[("wql", h)])
        _w(e, vE, ev[("xb_a", half)])
        _w(e, vE, ev[("xb_v", half)])
        if ("qlo", h - 2) in ev:
            _w(e, vE, ev[("qlo", h - 2)])
        par = h % 2
        dst = psQ[:, par * 512:(par + 1) * 512]
        for ic in range(NIC):
            wh = wqh[:, ic * 128:(ic + 1) * 128]
            wl = wql[:, ic * 128:(ic + 1) * 128]
            xh = xth[:, ic * HALF:(ic + 1) * HALF]
            xl = xtl[:, ic * HALF:(ic + 1) * HALF]
            e.matmul(dst, wh, xh, start=(ic == 0), stop=False)
            e.matmul(dst, wh, xl, start=False, stop=False)
            mmq = e.matmul(dst, wl, xh, start=False, stop=(ic == NIC - 1))
        tE.inc(mmq)
        ev[("qmm", h)] = tE.v

    def q_evac(e, h):
        _w(e, tE, ev[("qmm", h)])
        if ("amm_last", h - 2) in ev:
            _w(e, tE, ev[("amm_last", h - 2)])
        par = h % 2
        vE.inc(e.tensor_scalar(out=qsh[:, par * 512:(par + 1) * 512],
                               in0=psQ[:, par * 512:(par + 1) * 512],
                               scalar1=SCALE, scalar2=None, op0=ALU.mult))
        ev[("qhi", h)] = vE.v
        sw(e, vE)
        vE.inc(e.scalar_tensor_tensor(out=qsl[:, par * 512:(par + 1) * 512],
                                      in0=psQ[:, par * 512:(par + 1) * 512], scalar=SCALE,
                                      in1=qsh[:, par * 512:(par + 1) * 512],
                                      op0=ALU.mult, op1=ALU.subtract))
        ev[("qlo", h)] = vE.v

    def p1_half(half):
        slopes = _slopes()
        for tt in range(4):
            def f_sync(e, tt=tt):
                if tt > 0:
                    _w(e, tE, ev[("xb_trlast", (half, tt - 1))])
                s_xnat.dma(e.dma_start(out=pbuf[:, 0:4096],
                                       in_=x_own[(half * 4 + tt) * 128:(half * 4 + tt + 1) * 128, :]))
                ev[("x_nat", half, tt)] = (s_xnat, s_xnat.v)
                if tt == 0:
                    for s in range(4):
                        s_mask.dma(e.dma_start(out=mask_s[:, s * 640:(s + 1) * 640],
                                               in_=masks[half * 4 + s]))
                    ev[("mask", half)] = (s_mask, s_mask.v)
            B.sync(f_sync)

            def prew(e, tt=tt):
                wde(e, ("x_nat", half, tt))
                if tt > 0:
                    _w(e, vE, ev[("xb_lo", (half, tt - 1))])
            tsp_rounds("xb", (half, tt), NIC,
                       (lambda i: pbuf[:, i * 128:(i + 1) * 128]),
                       (lambda i, tt=tt: xth[:, i * HALF + tt * 128:i * HALF + (tt + 1) * 128]),
                       (lambda i, tt=tt: xtl[:, i * HALF + tt * 128:i * HALF + (tt + 1) * 128]),
                       psT, 4, 128, pre_wait=prew)
        ev[("xb_a", half)] = ev[("xb_hi", (half, 3))]
        ev[("xb_v", half)] = ev[("xb_lo", (half, 3))]

        def f_gp_sp(e):
            _w(e, vE, ev[("xb_a", half)])
            _w(e, vE, ev[("xb_v", half)])
            for ic in range(NIC):
                qG.dma(e.dma_start(out=xt_scr[0][ic, :, half * HALF:(half + 1) * HALF],
                                   in_=xth[:, ic * HALF:(ic + 1) * HALF]))
                qG.dma(e.dma_start(out=xt_scr[1][ic, :, half * HALF:(half + 1) * HALF],
                                   in_=xtl[:, ic * HALF:(ic + 1) * HALF]))
            ev[("xt_spill", half)] = qG.v
        B.gpsimd(f_gp_sp)

        def f_sync_pro(e):
            wq_dma(e, 0)
            s_al[0].dma(e.dma_start(out=albf[:, 0:512], in_=alibi[0]))
            ev[("al", 0)] = (s_al[0], s_al[0].v)
        B.sync(f_sync_pro)
        B.vector(lambda e: wq_split(e, 0))

        def f_sync_pro2(e):
            wq_dma(e, 1)
            _w(e, vE, ev[("wql", 0)])
            wq_dma(e, 2)
        B.sync(f_sync_pro2)
        B.tensor(lambda e: q_proj(e, 0, half))
        B.vector(lambda e: q_evac(e, 0))

        iters = [(h, s) for h in range(NH) for s in range(4)]
        NIT = len(iters)
        gb = [0]
        gtg = [0]

        def blocks_of(sg):
            L = LS[sg] * 128
            res = []
            off = 0
            while off < L:
                res.append((off, min(512, L - off)))
                off += 512
            return res

        pss_read = {}

        def score_block_secs(n):
            if not (0 <= n < NIT):
                return
            h, s = iters[n]
            sg = half * 4 + s
            L = LS[sg] * 128
            lpar = n % 3
            slope_h = float(slopes[h])
            for bi, (off, nb) in enumerate(blocks_of(sg)):
                gidx = gb[0]
                gb[0] += 1
                spar = gidx % 2

                def f_te(e, n=n, h=h, s=s, off=off, nb=nb, spar=spar, bi=bi):
                    if bi == 0 and s == 0:
                        _w(e, vE, ev[("qlo", h)])
                        wde(e, ("al", h))
                        if h == 0:
                            wde(e, ("mask", half))
                    if spar in pss_read:
                        _w(e, vE, pss_read[spar])
                    qpar = h % 2
                    qh_t = qsh[:, qpar * 512 + s * 128:qpar * 512 + (s + 1) * 128]
                    ql_t = qsl[:, qpar * 512 + s * 128:qpar * 512 + (s + 1) * 128]
                    dst = psS[:, spar * 512:spar * 512 + nb]
                    e.matmul(dst, qh_t, kth[:, off:off + nb], start=True, stop=False)
                    e.matmul(dst, qh_t, ktl[:, off:off + nb], start=False, stop=False)
                    mms_ = e.matmul(dst, ql_t, kth[:, off:off + nb], start=False, stop=True)
                    tE.inc(mms_)
                    ev[("smm", n, off)] = tE.v
                B.tensor(f_te)

                def f_dve(e, n=n, off=off, nb=nb, spar=spar, L=L, s=s,
                          lpar=lpar, slope_h=slope_h):
                    _w(e, tE, ev[("smm", n, off)])
                    r0 = L - 640
                    lo = max(off, r0)
                    hi = off + nb
                    if lo < hi:
                        vE.inc(e.tensor_tensor(
                            out=psS[:, spar * 512 + (lo - off):spar * 512 + (hi - off)],
                            in0=psS[:, spar * 512 + (lo - off):spar * 512 + (hi - off)],
                            in1=mask_s[:, s * 640 + (lo - r0):s * 640 + (hi - r0)],
                            op=ALU.add))
                        sw(e, vE)
                    c_hb = float(slope_h * off)
                    vE.inc(e.scalar_tensor_tensor(
                        out=logits[:, lpar * 2048 + off:lpar * 2048 + off + nb],
                        in0=psS[:, spar * 512:spar * 512 + nb], scalar=c_hb,
                        in1=albf[:, (h % 2) * 512:(h % 2) * 512 + nb],
                        op0=ALU.add, op1=ALU.add))
                    pss_read[spar] = vE.v
                    ev[("evac", n)] = vE.v
                B.vector(f_dve)
            if s == 3:
                ev[("amm_last", h)] = tE.v

        def pt_group_secs(n):
            m = n - 3
            if not (0 <= m < NIT):
                return
            h, s = iters[m]
            sg = half * 4 + s
            nu = LS[sg]
            lpar = m % 3
            for g in range((nu + 3) // 4):
                gidx = gtg[0]
                gtg[0] += 1
                ev[("pt_gpar", m, g)] = gidx
                u0 = g * 4
                u1 = min(u0 + 4, nu)

                def f_te(e, m=m, g=g, gidx=gidx, u0=u0, u1=u1, lpar=lpar):
                    if g == 0:
                        _w(e, vE, ev[("pn", m)])
                    if ("ptev_l", gidx - 2) in ev:
                        _w(e, vE, ev[("ptev_l", gidx - 2)])
                    tpar = gidx % 2
                    for u in range(u0, u1):
                        tE.inc(e.transpose(
                            psT[:, tpar * 512 + (u - u0) * 128:tpar * 512 + (u - u0 + 1) * 128],
                            logits[:, lpar * 2048 + u * 128:lpar * 2048 + (u + 1) * 128],
                            idn[:]))
                    ev[("pt_tr", m, g)] = tE.v

                def f_dve(e, m=m, g=g, gidx=gidx, u0=u0, u1=u1):
                    _w(e, tE, ev[("pt_tr", m, g)])
                    tpar = gidx % 2
                    ppar = m % 2
                    w_ = (u1 - u0) * 128
                    vE.inc(e.tensor_copy(
                        out=pth[:, ppar * 2048 + u0 * 128:ppar * 2048 + u0 * 128 + w_],
                        in_=psT[:, tpar * 512:tpar * 512 + w_]))
                    ev[("ptev_h_g", m, g)] = vE.v
                    sw(e, vE)
                    vE.inc(e.tensor_tensor(
                        out=ptl[:, ppar * 2048 + u0 * 128:ppar * 2048 + u0 * 128 + w_],
                        in0=psT[:, tpar * 512:tpar * 512 + w_],
                        in1=pth[:, ppar * 2048 + u0 * 128:ppar * 2048 + u0 * 128 + w_],
                        op=ALU.subtract))
                    ev[("ptev_l_g", m, g)] = vE.v
                    ev[("ptev_l", gidx)] = vE.v
                B.tensor(f_te)
                B.vector(f_dve)

        def te_sec(n):
            def f(e):
                m = n - 4
                if 0 <= m < NIT:
                    h, s = iters[m]
                    sg = half * 4 + s
                    nu = LS[sg]
                    opar = m % 2
                    if ("o_lo", m - 2) in ev:
                        _w(e, vE, ev[("o_lo", m - 2)])
                    ppar = m % 2
                    for u in range(nu):
                        g = u // 4
                        _w(e, vE, ev[("ptev_l_g", m, g)])
                        vh_t = vth[:, u * 128:(u + 1) * 128]
                        vl_t = vtl[:, u * 128:(u + 1) * 128]
                        ph_t = pth[:, ppar * 2048 + u * 128:ppar * 2048 + (u + 1) * 128]
                        pl_t = ptl[:, ppar * 2048 + u * 128:ppar * 2048 + (u + 1) * 128]
                        dst = psO[:, opar * 512:opar * 512 + 128]
                        e.matmul(dst, vh_t, ph_t, start=(u == 0), stop=False)
                        e.matmul(dst, vh_t, pl_t, start=False, stop=False)
                        mmav = e.matmul(dst, vl_t, ph_t, start=False, stop=(u == nu - 1))
                    tE.inc(mmav)
                    ev[("av", m)] = tE.v
                if n < NIT:
                    h, s = iters[n]
                    if s == 2 and h + 1 < NH:
                        q_proj(e, h + 1, half)
            return f

        def act_sec(n):
            def f(e):
                m = n - 1
                if 0 <= m < NIT:
                    h, s = iters[m]
                    sg = half * 4 + s
                    L = LS[sg] * 128
                    _w(e, vE, ev[("max", m)])
                    if ("pn", m - 2) in ev:
                        _w(e, vE, ev[("pn", m - 2)])
                    lpar = m % 3
                    ppar = m % 2
                    aE.inc(e.activation(out=pbuf[:, ppar * 2048:ppar * 2048 + L],
                                        in_=logits[:, lpar * 2048:lpar * 2048 + L],
                                        func=ACTF.Exp, bias=negm[:, m % 2:m % 2 + 1],
                                        scale=1.0, accum_out=sums[:, m % 2:m % 2 + 1]))
                    ev[("exp", m)] = aE.v
            return f

        def dve_sec(n):
            def f(e):
                if n < NIT:
                    h, s = iters[n]
                    if s == 1 and h + 1 < NH:
                        wq_split(e, h + 1)
                    if s == 3 and h + 1 < NH:
                        q_evac(e, h + 1)
                    sg = half * 4 + s
                    L = LS[sg] * 128
                    lpar = n % 3
                    if ("exp", n - 2) in ev:
                        _w(e, aE, ev[("exp", n - 2)])
                    sw(e, vE)
                    vE.inc(e.reduce_max(out=negm[:, n % 2:n % 2 + 1],
                                        in_=logits[:, lpar * 2048:lpar * 2048 + L],
                                        axis=AX, negate=True))
                    ev[("max", n)] = vE.v
                m = n - 1
                if 0 <= m < NIT:
                    _w(e, aE, ev[("exp", m)])
                    vE.inc(e.reciprocal(out=rrec[:, m % 2:m % 2 + 1],
                                        in_=sums[:, m % 2:m % 2 + 1]))
                    ev[("recip", m)] = vE.v
                m = n - 2
                if 0 <= m < NIT:
                    h, s = iters[m]
                    sg = half * 4 + s
                    L = LS[sg] * 128
                    sw(e, vE)
                    lpar = m % 3
                    ppar = m % 2
                    vE.inc(e.tensor_scalar(out=logits[:, lpar * 2048:lpar * 2048 + L],
                                           in0=pbuf[:, ppar * 2048:ppar * 2048 + L],
                                           scalar1=rrec[:, m % 2:m % 2 + 1], scalar2=None,
                                           op0=ALU.mult))
                    ev[("pn", m)] = vE.v
                m = n - 4
                if 0 <= m < NIT:
                    h, s = iters[m]
                    _w(e, tE, ev[("av", m)])
                    if ("ospill", h - 1) in ev:
                        wde(e, ("ospill", h - 1))
                    opar = m % 2
                    vE.inc(e.tensor_copy(out=obh[:, s * 128:(s + 1) * 128],
                                         in_=psO[:, opar * 512:opar * 512 + 128]))
                    ev[("o_hi", m)] = vE.v
                    sw(e, vE)
                    vE.inc(e.tensor_tensor(out=obl[:, s * 128:(s + 1) * 128],
                                           in0=psO[:, opar * 512:opar * 512 + 128],
                                           in1=obh[:, s * 128:(s + 1) * 128],
                                           op=ALU.subtract))
                    ev[("o_lo", m)] = vE.v
            return f

        def sync_sec(n):
            def f(e):
                if n < NIT:
                    h, s = iters[n]
                    if s == 2 and h + 3 < NH:
                        _w(e, vE, ev[("wql", h + 1)])
                        wq_dma(e, h + 3)
                    if s == 0 and h + 1 < NH:
                        if h >= 1:
                            _w(e, vE, ev[("max", (h - 1) * 4 + 3)])
                        p_ = (h + 1) % 2
                        s_al[p_].dma(e.dma_start(
                            out=albf[:, p_ * 512:(p_ + 1) * 512], in_=alibi[h + 1]))
                        ev[("al", h + 1)] = (s_al[p_], s_al[p_].v)
            return f

        def gp_sec(n):
            def f(e):
                m = n - 4
                if 0 <= m < NIT:
                    h, s = iters[m]
                    if s == 3:
                        _w(e, vE, ev[("o_lo", m)])
                        s_osp.dma(e.dma_start(out=o_scr[0][h, :, half * HALF:(half + 1) * HALF],
                                              in_=obh[:]))
                        s_osp.dma(e.dma_start(out=o_scr[1][h, :, half * HALF:(half + 1) * HALF],
                                              in_=obl[:]))
                        ev[("ospill", h)] = (s_osp, s_osp.v)
            return f

        for n in range(NIT + 5):
            pt_group_secs(n)
            score_block_secs(n)
            B.tensor(te_sec(n))
            B.scalar(act_sec(n))
            B.vector(dve_sec(n))
            B.sync(sync_sec(n))
            B.gpsimd(gp_sec(n))
        barrier()

    if MAXPH >= 1:
        for half in range(2):
            p1_half(half)
            ev = {k: v for k, v in ev.items() if not isinstance(k, tuple)}

    ps1.close()
    es1.close()
    es1kv.close()

    esA = ExitStack()
    acth = esA.enter_context(nc.sbuf_tensor("acth", [128, NIC * HALF], BF16))
    actl = esA.enter_context(nc.sbuf_tensor("actl", [128, NIC * HALF], BF16))
    q_b = esA.enter_context(nc.sbuf_tensor("q_b", [128, TPC], F32))
    mu_b = esA.enter_context(nc.sbuf_tensor("mu_b", [128, HALF], F32))
    r_b = esA.enter_context(nc.sbuf_tensor("r_b", [128, HALF], F32))
    mu1r = esA.enter_context(nc.sbuf_tensor("mu1r", [1, TPC], F32))
    rs1r = esA.enter_context(nc.sbuf_tensor("rs1r", [1, TPC], F32))
    mu2r = esA.enter_context(nc.sbuf_tensor("mu2r", [1, TPC], F32))
    rs2r = esA.enter_context(nc.sbuf_tensor("rs2r", [1, TPC], F32))
    srow = esA.enter_context(nc.sbuf_tensor("srow", [1, HALF], F32))
    srow2 = esA.enter_context(nc.sbuf_tensor("srow2", [1, HALF], F32))
    rowh = esA.enter_context(nc.sbuf_tensor("rowh", [1, HALF], BF16))
    rowl = esA.enter_context(nc.sbuf_tensor("rowl", [1, HALF], BF16))
    ones_row = esA.enter_context(nc.sbuf_tensor("ones_row", [1, HALF], BF16))

    def f_dve_or(e):
        vE.inc(e.memset(ones_row[:], 1.0))
        ev["q_ones_v"] = vE.v
    B.vector(f_dve_or)

    es2 = ExitStack()
    woF = es2.enter_context(nc.sbuf_tensor("woF", [128, NIC * 128], F32))
    woh = es2.enter_context(nc.sbuf_tensor("woh", [128, 2 * NIC * 128], BF16))
    wol = es2.enter_context(nc.sbuf_tensor("wol", [128, 2 * NIC * 128], BF16))
    xoch = es2.enter_context(nc.sbuf_tensor("xoch", [128, 2 * HALF], BF16))
    xocl = es2.enter_context(nc.sbuf_tensor("xocl", [128, 2 * HALF], BF16))
    z1f = es2.enter_context(nc.sbuf_tensor("z1f", [128, 2 * HALF], F32))
    sqf = es2.enter_context(nc.sbuf_tensor("sqf", [128, 2 * HALF], F32))
    z1hb = es2.enter_context(nc.sbuf_tensor("z1hb", [128, 2 * HALF], BF16))
    z1lb = es2.enter_context(nc.sbuf_tensor("z1lb", [128, 2 * HALF], BF16))
    sqhb = es2.enter_context(nc.sbuf_tensor("sqhb", [128, 2 * HALF], BF16))
    sqlb = es2.enter_context(nc.sbuf_tensor("sqlb", [128, 2 * HALF], BF16))
    ps2 = ExitStack()
    psY = ps2.enter_context(nc.psum_tensor("psY", [128, 1024], F32))
    psZs = ps2.enter_context(nc.psum_tensor("psZs", [1, 512], F32))
    psQs = ps2.enter_context(nc.psum_tensor("psQs", [1, 512], F32))

    def wo_dma(e, oc):
        for dcg in range(4):
            src_ = wo[dcg * 1024:(dcg + 1) * 1024,
                      oc * 128:(oc + 1) * 128].rearrange("(ic p) o -> p ic o", p=128)
            dst = woF[:, dcg * 1024:(dcg + 1) * 1024].rearrange("p (ic c) -> p ic c", c=128)
            s_wo.dma(e.dma_start(out=dst, in_=src_))
        ev[("woq", oc)] = (s_wo, s_wo.v)

    def p2_half(half):
        def f_gp0(e):
            for dc in range(NIC):
                s_old.dma(e.dma_start(out=acth[:, dc * HALF:(dc + 1) * HALF],
                                      in_=o_scr[0][dc, :, half * HALF:(half + 1) * HALF]))
                s_old.dma(e.dma_start(out=actl[:, dc * HALF:(dc + 1) * HALF],
                                      in_=o_scr[1][dc, :, half * HALF:(half + 1) * HALF]))
            ev[("oload", half)] = (s_old, s_old.v)
            for oc in range(2):
                s_xoc[oc % 2].dma(e.dma_start(out=xoch[:, (oc % 2) * HALF:(oc % 2 + 1) * HALF],
                                              in_=xt_scr[0][oc, :, half * HALF:(half + 1) * HALF]))
                s_xoc[oc % 2].dma(e.dma_start(out=xocl[:, (oc % 2) * HALF:(oc % 2 + 1) * HALF],
                                              in_=xt_scr[1][oc, :, half * HALF:(half + 1) * HALF]))
                ev[("xoc", half, oc)] = (s_xoc[oc % 2], s_xoc[oc % 2].v)
        B.gpsimd(f_gp0)

        def f_sync0(e):
            wo_dma(e, 0)
        B.sync(f_sync0)

        def f_dve0(e):
            wde(e, ("woq", 0))
            vE.inc(e.tensor_copy(out=woh[:, 0:NIC * 128], in_=woF[:]))
            ev[("wosh", 0)] = vE.v
            sw(e, vE)
            vE.inc(e.tensor_tensor(out=wol[:, 0:NIC * 128], in0=woF[:],
                                   in1=woh[:, 0:NIC * 128], op=ALU.subtract))
            ev[("wosl", 0)] = vE.v
        B.vector(f_dve0)

        def f_sync1(e):
            _w(e, vE, ev[("wosl", 0)])
            wo_dma(e, 1)
        B.sync(f_sync1)

        for oc in range(NIC + 2):
            def f_te(e, oc=oc):
                if oc < NIC:
                    _w(e, vE, ev[("wosl", oc)])
                    if oc == 0:
                        wde(e, ("oload", half))
                    if ("z1f", half, oc - 2) in ev:
                        _w(e, vE, ev[("z1f", half, oc - 2)])
                    par = oc % 2
                    wb = par * NIC * 128
                    dst = psY[:, par * 512:(par + 1) * 512]
                    for dc in range(NIC):
                        wh = woh[:, wb + dc * 128:wb + (dc + 1) * 128]
                        wl = wol[:, wb + dc * 128:wb + (dc + 1) * 128]
                        ohd = acth[:, dc * HALF:(dc + 1) * HALF]
                        old = actl[:, dc * HALF:(dc + 1) * HALF]
                        e.matmul(dst, wh, ohd, start=(dc == 0), stop=False)
                        e.matmul(dst, wh, old, start=False, stop=False)
                        mmy = e.matmul(dst, wl, ohd, start=False, stop=(dc == NIC - 1))
                    tE.inc(mmy)
                    ev[("ymm", half, oc)] = tE.v
                m = oc - 2
                if 0 <= m < NIC:
                    par = m % 2
                    _w(e, vE, ev[("z1sq_l", half, m)])
                    st = m == 0
                    e.matmul(psZs[:], onesb[:, 0:1], z1hb[:, par * HALF:(par + 1) * HALF],
                             start=st, stop=False)
                    e.matmul(psZs[:], onesb[:, 0:1], z1lb[:, par * HALF:(par + 1) * HALF],
                             start=False, stop=(m == NIC - 1))
                    e.matmul(psQs[:], onesb[:, 0:1], sqhb[:, par * HALF:(par + 1) * HALF],
                             start=st, stop=False)
                    mmst = e.matmul(psQs[:], onesb[:, 0:1],
                                    sqlb[:, par * HALF:(par + 1) * HALF],
                                    start=False, stop=(m == NIC - 1))
                    tE.inc(mmst)
                    ev[("stats", half, m)] = tE.v

            def f_dve(e, oc=oc):
                m = oc + 1
                if m < NIC:
                    wde(e, ("woq", m))
                    if ("ymm", half, m - 2) in ev:
                        _w(e, tE, ev[("ymm", half, m - 2)])
                    wb = (m % 2) * NIC * 128
                    vE.inc(e.tensor_copy(out=woh[:, wb:wb + NIC * 128], in_=woF[:]))
                    sw(e, vE)
                    vE.inc(e.tensor_tensor(out=wol[:, wb:wb + NIC * 128], in0=woF[:],
                                           in1=woh[:, wb:wb + NIC * 128],
                                           op=ALU.subtract))
                    ev[("wosh", m)] = vE.v
                    ev[("wosl", m)] = vE.v
                if oc < NIC:
                    _w(e, tE, ev[("ymm", half, oc)])
                    wde(e, ("xoc", half, oc))
                    if ("z1sq_h", half, oc - 2) in ev:
                        _w(e, vE, ev[("z1sq_h", half, oc - 2)])
                    sw(e, vE)
                    par = oc % 2
                    z1s = z1f[:, par * HALF:(par + 1) * HALF]
                    vE.inc(e.tensor_tensor(out=z1s, in0=psY[:, par * 512:(par + 1) * 512],
                                           in1=xoch[:, par * HALF:(par + 1) * HALF],
                                           op=ALU.add))
                    sw(e, vE)
                    vE.inc(e.tensor_tensor(out=z1s, in0=z1s,
                                           in1=xocl[:, par * HALF:(par + 1) * HALF],
                                           op=ALU.add))
                    ev[("z1f", half, oc)] = vE.v
                m = oc - 1
                if 0 <= m < NIC:
                    par = m % 2
                    if ("stats", half, m - 2) in ev:
                        _w(e, tE, ev[("stats", half, m - 2)])
                    if ("z1sp", half, m - 2) in ev:
                        wde(e, ("z1sp", half, m - 2))
                    sw(e, vE)
                    z1v = z1f[:, par * HALF:(par + 1) * HALF]
                    vE.inc(e.tensor_copy(out=z1hb[:, par * HALF:(par + 1) * HALF], in_=z1v))
                    vE.inc(e.tensor_tensor(out=sqf[:, par * HALF:(par + 1) * HALF],
                                           in0=z1v, in1=z1v, op=ALU.mult))
                    ev[("z1h", half, m)] = vE.v
                    sw(e, vE)
                    vE.inc(e.tensor_tensor(out=z1lb[:, par * HALF:(par + 1) * HALF],
                                           in0=z1v,
                                           in1=z1hb[:, par * HALF:(par + 1) * HALF],
                                           op=ALU.subtract))
                    vE.inc(e.tensor_copy(out=sqhb[:, par * HALF:(par + 1) * HALF],
                                         in_=sqf[:, par * HALF:(par + 1) * HALF]))
                    ev[("z1l", half, m)] = vE.v
                    sw(e, vE)
                    vE.inc(e.tensor_tensor(out=sqlb[:, par * HALF:(par + 1) * HALF],
                                           in0=sqf[:, par * HALF:(par + 1) * HALF],
                                           in1=sqhb[:, par * HALF:(par + 1) * HALF],
                                           op=ALU.subtract))
                    ev[("z1sq_h", half, m)] = vE.v
                    ev[("z1sq_l", half, m)] = vE.v

            def f_sync(e, oc=oc):
                m = oc + 2
                if m < NIC:
                    _w(e, vE, ev[("wosl", m - 1)])
                    wo_dma(e, m)

            def f_gp(e, oc=oc):
                m = oc + 2
                if m < NIC:
                    if ("z1f", half, m - 2) in ev:
                        _w(e, vE, ev[("z1f", half, m - 2)])
                    s_xoc[m % 2].dma(e.dma_start(out=xoch[:, (m % 2) * HALF:(m % 2 + 1) * HALF],
                                                 in_=xt_scr[0][m, :, half * HALF:(half + 1) * HALF]))
                    s_xoc[m % 2].dma(e.dma_start(out=xocl[:, (m % 2) * HALF:(m % 2 + 1) * HALF],
                                                 in_=xt_scr[1][m, :, half * HALF:(half + 1) * HALF]))
                    ev[("xoc", half, m)] = (s_xoc[m % 2], s_xoc[m % 2].v)
                m = oc - 1
                if 0 <= m < NIC:
                    _w(e, vE, ev[("z1sq_l", half, m)])
                    par = m % 2
                    s_z1sp[par].dma(e.dma_start(out=z1_scr[0][m, :, half * HALF:(half + 1) * HALF],
                                                in_=z1hb[:, par * HALF:(par + 1) * HALF]))
                    s_z1sp[par].dma(e.dma_start(out=z1_scr[1][m, :, half * HALF:(half + 1) * HALF],
                                                in_=z1lb[:, par * HALF:(par + 1) * HALF]))
                    ev[("z1sp", half, m)] = (s_z1sp[par], s_z1sp[par].v)
            B.tensor(f_te)
            B.vector(f_dve)
            B.sync(f_sync)
            B.gpsimd(f_gp)

        def f_dve_fin(e):
            _w(e, tE, ev[("stats", half, NIC - 1)])
            vE.inc(e.tensor_scalar(out=mu1r[0:1, half * HALF:(half + 1) * HALF], in0=psZs[:],
                                   scalar1=1.0 / D, scalar2=None, op0=ALU.mult))
            vE.inc(e.tensor_scalar(out=srow[:], in0=psQs[:], scalar1=1.0 / D, scalar2=None,
                                   op0=ALU.mult))
            mu = mu1r[0:1, half * HALF:(half + 1) * HALF]
            sw(e, vE)
            vE.inc(e.tensor_tensor(out=srow2[:], in0=mu, in1=mu, op=ALU.mult))
            sw(e, vE)
            vE.inc(e.tensor_tensor(out=srow[:], in0=srow[:], in1=srow2[:], op=ALU.subtract))
            sw(e, vE)
            vE.inc(e.tensor_scalar(out=srow[:], in0=srow[:], scalar1=1e-5, scalar2=None,
                                   op0=ALU.add))
            ev[("ln1_var", half)] = vE.v
        B.vector(f_dve_fin)

        def f_act_fin(e):
            _w(e, vE, ev[("ln1_var", half)])
            aE.inc(e.activation(out=srow[:], in_=srow[:], func=ACTF.Sqrt, bias=0.0, scale=1.0))
            ev[("ln1_sqrt", half)] = aE.v
        B.scalar(f_act_fin)

        def f_dve_fin2(e):
            _w(e, aE, ev[("ln1_sqrt", half)])
            vE.inc(e.reciprocal(out=rs1r[0:1, half * HALF:(half + 1) * HALF], in_=srow[:]))
        B.vector(f_dve_fin2)
        barrier()

    if MAXPH >= 2:
        for half in range(2):
            p2_half(half)
            ev = {k: v for k, v in ev.items() if not isinstance(k, tuple)}

    ps2.close()
    es2.close()

    es3 = ExitStack()
    zrh = es3.enter_context(nc.sbuf_tensor("zrh", [128, 2 * HALF], BF16))
    zrl = es3.enter_context(nc.sbuf_tensor("zrl", [128, 2 * HALF], BF16))
    x1tmp = es3.enter_context(nc.sbuf_tensor("x1tmp", [128, 2 * HALF], F32))
    es3b = ExitStack()
    dotk = es3b.enter_context(nc.sbuf_tensor("dotk", [128, 256], F32))
    psD_sb = es3b.enter_context(nc.sbuf_tensor("psD_sb", [64, 512], F32))
    ebuf = es3b.enter_context(nc.sbuf_tensor("ebuf", [128, 64], F32))
    ebufb = es3b.enter_context(nc.sbuf_tensor("ebufb", [128, 64], BF16))
    ebuf_oh = es3b.enter_context(nc.sbuf_tensor("ebuf_oh", [64, 128], F32))
    ohb = es3b.enter_context(nc.sbuf_tensor("ohb", [64, 128], BF16))
    dnegm = es3b.enter_context(nc.sbuf_tensor("dnegm", [128, 1], F32))
    dsum = es3b.enter_context(nc.sbuf_tensor("dsum", [128, 1], F32))
    dthr = es3b.enter_context(nc.sbuf_tensor("dthr", [128, 1], F32))
    q01 = es3b.enter_context(nc.sbuf_tensor("q01", [128, 1], F32))
    centf = es3b.enter_context(nc.sbuf_tensor("centf", [64, D], F32))
    cth = es3b.enter_context(nc.sbuf_tensor("cth", [128, NIC * 64], BF16))
    ctl = es3b.enter_context(nc.sbuf_tensor("ctl", [128, NIC * 64], BF16))
    cnt_s = es3b.enter_context(nc.sbuf_tensor("cnt_s", [64, 1], F32))
    glob_s = es3b.enter_context(nc.sbuf_tensor("glob_s", [64, 1], F32))
    arowb = es3b.enter_context(nc.sbuf_tensor("arowb", [1, 128], BF16))
    ar_i = es3b.enter_context(nc.sbuf_tensor("ar_i", [1, D], I32))
    ar_f = es3b.enter_context(nc.sbuf_tensor("ar_f", [1, D], F32))
    qrow = es3b.enter_context(nc.sbuf_tensor("qrow", [1, TPC], F32))
    qrowb = es3b.enter_context(nc.sbuf_tensor("qrowb", [1, TPC], BF16))
    ps3 = ExitStack()
    psD = ps3.enter_context(nc.psum_tensor("psD", [64, 512], F32))
    psT3 = ps3.enter_context(nc.psum_tensor("psT3", [128, 1024], F32))
    psC = ps3.enter_context(nc.psum_tensor("psC", [64, 512], F32))
    psB3 = ps3.enter_context(nc.psum_tensor("psB3", [128, 512], F32))
    psR3 = ps3.enter_context(nc.psum_tensor("psR3", [128, 512], F32))

    def f_sync_c(e):
        if MAXPH < 3:
            return
        s_cst.dma(e.dma_start(out=centf[:], in_=cent))
        s_cst.dma(e.dma_start(out=ar_i[:], in_=assign_row))
        ev["p3_loads"] = (s_cst, s_cst.v)
    B.sync(f_sync_c)

    def f_dve_ar(e):
        if MAXPH < 3:
            return
        wde(e, "p3_loads")
        vE.inc(e.tensor_copy(out=ar_f[:], in_=ar_i[:]))
    B.vector(f_dve_ar)

    if MAXPH >= 3:
        tsp_rounds("ct", "c", NIC,
               (lambda i: centf[:, i * 128:(i + 1) * 128]),
               (lambda i: cth[:, i * 64:(i + 1) * 64]),
               (lambda i: ctl[:, i * 64:(i + 1) * 64]),
               psT3, 4, 64,
                   pre_wait=(lambda e: wde(e, "p3_loads")),
                   scale=10.0, ident_ap=idn[0:64, 0:64])

    def bcast_rows(tag, srcrow, dst, half, psb):
        def f_dve(e):
            _w(e, tE, tE.v)
            _w(e, aE, aE.v)
            sw(e, vE)
            vE.inc(e.tensor_copy(out=rowh[:],
                                 in_=srcrow[0:1, half * HALF:(half + 1) * HALF]))
            ev[(tag + "_rh", half)] = vE.v
            sw(e, vE)
            vE.inc(e.scalar_tensor_tensor(out=rowl[:],
                                          in0=srcrow[0:1, half * HALF:(half + 1) * HALF],
                                          scalar=1.0, in1=rowh[:], op0=ALU.mult,
                                          op1=ALU.subtract))
            ev[(tag + "_rl", half)] = vE.v
        def f_te(e):
            _w(e, vE, ev[(tag + "_rl", half)])
            e.matmul(psb[:], onesb[0:1, :], rowh[:], start=True, stop=False)
            mmb = e.matmul(psb[:], onesb[0:1, :], rowl[:], start=False, stop=True)
            tE.inc(mmb)
            ev[(tag + "_mm", half)] = tE.v
        def f_dve2(e):
            _w(e, tE, ev[(tag + "_mm", half)])
            vE.inc(e.tensor_copy(out=dst[:], in_=psb[:]))
            ev[(tag + "_bc", half)] = vE.v
        B.vector(f_dve)
        B.tensor(f_te)
        B.vector(f_dve2)

    def x1_build(half, with_dots):
        for ic in range(NIC + 2):
            def f_gp(e, ic=ic):
                if ic < NIC:
                    if ("x1f", half, ic - 2) in ev:
                        _w(e, vE, ev[("x1f", half, ic - 2)])
                    par = ic % 2
                    s_zr[par].dma(e.dma_start(out=zrh[:, par * HALF:(par + 1) * HALF],
                                              in_=z1_scr[0][ic, :, half * HALF:(half + 1) * HALF]))
                    s_zr[par].dma(e.dma_start(out=zrl[:, par * HALF:(par + 1) * HALF],
                                              in_=z1_scr[1][ic, :, half * HALF:(half + 1) * HALF]))
                    ev[("zr", half, ic)] = (s_zr[par], s_zr[par].v)
            def f_dve(e, ic=ic):
                m = ic - 1
                if 0 <= m < NIC:
                    if m == 0:
                        _w(e, vE, ev[("ln1mu_bc", half)])
                        _w(e, vE, ev[("ln1rs_bc", half)])
                    par = m % 2
                    wde(e, ("zr", half, m))
                    if ("x1h", half, m - 2) in ev:
                        _w(e, vE, ev[("x1h", half, m - 2)])
                    sw(e, vE)
                    x1s = x1tmp[:, par * HALF:(par + 1) * HALF]
                    vE.inc(e.tensor_tensor(out=x1s, in0=zrh[:, par * HALF:(par + 1) * HALF],
                                           in1=zrl[:, par * HALF:(par + 1) * HALF], op=ALU.add))
                    sw(e, vE)
                    vE.inc(e.tensor_tensor(out=x1s, in0=x1s, in1=mu_b[:], op=ALU.subtract))
                    sw(e, vE)
                    vE.inc(e.tensor_tensor(out=x1s, in0=x1s, in1=r_b[:], op=ALU.mult))
                    ev[("x1f", half, m)] = vE.v
            def f_dve2(e, ic=ic):
                m = ic - 1
                if 0 <= m < NIC:
                    par = m % 2
                    sw(e, vE)
                    vE.inc(e.tensor_copy(out=acth[:, m * HALF:(m + 1) * HALF],
                                         in_=x1tmp[:, par * HALF:(par + 1) * HALF]))
                    ev[("x1h", half, m)] = vE.v
                    sw(e, vE)
                    vE.inc(e.tensor_tensor(out=actl[:, m * HALF:(m + 1) * HALF],
                                           in0=x1tmp[:, par * HALF:(par + 1) * HALF],
                                           in1=acth[:, m * HALF:(m + 1) * HALF],
                                           op=ALU.subtract))
                    ev[("x1l", half, m)] = vE.v
            def f_te(e, ic=ic):
                if not with_dots:
                    return
                m = ic - 1
                if 0 <= m < NIC:
                    if m == 0 and ("ct_hi", "c") in ev:
                        _w(e, vE, ev[("ct_lo", "c")])
                    _w(e, vE, ev[("x1l", half, m)])
                    ch = cth[:, m * 64:(m + 1) * 64]
                    cl = ctl[:, m * 64:(m + 1) * 64]
                    xh = acth[:, m * HALF:(m + 1) * HALF]
                    xl = actl[:, m * HALF:(m + 1) * HALF]
                    e.matmul(psD[:], ch, xh, start=(m == 0), stop=False)
                    e.matmul(psD[:], ch, xl, start=False, stop=False)
                    mmd = e.matmul(psD[:], cl, xh, start=False, stop=(m == NIC - 1))
                    tE.inc(mmd)
                    ev[("dmm", half, m)] = tE.v
            B.gpsimd(f_gp)
            B.vector(f_dve)
            B.vector(f_dve2)
            B.tensor(f_te)

    def p3_half(half):
        bcast_rows("ln1mu", mu1r, mu_b, half, psB3)
        bcast_rows("ln1rs", rs1r, r_b, half, psB3)
        x1_build(half, with_dots=True)

        def f_dve_cp(e):
            _w(e, tE, ev[("dmm", half, NIC - 1)])
            vE.inc(e.tensor_copy(out=psD_sb[:], in_=psD[:]))
            ev[("dot_sb", half)] = vE.v
        B.vector(f_dve_cp)

        for tt in range(4):
            def f_te(e, tt=tt):
                _w(e, vE, ev[("dot_sb", half)])
                if tt > 0:
                    _w(e, vE, ev[("dot_ev", half, tt - 1)])
                    _w(e, vE, ev[("q_ev", half, tt - 1)])
                tE.inc(e.transpose(psT3[:, 0:64], psD_sb[:, tt * 128:(tt + 1) * 128],
                                   idn[0:64, 0:64]))
                ev[("dot_tr", half, tt)] = tE.v
            B.tensor(f_te)

            def f_dve(e, tt=tt):
                _w(e, tE, ev[("dot_tr", half, tt)])
                vE.inc(e.tensor_copy(out=dotk[:, tt * 64:(tt + 1) * 64], in_=psT3[:, 0:64]))
                ev[("dot_ev", half, tt)] = vE.v
                sw(e, vE)
                if ("dot_exp", half, tt - 1) in ev:
                    _w(e, aE, ev[("dot_exp", half, tt - 1)])
                vE.inc(e.reduce_max(out=dnegm[:], in_=dotk[:, tt * 64:(tt + 1) * 64],
                                    axis=AX, negate=True))
                ev[("dot_max", half, tt)] = vE.v
            B.vector(f_dve)

            def f_act(e, tt=tt):
                _w(e, vE, ev[("dot_max", half, tt)])
                aE.inc(e.activation(out=ebuf[:], in_=dotk[:, tt * 64:(tt + 1) * 64],
                                    func=ACTF.Exp, bias=dnegm[:], scale=1.0, accum_out=dsum[:]))
                ev[("dot_exp", half, tt)] = aE.v
            B.scalar(f_act)

            def f_dve2(e, tt=tt):
                _w(e, aE, ev[("dot_exp", half, tt)])
                vE.inc(e.tensor_scalar(out=dthr[:], in0=dsum[:], scalar1=0.01, scalar2=None,
                                       op0=ALU.mult))
                sw(e, vE)
                vE.inc(e.tensor_scalar(out=ebuf[:], in0=ebuf[:], scalar1=dthr[:], scalar2=None,
                                       op0=ALU.is_gt))
                sw(e, vE)
                vE.inc(e.reduce_max(out=q01[:], in_=ebuf[:], axis=AX))
                ev[("dot_hits", half, tt)] = vE.v
            B.vector(f_dve2)

            def f_dve2b(e, tt=tt):
                sw(e, vE)
                vE.inc(e.tensor_copy(out=ebufb[:], in_=ebuf[:]))
                ev[("dot_hb", half, tt)] = vE.v
            B.vector(f_dve2b)

            def f_te2(e, tt=tt):
                _w(e, vE, ev[("dot_hb", half, tt)])
                st = (half == 0 and tt == 0)
                sp = (half == 1 and tt == 3)
                mmc = e.matmul(psC[:, 0:1], ebufb[:], onesb[:, 0:1], start=st, stop=sp)
                tE.inc(mmc)
                ev[("cnt", half, tt)] = tE.v
                _w(e, vE, ev[("dot_hits", half, tt)])
                _w(e, vE, ev[("dot_ev", half, tt)])
                tE.inc(e.transpose(psT3[0:1, 384:512], q01[:], idn[:]))
                ev[("q_tr", half, tt)] = tE.v
            B.tensor(f_te2)

            def f_dve3(e, tt=tt):
                _w(e, tE, ev[("q_tr", half, tt)])
                vE.inc(e.tensor_copy(
                    out=qrow[0:1, half * HALF + tt * 128:half * HALF + (tt + 1) * 128],
                    in_=psT3[0:1, 384:512]))
                ev[("q_ev", half, tt)] = vE.v
            B.vector(f_dve3)

    if MAXPH >= 3:
        p3_half(0)
        barrier()
        p3_half(1)
    barrier()

    def f_dve_cnt(e):
        if MAXPH < 3:
            return
        vE.inc(e.tensor_copy(out=cnt_s[:], in_=psC[:, 0:1]))
        ev["cnt_sb"] = vE.v
    B.vector(f_dve_cnt)

    def f_gp_cc(e):
        if MAXPH < 3:
            return
        _w(e, vE, ev["cnt_sb"])
        s_cc.dma(e.dma_start(out=cc_in.ap(), in_=cnt_s[:]))
        e.wait_ge(s_cc.h, s_cc.v)
        cS.inc(e.collective_compute(
            "AllReduce", ALU.add, replica_groups=[[0, 1, 2, 3, 4, 5, 6, 7]],
            ins=[cc_in.ap().opt()], outs=[cc_out.ap().opt()]))
        e.wait_ge(cS.h, cS.v)
        s_cc.dma(e.dma_start(out=glob_s[:], in_=cc_out.ap()))
        ev["cc_done"] = (s_cc, s_cc.v)
    B.gpsimd(f_gp_cc)

    def f_dve_cs(e):
        if MAXPH < 3:
            return
        wde(e, "cc_done")
        vE.inc(e.tensor_scalar(out=glob_s[:], in0=glob_s[:], scalar1=0.5, scalar2=None,
                               op0=ALU.is_ge))
        ev["cs01"] = vE.v
    B.vector(f_dve_cs)

    def f_dve_csb(e):
        if MAXPH < 3:
            return
        sw(e, vE)
        vE.inc(e.tensor_copy(out=csb[:], in_=glob_s[:]))
        ev["csb"] = vE.v
    B.vector(f_dve_csb)

    for oc in range(NIC if MAXPH >= 3 else 0):
        def f_dve0r(e, oc=oc):
            if oc == 0:
                _w(e, vE, ev["csb"])
            if ("abc", oc - 1) in ev:
                _w(e, tE, ev[("abc", oc - 1)])
            sw(e, vE)
            vE.inc(e.tensor_copy(out=arowb[:], in_=ar_f[0:1, oc * 128:(oc + 1) * 128]))
            ev[("arowb", oc)] = vE.v
        def f_te(e, oc=oc):
            _w(e, vE, ev[("arowb", oc)])
            if ("abc01", oc - 1) in ev:
                _w(e, vE, ev[("abc01", oc - 1)])
            mma = e.matmul(psB3[0:64, 0:128], onesb[0:1, 0:64], arowb[:],
                           start=True, stop=True)
            tE.inc(mma)
            ev[("abc", oc)] = tE.v
        def f_dve(e, oc=oc):
            _w(e, tE, ev[("abc", oc)])
            vE.inc(e.tensor_scalar(out=ebuf_oh[:], in0=psB3[0:64, 0:128], scalar1=iota_s[:],
                                   scalar2=None, op0=ALU.is_equal))
            ev[("abc01", oc)] = vE.v
        def f_dve1r(e, oc=oc):
            _w(e, vE, ev[("abc01", oc)])
            if ("rsmm", oc - 1) in ev:
                _w(e, tE, ev[("rsmm", oc - 1)])
            sw(e, vE)
            vE.inc(e.tensor_copy(out=ohb[:], in_=ebuf_oh[:]))
            ev[("ohb", oc)] = vE.v
        def f_te2(e, oc=oc):
            _w(e, vE, ev[("ohb", oc)])
            _w(e, vE, ev["csb"])
            if ("rs", oc - 1) in ev:
                _w(e, vE, ev[("rs", oc - 1)])
            mmr = e.matmul(psR3[:, 0:1], ohb[:], csb[:], start=True, stop=True)
            tE.inc(mmr)
            ev[("rsmm", oc)] = tE.v
        def f_dve2(e, oc=oc):
            _w(e, tE, ev[("rsmm", oc)])
            vE.inc(e.tensor_copy(out=rs_s[:, oc:oc + 1], in_=psR3[:, 0:1]))
            ev[("rs", oc)] = vE.v
        B.vector(f_dve0r)
        B.tensor(f_te)
        B.vector(f_dve)
        B.vector(f_dve1r)
        B.tensor(f_te2)
        B.vector(f_dve2)

    for half in range(2 if MAXPH >= 3 else 0):
        def f_dveq(e, half=half):
            _w(e, vE, ev[("q_ev", half, 3)] if ("q_ev", half, 3) in ev else 0)
            sw(e, vE)
            vE.inc(e.tensor_copy(out=qrowb[0:1, half * HALF:(half + 1) * HALF],
                                 in_=qrow[0:1, half * HALF:(half + 1) * HALF]))
            ev[("qrb", half)] = vE.v
        def f_te(e, half=half):
            _w(e, vE, ev[("qrb", half)])
            mmq3 = e.matmul(psB3[:], onesb[0:1, :],
                            qrowb[0:1, half * HALF:(half + 1) * HALF],
                            start=True, stop=True)
            tE.inc(mmq3)
            ev[("qmm3", half)] = tE.v
        def f_dve(e, half=half):
            _w(e, tE, ev[("qmm3", half)])
            vE.inc(e.tensor_copy(out=q_b[:, half * HALF:(half + 1) * HALF], in_=psB3[:]))
            ev[("qbc", half)] = vE.v
        B.vector(f_dveq)
        B.tensor(f_te)
        B.vector(f_dve)
    barrier()
    ps3.close()
    es3b.close()

    es4 = ExitStack()
    wfF = es4.enter_context(nc.sbuf_tensor("wfF", [128, D], F32))
    biasf4 = es4.enter_context(nc.sbuf_tensor("biasf4", [1, D], F32))
    biash = es4.enter_context(nc.sbuf_tensor("biash", [1, D], BF16))
    biasl = es4.enter_context(nc.sbuf_tensor("biasl", [1, D], BF16))
    wth = es4.enter_context(nc.sbuf_tensor("wth", [128, 2 * NIC * 128], BF16))
    wtl = es4.enter_context(nc.sbuf_tensor("wtl", [128, 2 * NIC * 128], BF16))
    gsf = es4.enter_context(nc.sbuf_tensor("gsf", [128, 2 * HALF], F32))
    z2ff = es4.enter_context(nc.sbuf_tensor("z2ff", [128, 2 * HALF], F32))
    z2hb = es4.enter_context(nc.sbuf_tensor("z2hb", [128, 2 * HALF], BF16))
    z2lb = es4.enter_context(nc.sbuf_tensor("z2lb", [128, 2 * HALF], BF16))
    sq2f = es4.enter_context(nc.sbuf_tensor("sq2f", [128, 2 * HALF], F32))
    sq2hb = es4.enter_context(nc.sbuf_tensor("sq2hb", [128, 2 * HALF], BF16))
    sq2lb = es4.enter_context(nc.sbuf_tensor("sq2lb", [128, 2 * HALF], BF16))
    ps4 = ExitStack()
    psF = ps4.enter_context(nc.psum_tensor("psF", [128, 1024], F32))
    psW = ps4.enter_context(nc.psum_tensor("psW", [128, 1024], F32))
    psZs4 = ps4.enter_context(nc.psum_tensor("psZs4", [1, 512], F32))
    psQs4 = ps4.enter_context(nc.psum_tensor("psQs4", [1, 512], F32))
    psB4 = ps4.enter_context(nc.psum_tensor("psB4", [128, 512], F32))

    def f_sync_b4(e):
        if MAXPH < 4:
            return
        s_cst.dma(e.dma_start(out=biasf4[:], in_=biasr))
        ev["p4_bias"] = (s_cst, s_cst.v)
    B.sync(f_sync_b4)

    def f_dve_b4(e):
        if MAXPH < 4:
            return
        wde(e, "p4_bias")
        vE.inc(e.tensor_copy(out=biash[:], in_=biasf4[:]))
        sw(e, vE)
        vE.inc(e.tensor_tensor(out=biasl[:], in0=biasf4[:], in1=biash[:], op=ALU.subtract))
        ev["q_ones"] = vE.v
    B.vector(f_dve_b4)

    def p4_half(half):
        bcast_rows("ln1mu", mu1r, mu_b, half, psB4)
        bcast_rows("ln1rs", rs1r, r_b, half, psB4)
        x1_build(half, with_dots=False)

        def f_sync0(e):
            s_wf.dma(e.dma_start(out=wfF[:], in_=wf[0:128, :]))
            ev[("wfq", half, 0)] = (s_wf, s_wf.v)
        B.sync(f_sync0)

        def wt_group_secs(oc):
            if oc >= NIC:
                return
            for g in range(8):
                def f_te(e, oc=oc, g=g):
                    if g == 0:
                        wde(e, ("wfq", half, oc))
                    if g == 0 and oc == 0:
                        _w(e, vE, ev[("x1l", half, NIC - 1)])
                    if g >= 2:
                        _w(e, vE, ev[("wt_l", half, oc, g - 2)])
                    elif oc > 0:
                        _w(e, vE, ev[("wt_l", half, oc - 1, 6 + g)])
                    tpar = g % 2
                    for u in range(4):
                        ic = g * 4 + u
                        tE.inc(e.transpose(
                            psW[:, tpar * 512 + u * 128:tpar * 512 + (u + 1) * 128],
                            wfF[:, ic * 128:(ic + 1) * 128], idn[:]))
                    ev[("wt_tr", half, oc, g)] = tE.v

                def f_dve(e, oc=oc, g=g):
                    _w(e, tE, ev[("wt_tr", half, oc, g)])
                    if ("fmm", half, oc - 2) in ev:
                        _w(e, tE, ev[("fmm", half, oc - 2)])
                    tpar = g % 2
                    wb = (oc % 2) * NIC * 128
                    vE.inc(e.tensor_copy(out=wth[:, wb + g * 512:wb + (g + 1) * 512],
                                         in_=psW[:, tpar * 512:(tpar + 1) * 512]))
                    sw(e, vE)
                    vE.inc(e.tensor_tensor(out=wtl[:, wb + g * 512:wb + (g + 1) * 512],
                                           in0=psW[:, tpar * 512:(tpar + 1) * 512],
                                           in1=wth[:, wb + g * 512:wb + (g + 1) * 512],
                                           op=ALU.subtract))
                    ev[("wt_l", half, oc, g)] = vE.v
                B.tensor(f_te)
                B.vector(f_dve)

        for oc in range(NIC + 3):
            def f_te(e, oc=oc):
                m = oc - 1
                if 0 <= m < NIC:
                    _w(e, vE, ev[("wt_l", half, m, 7)])
                    _w(e, vE, ev["q_ones"])
                    _w(e, vE, ev["q_ones_v"])
                    if ("gs", half, m - 2) in ev:
                        _w(e, vE, ev[("gs", half, m - 2)])
                    par = m % 2
                    wb = par * NIC * 128
                    dst = psF[:, par * 512:(par + 1) * 512]
                    for ic in range(NIC):
                        wh = wth[:, wb + ic * 128:wb + (ic + 1) * 128]
                        wl = wtl[:, wb + ic * 128:wb + (ic + 1) * 128]
                        xh = acth[:, ic * HALF:(ic + 1) * HALF]
                        xl = actl[:, ic * HALF:(ic + 1) * HALF]
                        e.matmul(dst, wh, xh, start=(ic == 0), stop=False)
                        e.matmul(dst, wh, xl, start=False, stop=False)
                        e.matmul(dst, wl, xh, start=False, stop=False)
                    e.matmul(dst, biash[0:1, m * 128:(m + 1) * 128], ones_row[:],
                             start=False, stop=False)
                    mmf = e.matmul(dst, biasl[0:1, m * 128:(m + 1) * 128], ones_row[:],
                                   start=False, stop=True)
                    tE.inc(mmf)
                    ev[("fmm", half, m)] = tE.v
                m = oc - 3
                if 0 <= m < NIC:
                    par = m % 2
                    _w(e, vE, ev[("z2sq_l", half, m)])
                    st = m == 0
                    e.matmul(psZs4[:], onesb[:, 0:1], z2hb[:, par * HALF:(par + 1) * HALF],
                             start=st, stop=False)
                    e.matmul(psZs4[:], onesb[:, 0:1], z2lb[:, par * HALF:(par + 1) * HALF],
                             start=False, stop=(m == NIC - 1))
                    e.matmul(psQs4[:], onesb[:, 0:1], sq2hb[:, par * HALF:(par + 1) * HALF],
                             start=st, stop=False)
                    mmst4 = e.matmul(psQs4[:], onesb[:, 0:1],
                                     sq2lb[:, par * HALF:(par + 1) * HALF],
                                     start=False, stop=(m == NIC - 1))
                    tE.inc(mmst4)
                    ev[("stats4", half, m)] = tE.v

            def f_dve(e, oc=oc):
                m = oc - 1
                if 0 <= m < NIC:
                    _w(e, tE, ev[("fmm", half, m)])
                    _w(e, vE, ev[("rs", m)])
                    _w(e, vE, ev[("qbc", half)])
                    par = m % 2
                    gs = gsf[:, par * HALF:(par + 1) * HALF]
                    vE.inc(e.scalar_tensor_tensor(out=gs, in0=psF[:, par * 512:(par + 1) * 512],
                                                  scalar=rs_s[:, m:m + 1],
                                                  in1=q_b[:, half * HALF:(half + 1) * HALF],
                                                  op0=ALU.mult, op1=ALU.mult))
                    ev[("gs", half, m)] = vE.v
                    if ("z2sq_l", half, m - 2) in ev:
                        _w(e, vE, ev[("z2sq_l", half, m - 2)])
                    z2s = z2ff[:, par * HALF:(par + 1) * HALF]
                    sw(e, vE)
                    vE.inc(e.tensor_tensor(out=z2s, in0=gs,
                                           in1=acth[:, m * HALF:(m + 1) * HALF], op=ALU.add))
                    sw(e, vE)
                    vE.inc(e.tensor_tensor(out=z2s, in0=z2s,
                                           in1=actl[:, m * HALF:(m + 1) * HALF], op=ALU.add))
                    ev[("z2f", half, m)] = vE.v
                m = oc - 2
                if 0 <= m < NIC:
                    par = m % 2
                    _w(e, vE, ev[("z2f", half, m)])
                    if ("stats4", half, m - 2) in ev:
                        _w(e, tE, ev[("stats4", half, m - 2)])
                    if ("z2sp", half, m - 2) in ev:
                        wde(e, ("z2sp", half, m - 2))
                    vE.inc(e.tensor_copy(out=z2hb[:, par * HALF:(par + 1) * HALF],
                                         in_=z2ff[:, par * HALF:(par + 1) * HALF]))
                    sw(e, vE)
                    vE.inc(e.tensor_tensor(out=z2lb[:, par * HALF:(par + 1) * HALF],
                                           in0=z2ff[:, par * HALF:(par + 1) * HALF],
                                           in1=z2hb[:, par * HALF:(par + 1) * HALF],
                                           op=ALU.subtract))
                    ev[("z2l", half, m)] = vE.v
                    vE.inc(e.tensor_tensor(out=sq2f[:, par * HALF:(par + 1) * HALF],
                                           in0=z2ff[:, par * HALF:(par + 1) * HALF],
                                           in1=z2ff[:, par * HALF:(par + 1) * HALF],
                                           op=ALU.mult))
                    sw(e, vE)
                    vE.inc(e.tensor_copy(out=sq2hb[:, par * HALF:(par + 1) * HALF],
                                         in_=sq2f[:, par * HALF:(par + 1) * HALF]))
                    sw(e, vE)
                    vE.inc(e.tensor_tensor(out=sq2lb[:, par * HALF:(par + 1) * HALF],
                                           in0=sq2f[:, par * HALF:(par + 1) * HALF],
                                           in1=sq2hb[:, par * HALF:(par + 1) * HALF],
                                           op=ALU.subtract))
                    ev[("z2sq_l", half, m)] = vE.v

            def f_sync(e, oc=oc):
                m = oc + 1
                if m < NIC:
                    _w(e, tE, ev[("wt_tr", half, m - 1, 7)])
                    s_wf.dma(e.dma_start(out=wfF[:], in_=wf[m * 128:(m + 1) * 128, :]))
                    ev[("wfq", half, m)] = (s_wf, s_wf.v)

            def f_gp(e, oc=oc):
                m = oc - 2
                if 0 <= m < NIC:
                    _w(e, vE, ev[("z2l", half, m)])
                    par = m % 2
                    s_z2sp[par].dma(e.dma_start(out=z2_scr[0][m, :, half * HALF:(half + 1) * HALF],
                                                in_=z2hb[:, par * HALF:(par + 1) * HALF]))
                    s_z2sp[par].dma(e.dma_start(out=z2_scr[1][m, :, half * HALF:(half + 1) * HALF],
                                                in_=z2lb[:, par * HALF:(par + 1) * HALF]))
                    ev[("z2sp", half, m)] = (s_z2sp[par], s_z2sp[par].v)
            wt_group_secs(oc)
            B.tensor(f_te)
            B.vector(f_dve)
            B.sync(f_sync)
            B.gpsimd(f_gp)

        def f_dve_fin(e):
            _w(e, tE, ev[("stats4", half, NIC - 1)])
            vE.inc(e.tensor_scalar(out=mu2r[0:1, half * HALF:(half + 1) * HALF], in0=psZs4[:],
                                   scalar1=1.0 / D, scalar2=None, op0=ALU.mult))
            vE.inc(e.tensor_scalar(out=srow[:], in0=psQs4[:], scalar1=1.0 / D, scalar2=None,
                                   op0=ALU.mult))
            mu = mu2r[0:1, half * HALF:(half + 1) * HALF]
            sw(e, vE)
            vE.inc(e.tensor_tensor(out=srow2[:], in0=mu, in1=mu, op=ALU.mult))
            sw(e, vE)
            vE.inc(e.tensor_tensor(out=srow[:], in0=srow[:], in1=srow2[:], op=ALU.subtract))
            sw(e, vE)
            vE.inc(e.tensor_scalar(out=srow[:], in0=srow[:], scalar1=1e-5, scalar2=None,
                                   op0=ALU.add))
            ev[("ln2_var", half)] = vE.v
        B.vector(f_dve_fin)

        def f_act_fin(e):
            _w(e, vE, ev[("ln2_var", half)])
            aE.inc(e.activation(out=srow[:], in_=srow[:], func=ACTF.Sqrt, bias=0.0, scale=1.0))
            ev[("ln2_sqrt", half)] = aE.v
        B.scalar(f_act_fin)

        def f_dve_fin2(e):
            _w(e, aE, ev[("ln2_sqrt", half)])
            vE.inc(e.reciprocal(out=rs2r[0:1, half * HALF:(half + 1) * HALF], in_=srow[:]))
        B.vector(f_dve_fin2)
        barrier()

    for half in range(2 if MAXPH >= 4 else 0):
        p4_half(half)
        keep = ("rs", "qbc", "z2sp", "q_ones")
        ev = {k: v for k, v in ev.items()
              if not isinstance(k, tuple) or k[0] in keep}

    ps4.close()
    es4.close()

    es5 = ExitStack()
    zoh = es5.enter_context(nc.sbuf_tensor("zoh", [128, 2 * HALF], BF16))
    zol = es5.enter_context(nc.sbuf_tensor("zol", [128, 2 * HALF], BF16))
    x2tmp = es5.enter_context(nc.sbuf_tensor("x2tmp", [128, 2 * HALF], F32))
    ostg = es5.enter_context(nc.sbuf_tensor("ostg", [128, 2 * HALF], F32))
    ps5 = ExitStack()
    psX5 = ps5.enter_context(nc.psum_tensor("psX5", [128, 1024], F32))
    psB5 = ps5.enter_context(nc.psum_tensor("psB5", [128, 512], F32))

    def p5_half(half):
        bcast_rows("ln2mu", mu2r, mu_b, half, psB5)
        bcast_rows("ln2rs", rs2r, r_b, half, psB5)
        for ic in range(NIC + 2):
            def f_gp(e, ic=ic):
                if ic < NIC:
                    if ("x2f", half, ic - 2) in ev:
                        _w(e, vE, ev[("x2f", half, ic - 2)])
                    par = ic % 2
                    s_zo[par].dma(e.dma_start(out=zoh[:, par * HALF:(par + 1) * HALF],
                                              in_=z2_scr[0][ic, :, half * HALF:(half + 1) * HALF]))
                    s_zo[par].dma(e.dma_start(out=zol[:, par * HALF:(par + 1) * HALF],
                                              in_=z2_scr[1][ic, :, half * HALF:(half + 1) * HALF]))
                    ev[("zo", half, ic)] = (s_zo[par], s_zo[par].v)
            def f_dve(e, ic=ic):
                m = ic - 1
                if 0 <= m < NIC:
                    if m == 0:
                        _w(e, vE, ev[("ln2mu_bc", half)])
                        _w(e, vE, ev[("ln2rs_bc", half)])
                    par = m % 2
                    wde(e, ("zo", half, m))
                    if ("x2tr", half, m - 2) in ev:
                        _w(e, tE, ev[("x2tr", half, m - 2)])
                    sw(e, vE)
                    x2 = x2tmp[:, par * HALF:(par + 1) * HALF]
                    vE.inc(e.tensor_tensor(out=x2, in0=zoh[:, par * HALF:(par + 1) * HALF],
                                           in1=zol[:, par * HALF:(par + 1) * HALF], op=ALU.add))
                    sw(e, vE)
                    vE.inc(e.tensor_tensor(out=x2, in0=x2, in1=mu_b[:], op=ALU.subtract))
                    sw(e, vE)
                    vE.inc(e.tensor_tensor(out=x2, in0=x2, in1=r_b[:], op=ALU.mult))
                    ev[("x2f", half, m)] = vE.v
            def f_te(e, ic=ic):
                m = ic - 1
                if 0 <= m < NIC:
                    par = m % 2
                    _w(e, vE, ev[("x2f", half, m)])
                    if ("x2ev", half, m - 2) in ev:
                        _w(e, vE, ev[("x2ev", half, m - 2)])
                    for tt in range(4):
                        tE.inc(e.transpose(
                            psX5[:, par * 512 + tt * 128:par * 512 + (tt + 1) * 128],
                            x2tmp[:, par * HALF + tt * 128:par * HALF + (tt + 1) * 128],
                            idn[:]))
                    ev[("x2tr", half, m)] = tE.v
            def f_dve2(e, ic=ic):
                m = ic - 1
                if 0 <= m < NIC:
                    par = m % 2
                    _w(e, tE, ev[("x2tr", half, m)])
                    if ("ost", half, m - 2) in ev:
                        wde(e, ("ost", half, m - 2))
                    vE.inc(e.tensor_copy(out=ostg[:, par * HALF:(par + 1) * HALF],
                                         in_=psX5[:, par * 512:(par + 1) * 512]))
                    ev[("x2ev", half, m)] = vE.v
            def f_sync(e, ic=ic):
                m = ic - 1
                if 0 <= m < NIC:
                    par = m % 2
                    _w(e, vE, ev[("x2ev", half, m)])
                    for tt in range(4):
                        s_ost[par].dma(e.dma_start(
                            out=out[(half * 4 + tt) * 128:(half * 4 + tt + 1) * 128,
                                    m * 128:(m + 1) * 128],
                            in_=ostg[:, par * HALF + tt * 128:par * HALF + (tt + 1) * 128]))
                    ev[("ost", half, m)] = (s_ost[par], s_ost[par].v)
            B.gpsimd(f_gp)
            B.vector(f_dve)
            B.tensor(f_te)
            B.vector(f_dve2)
            B.sync(f_sync)

    for half in range(2 if MAXPH >= 5 else 0):
        p5_half(half)
        barrier()

    def f_sync_fin(e):
        for sset in (SY_SEMS, GP_SEMS):
            for sm in sset:
                if sm.v > 0:
                    e.wait_ge(sm.h, sm.v)
    B.sync(f_sync_fin)

    ps5.close()
    es5.close()
    es3.close()
    esA.close()
    B.__exit__(None, None, None)
    es.close()
    return nc


_NC_CACHE = {}


def kernel(L0_attn_hidden_states, L0_attn_Wq, L0_attn_Wkv, L0_attn_Wo,
           L0_ffn_weight, L0_ffn_bias, L0_ffn_centroids, L0_ffn_assignments):
    x = np.ascontiguousarray(np.asarray(L0_attn_hidden_states, dtype=np.float32))
    wq = np.ascontiguousarray(np.asarray(L0_attn_Wq, dtype=np.float32))
    wkv = np.ascontiguousarray(np.asarray(L0_attn_Wkv, dtype=np.float32))
    wo = np.ascontiguousarray(np.asarray(L0_attn_Wo, dtype=np.float32))
    wf = np.ascontiguousarray(np.asarray(L0_ffn_weight, dtype=np.float32))
    bias = np.ascontiguousarray(np.asarray(L0_ffn_bias, dtype=np.float32)).reshape(1, D)
    cent = np.ascontiguousarray(np.asarray(L0_ffn_centroids, dtype=np.float32))
    assign = np.ascontiguousarray(np.asarray(L0_ffn_assignments, dtype=np.int32)).reshape(1, D)

    slopes = _slopes()
    jj = np.arange(512, dtype=np.float32)[None, None, :]
    ii = np.arange(128, dtype=np.float32)[None, :, None]
    alibi = (slopes[:, None, None] * (jj - ii)).astype(np.float32)

    ident = np.eye(128, dtype=np.float32)
    ones128 = np.ones((128, 128), dtype=np.float32)
    iota = np.arange(64, dtype=np.float32).reshape(64, 1)

    in_maps = []
    for c in range(8):
        b, j = divmod(c, 2)
        ch = CHUNKS[j]
        x_own = np.concatenate([x[b, k * 128:(k + 1) * 128] for k in ch], axis=0)
        mk = np.zeros((8, 128, 640), dtype=np.float32)
        for sg in range(8):
            L = LS[sg] * 128
            t0 = ch[sg] * 128
            r0 = L - 640
            keys = r0 + np.arange(640)[None, :]
            toks = t0 + np.arange(128)[:, None]
            mk[sg] = np.where(keys <= toks, 0.0, NEG).astype(np.float32)
        in_maps.append({
            "x_own": x_own, "x_kv": x[b], "wq": wq, "wo": wo, "wf": wf,
            "wkv": wkv, "biasr": bias, "cent": cent, "assign_row": assign,
            "alibi": alibi, "masks": mk.astype(BF),
            "identf": ident, "onesf": ones128, "iota64": iota,
        })

    if "nc" not in _NC_CACHE:
        _NC_CACHE["nc"] = build_nc()
    nc = _NC_CACHE["nc"]
    res = run_bass_kernel_spmd(nc, in_maps, core_ids=list(range(8)))

    full = np.empty((Bb, S, D), dtype=np.float32)
    for c in range(8):
        b, j = divmod(c, 2)
        o = np.asarray(res.results[c]["out"])
        for k, chk in enumerate(CHUNKS[j]):
            full[b, chk * 128:(chk + 1) * 128] = o[k * 128:(k + 1) * 128]
    return full
